# version 04
# baseline (speedup 1.0000x reference)
from contextlib import ExitStack
import numpy as np
import concourse.bass as bass
import concourse.mybir as mybir
from concourse.alu_op_type import AluOpType as OP

F32 = mybir.dt.float32
BF16 = mybir.dt.bfloat16
AF = mybir.ActivationFunctionType

D = 1024
NTOK = 2048
NCH = 8
NTILE = 16
TB = 512
NBLK = NTOK // TB

def tmap(half):
    return [t for t in range(32) if ((t % 4) in (0, 3)) == (half == 0)]

def own_token_idx(half):
    return np.concatenate([np.arange(t * 128, (t + 1) * 128) for t in tmap(half)])

ALIBI_SLOPES = np.asarray([2.0 ** (-(i + 1) / 2.0) for i in range(16)], np.float32)


class _Handle:
    def __init__(self):
        self.incs = []
    def then_inc(self, sem, n=1):
        self.incs.append((sem, n))
        return self


class _EngProxy:
    def __init__(self, plan, eng):
        self._plan = plan
        self._eng = eng
    def __getattr__(self, name):
        def rec(*args, **kwargs):
            h = _Handle()
            self._plan.ops[self._eng].append((name, args, kwargs, h))
            return h
        return rec


class Plan:
    ENGINES = ("sync", "gpsimd", "tensor", "vector", "scalar")
    def __init__(self, nc):
        self.nc = nc
        self.ops = {e: [] for e in self.ENGINES}
        self._vsem = nc.semaphore("vchain").__enter__()
        self._vscr = nc.sbuf_tensor("vscratch", [1, 2], mybir.dt.float32).__enter__()
        self.t = _EngProxy(self, "tensor")
        self.v = _EngProxy(self, "vector")
        self.s = _EngProxy(self, "scalar")
        self.g = _EngProxy(self, "gpsimd")
        self.y = _EngProxy(self, "sync")

    def build(self):
        nc = self.nc
        vsem = self._vsem
        vscr = self._vscr
        with nc.Block() as block:
            for eng in self.ENGINES:
                lst = self.ops[eng]
                if not lst:
                    continue
                def make(lst, eng=eng):
                    def body(e):
                        cnt = 0
                        for name, args, kwargs, h in lst:
                            if eng != "vector" or name == "wait_ge":
                                r = getattr(e, name)(*args, **kwargs)
                                for sm, n in h.incs:
                                    r = r.then_inc(sm, n)
                                continue
                            if cnt > 0:
                                e.wait_ge(vsem, cnt)
                            r = getattr(e, name)(*args, **kwargs)
                            r.then_inc(vsem, 1)
                            cnt += 1
                            for sm, n in h.incs:
                                e.wait_ge(vsem, cnt)
                                e.tensor_copy(vscr[:, 0:1], vscr[:, 1:2])\
                                    .then_inc(sm, n)
                                cnt += 0
                    return body
                getattr(block, eng)(make(lst))


_UID = [0]
def _u(name):
    _UID[0] += 1
    return f"{name}_u{_UID[0]}"

def sb(st, nc, name, shape, dtype, side=None):
    kw = {"side": side} if side else {}
    return st.enter_context(nc.sbuf_tensor(_u(name), shape, dtype, **kw))

def ps(st, nc, name, shape, dtype=F32):
    return st.enter_context(nc.psum_tensor(_u(name), shape, dtype))

def sem(st, nc, name):
    return st.enter_context(nc.semaphore(_u(name)))

def cols(blk, tb=TB):
    return slice(blk * tb, (blk + 1) * tb)


def emit_ln(p, stm, sts, name, src, dst_f32=None, dst_bf=None, gamma=None, beta=None):
    nc = p.nc
    sq = [sb(stm, nc, f"{name}_sq{i}", [128, NTOK], BF16, side="right") for i in range(2)]
    xbf = [sb(stm, nc, f"{name}_xbf{i}", [128, NTOK], BF16, side="right") for i in range(2)]
    srow = [sb(stm, nc, f"{name}_sr{i}", [1, TB], F32, side="right") for i in range(8)]
    R = [sb(stm, nc, f"{name}_R{i}", [1, TB], F32, side="right") for i in range(5)]
    tmp = [sb(stm, nc, f"{name}_tmp{i}", [128, TB], F32, side="right") for i in range(2)]
    ones_bf = sb(stm, nc, f"{name}_1bf", [128, 1], BF16, side="right")
    ones_f = sb(stm, nc, f"{name}_1f", [1, 128], F32, side="right")
    s_pre = sem(sts, nc, f"{name}_pre")
    s_mm = sem(sts, nc, f"{name}_mm")
    s_row = sem(sts, nc, f"{name}_row")
    s_nrm = sem(sts, nc, f"{name}_nrm")

    p.v.memset(ones_bf[:], 1.0)
    p.v.memset(ones_f[:], 1.0)
    pst_scope = ExitStack()
    ps_st = [ps(pst_scope, nc, f"{name}_ps{i}", [1, TB]) for i in range(8)]
    for d in range(NCH):
        if d >= 2:
            p.s.wait_ge(s_mm, d - 1)
            p.v.wait_ge(s_mm, d - 1)
        p.s.activation(sq[d % 2][:], src[d], AF.Square).then_inc(s_pre, 1)
        p.v.tensor_copy(xbf[d % 2][:], src[d]).then_inc(s_pre, 1)
        p.t.wait_ge(s_pre, 2 * d + 2)
        for blk in range(NBLK):
            p.t.matmul(ps_st[blk][:], ones_bf[:], xbf[d % 2][:, cols(blk)],
                       start=(d == 0), stop=(d == NCH - 1))
        for blk in range(NBLK):
            h = p.t.matmul(ps_st[4 + blk][:], ones_bf[:], sq[d % 2][:, cols(blk)],
                           start=(d == 0), stop=(d == NCH - 1))
        h.then_inc(s_mm, 1)

    p.v.wait_ge(s_mm, NCH)
    for blk in range(NBLK):
        p.v.tensor_copy(srow[blk][:], ps_st[blk][:])
        h = p.v.tensor_copy(srow[4 + blk][:], ps_st[4 + blk][:])
    h.then_inc(s_row, 1)
    pst_scope.close()
    ps_b = [ps(stm, nc, f"{name}_pb{i}", [128, TB]) for i in range(4)]

    for blk in range(NBLK):
        base = 1 + 5 * blk
        p.v.tensor_scalar(R[0][:], srow[blk][:], 1.0 / D, None, OP.mult)
        p.v.tensor_scalar(R[1][:], srow[4 + blk][:], 1.0 / D, None, OP.mult)\
            .then_inc(s_row, 1)
        p.s.wait_ge(s_row, base + 1)
        p.s.activation(R[2][:], R[0][:], AF.Square).then_inc(s_row, 1)
        p.v.wait_ge(s_row, base + 2)
        p.v.scalar_tensor_tensor(R[1][:], R[1][:], 1e-5, R[2][:], OP.add, OP.subtract)\
            .then_inc(s_row, 1)
        p.s.wait_ge(s_row, base + 3)
        p.s.activation(R[2][:], R[1][:], AF.Sqrt).then_inc(s_row, 1)
        p.v.wait_ge(s_row, base + 4)
        p.v.reciprocal(R[3][:], R[2][:])
        p.v.tensor_tensor(R[4][:], R[0][:], R[3][:], OP.mult)\
            .then_inc(s_row, 1)
        br, bm = ps_b[2 * (blk % 2)], ps_b[2 * (blk % 2) + 1]
        p.t.wait_ge(s_row, base + 5)
        if blk >= 2:
            p.t.wait_ge(s_nrm, (blk - 1) * NCH)
        p.t.matmul(br[:], ones_f[:], R[3][:], start=True, stop=True)
        p.t.matmul(bm[:], ones_f[:], R[4][:], start=True, stop=True)\
            .then_inc(s_mm, 1)
        p.v.wait_ge(s_mm, NCH + 1 + blk)
        for d in range(NCH):
            t0, t1 = tmp[0], tmp[1]
            p.v.tensor_tensor(t0[:], src[d][:, cols(blk)], br[:], OP.mult)
            hxn = p.v.tensor_tensor(t1[:], t0[:], bm[:], OP.subtract)
            if gamma is not None:
                if dst_f32 is not None:
                    p.v.tensor_scalar(dst_f32[d][:, cols(blk)], t1[:],
                                      gamma[d], beta[d], OP.mult, OP.add)
                if dst_bf is not None:
                    hxn = p.v.tensor_scalar(dst_bf[d][:, cols(blk)], t1[:],
                                            gamma[d], beta[d], OP.mult, OP.add)
            else:
                if dst_f32 is not None:
                    hxn = p.v.tensor_copy(dst_f32[d][:, cols(blk)], t1[:])
                if dst_bf is not None:
                    hxn = p.v.tensor_copy(dst_bf[d][:, cols(blk)], t1[:])
            hxn.then_inc(s_nrm, 1)
    return s_nrm, NBLK * NCH


class StagedEvac:
    def __init__(self, p, stm, sts, name, n_stages=4, tb=TB, dtype=BF16):
        self.p = p
        self.stages = [sb(stm, p.nc, f"{name}_stg{i}", [128, tb], dtype)
                       for i in range(n_stages)]
        self.n_stages = n_stages
        self.s_st = sem(sts, p.nc, f"{name}_sst")
        self.s_dma = sem(sts, p.nc, f"{name}_sdma")
        self.g = 0

    def __call__(self, bank_ap, s_mm, mmcnt, dst_ap, scale=1.0, bias=0.0):
        p = self.p
        g = self.g
        stage = self.stages[g % self.n_stages]
        p.s.wait_ge(s_mm, mmcnt)
        if g >= self.n_stages:
            p.s.wait_ge(self.s_dma, 16 * (g - self.n_stages + 1))
        func = AF.Copy if isinstance(bias, float) else AF.Identity
        h = p.s.activation(stage[:], bank_ap, func, scale=scale, bias=bias)
        h.then_inc(self.s_st, 1)
        p.y.wait_ge(self.s_st, g + 1)
        p.y.dma_start(dst_ap, stage[:]).then_inc(self.s_dma, 16)
        self.g += 1
        return (self.s_st, self.g)

    def flush_wait(self):
        self.p.y.wait_ge(self.s_dma, 16 * self.g)


def emit_fm_matmul(p, stm, sts, name, oc_list, nch, get_lhsT, get_rhs, evac,
                   nblk=NBLK, tb=TB):
    nc = p.nc
    pss = [ps(stm, nc, f"{name}_ps{i}", [128, tb]) for i in range(4)]
    s_mm = sem(sts, nc, f"{name}_mm")
    frees = []
    g = 0
    for oc in oc_list:
        for blk in range(nblk):
            bank = pss[g % 4]
            if g >= 4:
                fs, fv = frees[g - 4]
                p.t.wait_ge(fs, fv)
            for d in range(nch):
                h = p.t.matmul(bank[:], get_lhsT(d, oc), get_rhs(d, blk),
                               start=(d == 0), stop=(d == nch - 1))
            h.then_inc(s_mm, 1)
            frees.append(evac(oc, blk, bank, s_mm, g + 1))
            g += 1
    return frees[-1]


def build_prep(layer):
    nc = bass.Bass("TRN2", target_bir_lowering=False)
    NE = 14 if layer == 0 else 16
    xT = nc.dram_tensor("xT", [NCH, 128, NTOK], F32, kind="ExternalInput").ap()
    wqkvT = nc.dram_tensor("wqkvT", [NCH, 128, 3072], F32, kind="ExternalInput").ap()
    gwT = nc.dram_tensor("gwT", [NCH, 128, NE], F32, kind="ExternalInput").ap()
    ident = nc.dram_tensor("ident", [128, 128], F32, kind="ExternalInput").ap()
    if layer == 0:
        gcol = nc.dram_tensor("gcol", [NCH, 128, 1], F32, kind="ExternalInput").ap()
        bcol = nc.dram_tensor("bcol", [NCH, 128, 1], F32, kind="ExternalInput").ap()
    else:
        bgc = nc.dram_tensor("bgc", [NE, 1], F32, kind="ExternalInput").ap()
    o_xn = nc.dram_tensor("o_xn", [NCH, 128, NTOK], F32, kind="ExternalOutput").ap()
    o_xnbf = nc.dram_tensor("o_xnbf", [NCH, 128, NTOK], BF16, kind="ExternalOutput").ap()
    o_q = nc.dram_tensor("o_q", [NCH, 128, NTOK], BF16, kind="ExternalOutput").ap()
    o_k = nc.dram_tensor("o_k", [NCH, 128, NTOK], BF16, kind="ExternalOutput").ap()
    o_v = nc.dram_tensor("o_v", [NTILE, 128, D], BF16, kind="ExternalOutput").ap()
    o_gate = nc.dram_tensor("o_gate", [NE, NTOK], F32, kind="ExternalOutput").ap()

    p = Plan(nc)
    st = ExitStack()
    x_scope = ExitStack()
    x_sb = [sb(x_scope, nc, f"x{d}", [128, NTOK], F32, side="right") for d in range(NCH)]
    xn_sb = [sb(st, nc, f"xn{d}", [128, NTOK], F32) for d in range(NCH)]
    xnbf_sb = [sb(st, nc, f"xnbf{d}", [128, NTOK], BF16) for d in range(NCH)]
    id_sb = sb(st, nc, "idf", [128, 128], F32)
    s_ld = sem(st, nc, "ld")
    s_fin = sem(st, nc, "fin")
    fin = 0
    ld = 0

    for d in range(NCH):
        p.y.dma_start(x_sb[d][:], xT[d]).then_inc(s_ld, 16)
        ld += 16
    p.g.dma_start(id_sb[:], ident).then_inc(s_ld, 16)
    ld += 16
    if layer == 0:
        gc_sb = [sb(st, nc, f"gc{d}", [128, 1], F32) for d in range(NCH)]
        bc_sb = [sb(st, nc, f"bc{d}", [128, 1], F32) for d in range(NCH)]
        for d in range(NCH):
            p.g.dma_start(gc_sb[d][:], gcol[d]).then_inc(s_ld, 16)
            p.g.dma_start(bc_sb[d][:], bcol[d]).then_inc(s_ld, 16)
            ld += 32
    else:
        bg_sb = sb(st, nc, "bgsb", [NE, 1], F32)
        p.g.dma_start(bg_sb[:], bgc).then_inc(s_ld, 16)
        ld += 16
    LD_ALL = ld

    for e in (p.s, p.v, p.t):
        e.wait_ge(s_ld, LD_ALL)

    ln1_scope = ExitStack()
    s_n1, n1 = emit_ln(p, ln1_scope, st, "ln1", [x_sb[d][:] for d in range(NCH)],
                       dst_f32=xn_sb, dst_bf=xnbf_sb)
    p.y.wait_ge(s_n1, n1)
    for d in range(NCH):
        p.y.dma_start(o_xn[d], xn_sb[d][:]).then_inc(s_fin, 16)
        p.y.dma_start(o_xnbf[d], xnbf_sb[d][:]).then_inc(s_fin, 16)
        fin += 32

    if layer == 0:
        for e in (p.s, p.v, p.t):
            e.wait_ge(s_n1, n1)
        ln1_scope.close()
        x_scope.close()
        xn2bf_sb = [sb(st, nc, f"xn2bf{d}", [128, NTOK], BF16) for d in range(NCH)]
        ln2_scope = ExitStack()
        s_n2, n2 = emit_ln(p, ln2_scope, st, "ln2", [xn_sb[d][:] for d in range(NCH)],
                           dst_bf=xn2bf_sb, gamma=[gc_sb[d][:] for d in range(NCH)],
                           beta=[bc_sb[d][:] for d in range(NCH)])
        qkv_src = xn2bf_sb
        s_rdy, nrdy = s_n2, n2
        end_scope = ln2_scope
    else:
        qkv_src = xnbf_sb
        s_rdy, nrdy = s_n1, n1
        end_scope = ln1_scope

    for e in (p.s, p.v, p.t):
        e.wait_ge(s_rdy, nrdy)
    end_scope.close()
    if layer != 0:
        x_scope.close()

    w_sb = [sb(st, nc, f"w{d}", [128, 3072], BF16) for d in range(NCH)]
    gw_sb = [sb(st, nc, f"gw{d}", [128, NE], F32) for d in range(NCH)]
    p.g.wait_ge(s_rdy, nrdy)
    for d in range(NCH):
        p.g.dma_start(w_sb[d][:], wqkvT[d]).then_inc(s_ld, 16)
        p.g.dma_start(gw_sb[d][:], gwT[d]).then_inc(s_ld, 16)
        ld += 32
    LD_W = ld
    for e in (p.s, p.v, p.t):
        e.wait_ge(s_ld, LD_W)

    qkv_scope = ExitStack()
    ev_qk = StagedEvac(p, qkv_scope, st, "eqk")

    def evac_qk(oc, blk, bank, s_mm, mmcnt):
        dst = o_q if oc < 8 else o_k
        return ev_qk(bank[:], s_mm, mmcnt, dst[oc % 8, :, cols(blk)],
                     scale=(0.125 if oc < 8 else 1.0))

    emit_fm_matmul(p, qkv_scope, st, "qk", list(range(16)), NCH,
                   lambda d, oc: w_sb[d][:, oc*128:(oc+1)*128],
                   lambda d, blk: qkv_src[d][:, cols(blk)], evac_qk)

    def evac_v(tt, vb, bank, s_mm, mmcnt):
        return ev_qk(bank[:], s_mm, mmcnt, o_v[tt, :, vb*TB:(vb+1)*TB])

    s_ev_v, n_ev_v = emit_fm_matmul(p, qkv_scope, st, "v", list(range(NTILE)), NCH,
                   lambda d, tt: qkv_src[d][:, tt*128:(tt+1)*128],
                   lambda d, vb: w_sb[d][:, 2048+vb*TB:2048+(vb+1)*TB],
                   evac_v, nblk=2)
    ev_qk.flush_wait()
    p.t.wait_ge(s_ev_v, n_ev_v)
    p.v.wait_ge(ev_qk.s_dma, 16 * ev_qk.g)
    p.s.wait_ge(ev_qk.s_dma, 16 * ev_qk.g)
    qkv_scope.close()

    gscope = ExitStack()
    ps_g = [ps(gscope, nc, f"gps{i}", [128, NE]) for i in range(2)]
    ps_t = [ps(gscope, nc, f"gpt{i}", [128, 128]) for i in range(2)]
    s_gmm = sem(st, nc, "gmm")
    s_gv = sem(st, nc, "gv")
    s_gt = sem(st, nc, "gt")
    gateT_sb = sb(gscope, nc, "gateT", [NE, NTOK], F32)
    zeroNE = sb(gscope, nc, "zeroNE", [128, NE], F32)
    p.v.memset(zeroNE[:], 0.0)
    coef = [sb(gscope, nc, f"coef{i}", [128, NE], F32) for i in range(2)]
    work = {nm: sb(gscope, nc, f"gw_{nm}", [128, NE], F32)
            for nm in ("e", "pp", "pm", "ge2", "pw")}
    colw = {nm: sb(gscope, nc, f"gc_{nm}", [128, 1], F32)
            for nm in ("m", "negm", "sum", "rsum", "m1", "m2", "den", "rden")}
    VPT = 3 if layer == 0 else 1
    for tt in range(NTILE):
        bank = ps_g[tt % 2]
        if tt >= 2:
            p.t.wait_ge(s_gv, VPT * (tt - 2) + VPT)
        for d in range(NCH):
            h = p.t.matmul(bank[:], xn_sb[d][:, tt*128:(tt+1)*128],
                           gw_sb[d][:], start=(d == 0), stop=(d == NCH - 1))
        h.then_inc(s_gmm, 1)
        cf = coef[tt % 2]
        p.v.wait_ge(s_gmm, 2 * tt + 1)
        if layer == 0:
            p.v.reduce_max(colw["m"][:], bank[:], axis=mybir.AxisListType.X)
            p.v.tensor_scalar(colw["negm"][:], colw["m"][:], -1.0, None, OP.mult)\
                .then_inc(s_gv, 1)
            p.s.wait_ge(s_gv, VPT * tt + 1)
            p.s.activation(work["e"][:], bank[:], AF.Exp, bias=colw["negm"][:])\
                .then_inc(s_gv, 1)
            p.v.wait_ge(s_gv, VPT * tt + 2)
            p.v.reduce_sum(colw["sum"][:], work["e"][:], axis=mybir.AxisListType.X)
            p.v.reciprocal(colw["rsum"][:], colw["sum"][:])
            p.v.tensor_scalar(work["pp"][:], work["e"][:], colw["rsum"][:], None, OP.mult)
            p.v.reduce_max(colw["m1"][:], work["pp"][:], axis=mybir.AxisListType.X)
            p.v.tensor_scalar(work["ge2"][:], work["pp"][:], colw["m1"][:], None, OP.is_ge)
            p.v.tensor_tensor(work["pm"][:], work["pp"][:], work["ge2"][:], OP.mult)
            p.v.tensor_tensor(work["pm"][:], work["pp"][:], work["pm"][:], OP.subtract)
            p.v.reduce_max(colw["m2"][:], work["pm"][:], axis=mybir.AxisListType.X)
            p.v.tensor_tensor(colw["den"][:], colw["m1"][:], colw["m2"][:], OP.add)
            p.v.reciprocal(colw["rden"][:], colw["den"][:])
            p.v.tensor_scalar(work["ge2"][:], work["pp"][:], colw["m2"][:], None, OP.is_ge)
            p.v.tensor_scalar(work["pw"][:], work["pp"][:], colw["rden"][:], None, OP.mult)
            p.v.tensor_tensor(cf[:], work["pw"][:], work["ge2"][:], OP.mult)\
                .then_inc(s_gv, 1)
        else:
            p.v.tensor_copy(cf[:], bank[:]).then_inc(s_gv, 1)
        tbk = ps_t[tt % 2]
        p.t.wait_ge(s_gv, VPT * tt + VPT)
        if tt >= 2:
            p.t.wait_ge(s_gt, tt - 1)
        p.t.transpose(tbk[:NE, :], cf[:], id_sb[:]).then_inc(s_gmm, 1)
        p.v.wait_ge(s_gmm, 2 * tt + 2)
        if layer == 0:
            hh = p.v.tensor_copy(gateT_sb[:, tt*128:(tt+1)*128], tbk[:NE, :])
        else:
            hh = p.v.tensor_scalar(gateT_sb[:, tt*128:(tt+1)*128], tbk[:NE, :],
                                   bg_sb[:], None, OP.add)
        hh.then_inc(s_gt, 1)
    p.y.wait_ge(s_gt, NTILE)
    p.y.dma_start(o_gate, gateT_sb[:]).then_inc(s_fin, 16)
    fin += 16

    p.y.wait_ge(s_fin, fin)
    p.build()
    gscope.close()
    qkv_scope.close()
    st.close()
    return nc


def host_prep_inputs(layer, d, core):
    half = core % 2
    b = core // 2
    idx = own_token_idx(half)
    if layer == 0:
        x = d["L0_attn_x"][b][idx]
        wq = d["L0_attn_w_qkv"]
        wqkvT = np.ascontiguousarray(wq.T).reshape(NCH, 128, 3072)
        gwT = np.ascontiguousarray(d["L0_ffn_gate_w"].T).reshape(NCH, 128, 14)
        m = {
            "xT": np.ascontiguousarray(x.T).reshape(NCH, 128, NTOK),
            "wqkvT": wqkvT,
            "gwT": gwT,
            "ident": np.eye(128, dtype=np.float32),
            "gcol": d["L0_attn_ln_gamma"].reshape(NCH, 128, 1).astype(np.float32),
            "bcol": d["L0_attn_ln_beta"].reshape(NCH, 128, 1).astype(np.float32),
        }
    else:
        x = d["x1"][b][idx]
        wqkvT = np.ascontiguousarray(d["L1_attn_W_qkv"]).reshape(NCH, 128, 3072)
        gwT = np.ascontiguousarray(d["L1_ffn_Wg"]).reshape(NCH, 128, 16)
        m = {
            "xT": np.ascontiguousarray(x.T).reshape(NCH, 128, NTOK),
            "wqkvT": wqkvT,
            "gwT": gwT,
            "ident": np.eye(128, dtype=np.float32),
            "bgc": d["L1_ffn_bg"].reshape(16, 1).astype(np.float32),
        }
    return m


def attn_schedule(alibi):
    for J in range(4):
        for h in range(16):
            qb = qblk_of_head(h, alibi)
            for s_ in range(512 // qb):
                off = J * 512 + s_ * qb
                j0 = off // 128
                QB = max(qb // 128, 1)
                KT = 2 * (j0 + QB)
                NM = 2 * QB
                yield (J, h, qb, j0, off, KT, NM)

def qblk_of_head(h, alibi):
    if not alibi:
        return 512
    return {0: 64, 1: 128, 2: 128, 3: 128, 4: 128, 5: 256, 6: 256}.get(h, 512)

MASK_CLASSES_L0 = ((512, 8, [0], [0]),)
MASK_CLASSES_L1 = ((512, 8, [0], [0]), (256, 4, [0], [0]),
                   (128, 2, [0, 1], [0]), (64, 2, [0, 1], [0, 1]))

def mask_patterns(half, alibi):
    pats = {}
    def par_fn(lt):
        return (lt % 2) if half == 0 else (1 - lt % 2)
    for qb, rels, pars, h64s in (MASK_CLASSES_L1 if alibi else MASK_CLASSES_L0):
        QB = max(qb // 128, 1)
        for par in pars:
            for h64 in h64s:
                j0 = par
                for rel in range(rels):
                    m = np.zeros((128, qb), np.float32)
                    for c in range(qb):
                        lt = j0 + (c + (h64 * 64 if qb == 64 else 0)) // 128
                        a = 2 * lt + par_fn(lt)
                        qpos = a * 128 + (c + h64 * 64) % 128
                        kbase = (2 * j0 + rel) * 128
                        i = np.arange(128)
                        m[:, c] = (kbase + i <= qpos).astype(np.float32)
                    pats[(qb, rel, par, h64)] = m
    keys = sorted(pats.keys())
    offs, bank, o = {}, [], 0
    for k in keys:
        offs[k] = o
        bank.append(pats[k])
        o += k[0]
    return np.concatenate(bank, axis=1), offs

def mask_offsets(alibi):
    offs, o = {}, 0
    for k in sorted([(qb, rel, par, h64)
                     for qb, rels, pars, h64s in (MASK_CLASSES_L1 if alibi else MASK_CLASSES_L0)
                     for par in pars for h64 in h64s for rel in range(rels)]):
        offs[k] = o
        o += k[0]
    return offs, o

def host_attn_inputs(layer, core, q, k_full, vpad, woutT, boutcol):
    half = core % 2
    alibi = layer == 1
    tm = tmap(half)
    maskbank, _ = mask_patterns(half, alibi)
    ncols = sum(KT for (_, _, _, _, _, KT, _) in attn_schedule(alibi))
    biascols = np.zeros((ncols, 128), np.float32)
    i = 0
    for (J, h, qb, j0, off, KT, NM) in attn_schedule(alibi):
        lts = range(j0, j0 + max(qb // 128, 1))
        if qb == 64:
            a = 2 * j0 + ((j0 % 2) if half == 0 else (1 - j0 % 2))
            h64 = (off % 128) // 64
            maxpos = a * 128 + h64 * 64 + 63
        else:
            maxpos = max(2 * lt + (((lt % 2) if half == 0 else (1 - lt % 2)))
                         for lt in lts) * 128 + 127
        q_end = maxpos + 1
        own_max_tile = maxpos // 128
        sl = ALIBI_SLOPES[h] if alibi else 0.0
        for kt in range(KT):
            if kt > own_max_tile:
                biascols[i] = -60000.0
            elif alibi:
                biascols[i] = sl * ((kt * 128 + np.arange(128)) - q_end)
            i += 1
    import ml_dtypes
    m = {
        "q": q, "k": k_full, "vpad": vpad, "woutT": woutT, "boutcol": boutcol,
        "maskbank": np.ascontiguousarray(maskbank).astype(ml_dtypes.bfloat16),
    }
    if alibi:
        m["biascols"] = np.ascontiguousarray(biascols.T)
    return m


def build_attn(layer):
    alibi = layer == 1
    nc = bass.Bass("TRN2", target_bir_lowering=False)
    moffs, mcols = mask_offsets(alibi)
    ncols = sum(KT for (_, _, _, _, _, KT, _) in attn_schedule(alibi))
    q_h = nc.dram_tensor("q", [NCH, 128, NTOK], BF16, kind="ExternalInput").ap()
    k_h = nc.dram_tensor("k", [NCH, 128, 4096], BF16, kind="ExternalInput").ap()
    v_h = nc.dram_tensor("vpad", [32, 128, 1040], BF16, kind="ExternalInput").ap()
    w_h = nc.dram_tensor("woutT", [NCH, 128, D], F32, kind="ExternalInput").ap()
    bo_h = nc.dram_tensor("boutcol", [NCH, 128, 1], F32, kind="ExternalInput").ap()
    if alibi:
        bc_h = nc.dram_tensor("biascols", [128, ncols], F32, kind="ExternalInput").ap()
    mk_h = nc.dram_tensor("maskbank", [128, mcols], BF16, kind="ExternalInput").ap()
    o_s = nc.dram_tensor("o_s", [NCH, 128, NTOK], F32, kind="ExternalOutput").ap()

    p = Plan(nc)
    st = ExitStack()
    qj_sb = [[sb(st, nc, f"q{i}_{d}", [128, TB], BF16) for d in range(NCH)]
             for i in range(2)]
    k_sb = [sb(st, nc, f"k{d}", [128, 4096], BF16) for d in range(NCH)]
    v_sb = [sb(st, nc, f"v{t}", [128, 1040], BF16) for t in range(32)]
    w_sb = [sb(st, nc, f"w{d}", [128, D], BF16) for d in range(NCH)]
    bo_sb = [sb(st, nc, f"bo{d}", [128, 1], F32) for d in range(NCH)]
    bc_sb = sb(st, nc, "bc", [128, ncols], F32) if alibi else None
    mk_sb = sb(st, nc, "mk", [128, mcols], BF16)
    ones_f = sb(st, nc, "onesf", [1, 64], F32)
    abuf = [sb(st, nc, f"ab{i}", [128, TB], BF16) for i in range(4)]
    rrow = sb(st, nc, "rrow", [1, TB], F32)
    rb_sb = sb(st, nc, "rb", [64, TB], F32)
    s_buf = [sb(st, nc, f"sb{d}", [128, TB], BF16) for d in range(NCH)]

    s_ld = sem(st, nc, "ld")
    s_sc = sem(st, nc, "sc")
    s_exp = sem(st, nc, "exp")
    s_msk = sem(st, nc, "msk")
    s_av = sem(st, nc, "av")
    s_fz = sem(st, nc, "fz")
    s_pj = sem(st, nc, "pj")
    s_fin = sem(st, nc, "fin")

    s_qld = sem(st, nc, "qld")
    qld = [0]
    def load_qJ(Jn):
        for d in range(NCH):
            p.y.dma_start(qj_sb[Jn % 2][d][:], q_h[d, :, Jn*TB:(Jn+1)*TB])\
                .then_inc(s_qld, 16)
            qld[0] += 16
    ld = 0
    load_qJ(0)
    load_qJ(1)
    for d in range(NCH):
        p.y.dma_start(k_sb[d][:], k_h[d]).then_inc(s_ld, 16); ld += 16
    for t in range(32):
        p.y.dma_start(v_sb[t][:], v_h[t]).then_inc(s_ld, 16); ld += 16
    for d in range(NCH):
        p.g.dma_start(w_sb[d][:], w_h[d]).then_inc(s_ld, 16); ld += 16
        p.g.dma_start(bo_sb[d][:], bo_h[d]).then_inc(s_ld, 16); ld += 16
    if alibi:
        p.g.dma_start(bc_sb[:], bc_h).then_inc(s_ld, 16); ld += 16
    p.g.dma_start(mk_sb[:], mk_h).then_inc(s_ld, 16); ld += 16
    p.v.memset(ones_f[:], 1.0)
    for e in (p.t, p.s, p.v):
        e.wait_ge(s_ld, ld)

    ps_sc = [ps(st, nc, f"psc{i}", [128, TB]) for i in range(3)]
    ps_o = ps(st, nc, "pso", [65, TB])
    ps_b = ps(st, nc, "psb", [64, TB])
    ps_pj = [ps(st, nc, f"pspj{i}", [128, TB]) for i in range(2)]
    ev = StagedEvac(p, st, st, "es", n_stages=2, dtype=F32)

    C = dict(sc=0, exp=0, msk=0, av=0, fz=0, pj=0, g=0)
    frees_pj = []
    colix = 0
    sched = list(attn_schedule(alibi))
    si = 0
    for J in range(4):
        p.t.wait_ge(s_qld, 128 * (J + 1))
        items = [it for it in sched if it[0] == J]
        for (J_, h, qb, j0, off, KT, NM) in items:
            hp, hh = h // 2, h % 2
            rows = slice(hh * 64, hh * 64 + 64)
            qcols = slice(off - J * 512, off - J * 512 + qb)
            for kt in range(KT):
                g = C["g"]
                bank = ps_sc[g % 3]
                if g >= 3:
                    p.t.wait_ge(s_exp, g - 2)
                p.t.matmul(bank[:, :qb], k_sb[hp][rows, kt*128:(kt+1)*128],
                           qj_sb[J % 2][hp][rows, qcols], start=True, stop=True)\
                    .then_inc(s_sc, 1)
                C["sc"] += 1
                ab = abuf[g % 4]
                p.s.wait_ge(s_sc, C["sc"])
                if g >= 4:
                    p.s.wait_ge(s_av, g - 3)
                bias = bc_sb[:, colix:colix+1] if alibi else 0.0
                p.s.activation(ab[:, :qb], bank[:, :qb], AF.Exp, bias=bias)\
                    .then_inc(s_exp, 1)
                C["exp"] += 1
                colix += 1
                masked = kt >= KT - NM
                if masked:
                    rel = kt - (KT - NM)
                    par = j0 % 2 if qb <= 128 else 0
                    h64 = (off % 128) // 64 if qb == 64 else 0
                    mo = moffs[(qb, rel, par, h64)]
                    p.v.wait_ge(s_exp, C["exp"])
                    p.v.tensor_tensor(ab[:, :qb], ab[:, :qb],
                                      mk_sb[:, mo:mo+qb], OP.mult)\
                        .then_inc(s_msk, 1)
                    C["msk"] += 1
                if masked:
                    p.t.wait_ge(s_msk, C["msk"])
                else:
                    p.t.wait_ge(s_exp, C["exp"])
                if kt == 0:
                    if si > 0:
                        p.t.wait_ge(s_fz, 3 * si)
                p.t.matmul(ps_o[:, :qb], v_sb[kt][:, h*65:(h+1)*65],
                           ab[:, :qb], start=(kt == 0), stop=(kt == KT - 1))\
                    .then_inc(s_av, 1)
                C["av"] += 1
                C["g"] += 1
            p.v.wait_ge(s_av, C["av"])
            p.v.reciprocal(rrow[:, :qb], ps_o[64:65, :qb]).then_inc(s_fz, 1)
            C["fz"] += 1
            p.t.wait_ge(s_fz, C["fz"])
            if si > 0:
                p.t.wait_ge(s_fz, 3 * si)
            p.t.matmul(ps_b[:, :qb], ones_f[:], rrow[:, :qb], start=True,
                       stop=True).then_inc(s_sc, 1)
            C["sc"] += 1
            p.v.wait_ge(s_sc, C["sc"])
            p.v.tensor_copy(rb_sb[:, :qb], ps_b[:, :qb]).then_inc(s_fz, 1)
            C["fz"] += 1
            scols = slice(off - J * 512, off - J * 512 + qb)
            p.v.tensor_tensor(s_buf[hp][rows, scols], ps_o[0:64, :qb],
                              rb_sb[:, :qb], OP.mult).then_inc(s_fz, 1)
            C["fz"] += 1
            si += 1
        p.t.wait_ge(s_fz, 3 * si)
        for oc in range(NCH):
            bank = ps_pj[oc % 2]
            gp = J * NCH + oc
            if gp >= 2:
                fs, fv = frees_pj[gp - 2]
                p.t.wait_ge(fs, fv)
            for e in range(NCH):
                hmm = p.t.matmul(bank[:], w_sb[e][:, oc*128:(oc+1)*128],
                                 s_buf[e][:], start=(e == 0), stop=(e == NCH - 1))
            hmm.then_inc(s_pj, 1)
            C["pj"] += 1
            frees_pj.append(ev(bank[:], s_pj, C["pj"], o_s[oc, :, J*512:(J+1)*512],
                               bias=bo_sb[oc][:]))
        p.v.wait_ge(s_pj, C["pj"])
        if J + 2 < 4:
            p.y.wait_ge(s_pj, C["pj"])
            load_qJ(J + 2)
    ev.flush_wait()
    p.build()
    st.close()
    return nc


def build_moe0(NEXP=14):
    nc = bass.Bass("TRN2", target_bir_lowering=False)
    NE = 14
    xnbf_h = nc.dram_tensor("xnbf", [NCH, 128, NTOK], BF16, kind="ExternalInput").ap()
    xn_h = nc.dram_tensor("xn", [NCH, 128, NTOK], F32, kind="ExternalInput").ap()
    cf_h = nc.dram_tensor("coeffT", [NE, NTOK], F32, kind="ExternalInput").ap()
    w1_h = nc.dram_tensor("w1", [NE, NCH, 128, 2048], F32, kind="ExternalInput").ap()
    b1_h = nc.dram_tensor("b1c", [128, NE * 16], F32, kind="ExternalInput").ap()
    w2_h = nc.dram_tensor("w2", [NE, 16, 128, D], F32, kind="ExternalInput").ap()
    b2_h = nc.dram_tensor("b2r", [NE, D], F32, kind="ExternalInput").ap()
    x_h = nc.dram_tensor("xT", [NCH, 128, NTOK], F32, kind="ExternalInput").ap()
    s_h = nc.dram_tensor("s0", [NCH, 128, NTOK], F32, kind="ExternalInput").ap()
    sel_h = nc.dram_tensor("sel", [14, 128 * 14], F32, kind="ExternalInput").ap()
    o_x1 = nc.dram_tensor("o_x1", [NCH, 128, NTOK], F32, kind="ExternalOutput").ap()

    p = Plan(nc)
    st = ExitStack()
    xnbf_sb = [sb(st, nc, f"xnbf{d}", [128, NTOK], BF16) for d in range(NCH)]
    gacc = [sb(st, nc, f"ga{d}", [128, NTOK], F32) for d in range(NCH)]
    cf_sb = sb(st, nc, "cfsb", [NE, NTOK], F32)
    b1_sb = sb(st, nc, "b1sb", [128, NE * 16], F32)
    b2_sb = sb(st, nc, "b2sb", [NE, D], F32)
    ones_f = sb(st, nc, "onesf", [1, 128], F32)
    sel_sb = sb(st, nc, "selsb", [14, 128 * 14], F32)
    cb_sb = sb(st, nc, "cb", [128, TB], F32)
    tmp_sb = sb(st, nc, "tmp", [128, TB], F32)
    s_ld = sem(st, nc, "ld")
    s_w = sem(st, nc, "w")
    s_mm = sem(st, nc, "mm")
    s_ev = sem(st, nc, "ev")
    s_acc = sem(st, nc, "acc")
    s_cb = sem(st, nc, "cbk")
    s_fin = sem(st, nc, "fin")
    p.v.memset(ones_f[:], 1.0)

    ld = 0
    for d in range(NCH):
        p.y.dma_start(xnbf_sb[d][:], xnbf_h[d]).then_inc(s_ld, 16); ld += 16
    p.y.dma_start(cf_sb[:], cf_h).then_inc(s_ld, 16); ld += 16
    p.y.dma_start(b1_sb[:], b1_h).then_inc(s_ld, 16); ld += 16
    p.y.dma_start(b2_sb[:], b2_h).then_inc(s_ld, 16); ld += 16
    p.y.dma_start(sel_sb[:], sel_h).then_inc(s_ld, 16); ld += 16
    for e in (p.t, p.s, p.v):
        e.wait_ge(s_ld, ld)

    iscope = ExitStack()
    xnb = [sb(iscope, nc, f"xnb{i}", [128, NTOK], F32, side="right") for i in range(2)]
    with ExitStack() as sti:
        ps_i = [ps(sti, nc, f"pi{i}", [128, TB]) for i in range(4)]
        g = 0
        for d in range(NCH):
            if d >= 2:
                p.y.wait_ge(s_acc, (d - 1) * NBLK)
            p.y.dma_start(xnb[d % 2][:], xn_h[d]).then_inc(s_ld, 16)
            ld += 16
            p.v.wait_ge(s_ld, ld)
            for blk in range(NBLK):
                bank = ps_i[g % 4]
                if g >= 4:
                    p.t.wait_ge(s_acc, g - 3)
                p.t.matmul(bank[:], b2_sb[:, d*128:(d+1)*128], cf_sb[:, cols(blk)],
                           start=True, stop=True).then_inc(s_mm, 1)
                p.v.wait_ge(s_mm, g + 1)
                p.v.tensor_tensor(gacc[d][:, cols(blk)], xnb[d % 2][:, cols(blk)],
                                  bank[:], OP.add).then_inc(s_acc, 1)
                g += 1
        NB_INIT = g

    iscope.close()
    wscope = ExitStack()
    w1_sb = [sb(wscope, nc, f"w1_{d}", [128, 2048], BF16) for d in range(NCH)]
    w2_sb = [sb(wscope, nc, f"w2_{f}", [128, D], BF16) for f in range(16)]
    h_sb = [sb(wscope, nc, f"h{f}", [128, TB], BF16) for f in range(16)]
    ps_h = [ps(wscope, nc, f"ph{i}", [128, TB]) for i in range(2)]
    ps_o = [ps(wscope, nc, f"po{i}", [128, TB]) for i in range(2)]
    ps_c = ps(wscope, nc, "pc", [128, TB])
    wld = [0]
    ghc = [0]
    omc = [0]
    acc2 = [NB_INIT]
    cbc = [0]
    mmc = [NB_INIT]

    def load_we(e):
        p.g.wait_ge(s_acc, NB_INIT + e * 32)
        for d in range(NCH):
            p.g.dma_start(w1_sb[d][:], w1_h[e, d]).then_inc(s_w, 16); wld[0] += 16
        for f in range(16):
            p.g.dma_start(w2_sb[f][:], w2_h[e, f]).then_inc(s_w, 16); wld[0] += 16

    for e in range(NEXP):
        load_we(e)
        p.t.wait_ge(s_w, wld[0])
        for tb in range(NBLK):
            if cbc[0] > 0:
                p.t.wait_ge(s_cb, 2 * cbc[0])
            p.t.matmul(ps_c[:], sel_sb[:, e*128:(e+1)*128], cf_sb[:, cols(tb)],
                       start=True, stop=True).then_inc(s_mm, 1)
            mmc[0] += 1
            p.v.wait_ge(s_mm, mmc[0])
            p.v.tensor_copy(cb_sb[:], ps_c[:]).then_inc(s_cb, 1)
            for fc in range(16):
                bank = ps_h[fc % 2]
                gh = ghc[0]
                if gh >= 2:
                    p.t.wait_ge(s_ev, gh - 1)
                for d in range(NCH):
                    h = p.t.matmul(bank[:], w1_sb[d][:, fc*128:(fc+1)*128],
                                   xnbf_sb[d][:, cols(tb)],
                                   start=(d == 0), stop=(d == NCH - 1))
                h.then_inc(s_mm, 1)
                mmc[0] += 1
                p.s.wait_ge(s_mm, mmc[0])
                p.s.activation(h_sb[fc][:], bank[:], AF.Gelu,
                               bias=b1_sb[:, e*16+fc:e*16+fc+1]).then_inc(s_ev, 1)
                ghc[0] += 1
            for dd in range(NCH):
                bank = ps_o[dd % 2]
                om = omc[0]
                if dd == 0:
                    p.t.wait_ge(s_ev, ghc[0])
                if om >= 2:
                    p.t.wait_ge(s_acc, acc2[0] - 1)
                for f in range(16):
                    h = p.t.matmul(bank[:], w2_sb[f][:, dd*128:(dd+1)*128],
                                   h_sb[f][:], start=(f == 0), stop=(f == 15))
                h.then_inc(s_mm, 1)
                mmc[0] += 1
                p.v.wait_ge(s_mm, mmc[0])
                p.v.tensor_tensor(tmp_sb[:], bank[:], cb_sb[:], OP.mult)
                p.v.tensor_tensor(gacc[dd][:, cols(tb)], gacc[dd][:, cols(tb)],
                                  tmp_sb[:], OP.add).then_inc(s_acc, 1)
                acc2[0] += 1
                omc[0] += 1
            p.v.nop().then_inc(s_cb, 1)
            cbc[0] += 1
    wscope.close()

    fscope = ExitStack()
    xb = [sb(fscope, nc, f"xb{i}", [128, NTOK], F32, side="right") for i in range(2)]
    sb_ = [sb(fscope, nc, f"sbb{i}", [128, NTOK], F32, side="right") for i in range(2)]
    gb = sb(fscope, nc, "gb", [128, NTOK], F32, side="right")
    x1b = [sb(fscope, nc, f"x1b{i}", [128, NTOK], F32, side="right") for i in range(2)]
    s_clip = sem(st, nc, "clip")
    s_silu = sem(st, nc, "silu")
    s_x1 = sem(st, nc, "x1")
    s_od = sem(st, nc, "od")
    echo = sb(fscope, nc, "echo", [128, 1], F32, side="right")
    p.y.wait_ge(s_acc, acc2[0])
    for d in range(NCH):
        if d >= 2:
            p.y.wait_ge(s_x1, d - 1)
        p.y.dma_start(xb[d % 2][:], x_h[d]).then_inc(s_ld, 16)
        p.y.dma_start(sb_[d % 2][:], s_h[d]).then_inc(s_ld, 16)
        ld += 32
        p.v.wait_ge(s_ld, ld)
        p.v.tensor_scalar(gacc[d][:], gacc[d][:], 100.0, -100.0, OP.min, OP.max)\
            .then_inc(s_clip, 1)
        p.s.wait_ge(s_clip, d + 1)
        if d >= 1:
            p.s.wait_ge(s_x1, d)
        p.s.activation(gb[:], gacc[d][:], AF.Silu).then_inc(s_silu, 1)
        p.v.wait_ge(s_silu, d + 1)
        if d >= 2:
            p.v.wait_ge(s_od, 16 * (d - 1))
        p.v.tensor_tensor(gb[:], gb[:], sb_[d % 2][:], OP.mult)
        p.v.tensor_tensor(x1b[d % 2][:], gb[:], xb[d % 2][:], OP.add)
        p.v.tensor_copy(echo[:], x1b[d % 2][:, NTOK-1:NTOK]).then_inc(s_x1, 1)
        p.y.wait_ge(s_x1, d + 1)
        p.y.dma_start(o_x1[d], x1b[d % 2][:]).then_inc(s_od, 16)
    p.y.wait_ge(s_od, 16 * NCH)
    p.build()
    fscope.close()
    st.close()
    return nc


def host_moe0_inputs(d, core, xn, xnbf, coeffT, s0, ri=None):
    half = core % 2
    b = core // 2
    return {
        "xnbf": xnbf, "xn": xn, "coeffT": coeffT,
        "w1": np.ascontiguousarray(d["L0_ffn_w1"].reshape(14, 8, 128, 2048)),
        "b1c": np.ascontiguousarray(d["L0_ffn_b1"].reshape(14 * 16, 128).T),
        "w2": np.ascontiguousarray(d["L0_ffn_w2"].reshape(14, 16, 128, 1024)),
        "b2r": np.ascontiguousarray(d["L0_ffn_b2"]),
        "xT": np.ascontiguousarray(d["L0_attn_x"][b][own_token_idx(half)].T)
              .reshape(NCH, 128, NTOK),
        "s0": s0,
        "sel": np.ascontiguousarray(
            np.repeat(np.eye(14, dtype=np.float32), 128, axis=1)),
    }


def build_ffn1(NEXP=16, BIS=40):
    nc = bass.Bass("TRN2", target_bir_lowering=False)
    NE = 16
    S = 4096
    xnbf_h = nc.dram_tensor("xnbf", [NCH, 128, NTOK], BF16, kind="ExternalInput").ap()
    lg_h = nc.dram_tensor("logitsP", [NE, S], F32, kind="ExternalInput").ap()
    w1_h = nc.dram_tensor("w1", [NE, NCH, 128, D], F32, kind="ExternalInput").ap()
    w2_h = nc.dram_tensor("w2", [NE, NCH, 128, D], F32, kind="ExternalInput").ap()
    w3_h = nc.dram_tensor("w3", [NE, NCH, 128, D], F32, kind="ExternalInput").ap()
    b1_h = nc.dram_tensor("b1r", [1, NE], F32, kind="ExternalInput").ap()
    b2_h = nc.dram_tensor("b2r", [1, NE], F32, kind="ExternalInput").ap()
    b3_h = nc.dram_tensor("b3c", [NE, 1], F32, kind="ExternalInput").ap()
    sel_h = nc.dram_tensor("sel", [NE, 128 * NE], F32, kind="ExternalInput").ap()
    x_h = nc.dram_tensor("x1T", [NCH, 128, NTOK], F32, kind="ExternalInput").ap()
    s_h = nc.dram_tensor("s1", [NCH, 128, NTOK], F32, kind="ExternalInput").ap()
    o_x2 = nc.dram_tensor("o_x2", [NCH, 128, NTOK], F32, kind="ExternalOutput").ap()
    o_cf = nc.dram_tensor("o_cf", [NE, S], F32, kind="ExternalOutput").ap()

    p = Plan(nc)
    st = ExitStack()
    xnbf_sb = [sb(st, nc, f"xnbf{d}", [128, NTOK], BF16) for d in range(NCH)]
    gacc = [sb(st, nc, f"ga{d}", [128, NTOK], F32) for d in range(NCH)]
    coefP = sb(st, nc, "coefP", [128, S], F32)
    sel_sb = sb(st, nc, "selsb", [NE, 128 * NE], F32)
    b3c_sb = sb(st, nc, "b3csb", [NE, 1], F32)
    ones_f = sb(st, nc, "onesf", [1, 128], F32)
    cb_sb = sb(st, nc, "cb", [128, TB], F32)
    tmp_sb = sb(st, nc, "tmp", [128, TB], F32)
    B1c = sb(st, nc, "B1c", [128, NE], F32)
    B2c = sb(st, nc, "B2c", [128, NE], F32)
    rowb = sb(st, nc, "rowb", [1, TB], F32)
    colw = {nm: sb(st, nc, f"c_{nm}", [128, 1], F32)
            for nm in ("mx", "negmx", "sum", "rsum", "lo", "hi", "u", "t",
                       "cnt", "sel1", "nsel", "dlt")}
    s_ld = sem(st, nc, "ld")
    s_w = sem(st, nc, "w")
    s_mm = sem(st, nc, "mm")
    s_ev = sem(st, nc, "ev")
    s_acc = sem(st, nc, "acc")
    s_cb = sem(st, nc, "cbk")
    s_gate = sem(st, nc, "gate")
    p.v.memset(ones_f[:], 1.0)

    ld = 0
    for d in range(NCH):
        p.y.dma_start(xnbf_sb[d][:], xnbf_h[d]).then_inc(s_ld, 16); ld += 16
    p.y.dma_start(sel_sb[:], sel_h).then_inc(s_ld, 16); ld += 16
    p.y.dma_start(b3c_sb[:], b3_h).then_inc(s_ld, 16); ld += 16

    gscope = ExitStack()
    lv_sb = sb(gscope, nc, "lv", [128, S], F32, side="right")
    pv_sb = sb(gscope, nc, "pv", [128, S], F32, side="right")
    ge_sb = sb(gscope, nc, "ge", [128, S], F32, side="right")
    b1r_sb = sb(gscope, nc, "b1rsb", [1, NE], F32, side="right")
    b2r_sb = sb(gscope, nc, "b2rsb", [1, NE], F32, side="right")
    p.v.memset(lv_sb[:], 0.0).then_inc(s_gate, 100)
    p.y.wait_ge(s_gate, 100)
    p.y.dma_start(lv_sb[0:NE, :], lg_h).then_inc(s_ld, 16); ld += 16
    p.y.dma_start(b1r_sb[:], b1_h).then_inc(s_ld, 16); ld += 16
    p.y.dma_start(b2r_sb[:], b2_h).then_inc(s_ld, 16); ld += 16
    for e in (p.t, p.s, p.v):
        e.wait_ge(s_ld, ld)
    with ExitStack() as stb:
        ps_bc = [ps(stb, nc, f"pbc{i}", [128, NE]) for i in range(2)]
        p.t.matmul(ps_bc[0][:], ones_f[:], b1r_sb[:], start=True, stop=True)
        p.t.matmul(ps_bc[1][:], ones_f[:], b2r_sb[:], start=True, stop=True)\
            .then_inc(s_mm, 1)
        p.v.wait_ge(s_mm, 1)
        p.v.tensor_copy(B1c[:], ps_bc[0][:])
        p.v.tensor_copy(B2c[:], ps_bc[1][:]).then_inc(s_gate, 1)
    X = mybir.AxisListType.X
    p.v.reduce_max(colw["mx"][:], lv_sb[:], axis=X)
    p.v.tensor_scalar(colw["negmx"][:], colw["mx"][:], -1.0, None, OP.mult)\
        .then_inc(s_gate, 1)
    p.s.wait_ge(s_gate, 102)
    p.s.activation(pv_sb[:], lv_sb[:], AF.Exp, bias=colw["negmx"][:])\
        .then_inc(s_gate, 1)
    p.v.wait_ge(s_gate, 103)
    p.v.reduce_sum(colw["sum"][:], pv_sb[:], axis=X)
    p.v.reciprocal(colw["rsum"][:], colw["sum"][:])
    p.v.tensor_scalar(pv_sb[:], pv_sb[:], colw["rsum"][:], None, OP.mult)
    p.v.memset(colw["lo"][:], 0.0)
    p.v.memset(colw["hi"][:], 1.0)
    for it in range(BIS):
        p.v.tensor_tensor(colw["u"][:], colw["lo"][:], colw["hi"][:], OP.add)
        p.v.tensor_scalar(colw["t"][:], colw["u"][:], 0.5, None, OP.mult)
        p.v.tensor_scalar(ge_sb[:], pv_sb[:], colw["t"][:], None, OP.is_ge)
        p.v.reduce_sum(colw["cnt"][:], ge_sb[:], axis=X)
        p.v.tensor_scalar(colw["sel1"][:], colw["cnt"][:], 256.0, None, OP.is_ge)
        p.v.tensor_scalar(colw["nsel"][:], colw["sel1"][:], -1.0, 1.0, OP.mult, OP.add)
        p.v.tensor_tensor(colw["dlt"][:], colw["t"][:], colw["lo"][:], OP.subtract)
        p.v.tensor_tensor(colw["dlt"][:], colw["dlt"][:], colw["sel1"][:], OP.mult)
        p.v.tensor_tensor(colw["lo"][:], colw["lo"][:], colw["dlt"][:], OP.add)
        p.v.tensor_tensor(colw["dlt"][:], colw["t"][:], colw["hi"][:], OP.subtract)
        p.v.tensor_tensor(colw["dlt"][:], colw["dlt"][:], colw["nsel"][:], OP.mult)
        p.v.tensor_tensor(colw["hi"][:], colw["hi"][:], colw["dlt"][:], OP.add)
    p.v.tensor_scalar(ge_sb[:], pv_sb[:], colw["lo"][:], None, OP.is_ge)
    p.v.tensor_tensor(coefP[:], pv_sb[:], ge_sb[:], OP.mult).then_inc(s_gate, 1)

    p.y.wait_ge(s_gate, 104)
    p.y.dma_start(o_cf, coefP[0:NE, :]).then_inc(s_ld, 16)
    ld += 16
    with ExitStack() as sti:
        ps_r = ps(sti, nc, "pr", [1, TB])
        ps_b = [ps(sti, nc, f"pb{i}", [128, TB]) for i in range(2)]
        p.t.wait_ge(s_gate, 104)
        g = 0
        acc0 = 0
        for tb in range(NBLK):
            if tb >= 1:
                p.t.wait_ge(s_gate, 104 + tb)
            p.t.matmul(ps_r[:], b3c_sb[:], coefP[0:NE, cols(tb)], start=True,
                       stop=True).then_inc(s_mm, 1)
            p.v.wait_ge(s_mm, 2 + 3 * tb)
            p.v.tensor_copy(rowb[:], ps_r[:]).then_inc(s_gate, 1)
            bank = ps_b[tb % 2]
            p.t.wait_ge(s_gate, 105 + tb)
            if tb >= 2:
                p.t.wait_ge(s_acc, (tb - 1) * NCH)
            p.t.matmul(bank[:], ones_f[:], rowb[:], start=True, stop=True)\
                .then_inc(s_mm, 2)
            p.v.wait_ge(s_mm, 4 + 3 * tb)
            for d in range(NCH):
                p.v.tensor_copy(gacc[d][:, cols(tb)], bank[:]).then_inc(s_acc, 1)
                acc0 += 1
        NB_INIT = acc0
        MM_INIT = 1 + 3 * NBLK
    gscope.close()

    wscope = ExitStack()
    w1_sb = [sb(wscope, nc, f"w1_{d}", [128, D], BF16) for d in range(NCH)]
    w2_sb = [sb(wscope, nc, f"w2_{d}", [128, D], BF16) for d in range(NCH)]
    w3_sb = [sb(wscope, nc, f"w3_{d}", [128, D], BF16) for d in range(NCH)]
    h1 = [sb(wscope, nc, f"h1_{d}", [128, TB], BF16) for d in range(NCH)]
    h2 = [sb(wscope, nc, f"h2_{d}", [128, TB], BF16) for d in range(NCH)]
    ps_h = [ps(wscope, nc, f"ph{i}", [128, TB]) for i in range(2)]
    ps_o = [ps(wscope, nc, f"po{i}", [128, TB]) for i in range(2)]
    ps_c = ps(wscope, nc, "pc", [128, TB])
    wld = [0]
    evc = [0]
    omc = [0]
    accc = [NB_INIT]
    cbc = [0]
    mmc = [MM_INIT]

    for e in range(NEXP):
        p.g.wait_ge(s_acc, NB_INIT + e * 32)
        for d in range(NCH):
            p.g.dma_start(w1_sb[d][:], w1_h[e, d]).then_inc(s_w, 16); wld[0] += 16
            p.g.dma_start(w2_sb[d][:], w2_h[e, d]).then_inc(s_w, 16); wld[0] += 16
            p.g.dma_start(w3_sb[d][:], w3_h[e, d]).then_inc(s_w, 16); wld[0] += 16
        p.t.wait_ge(s_w, wld[0])
        for tb in range(NBLK):
            if cbc[0] > 0:
                p.t.wait_ge(s_cb, 2 * cbc[0])
            p.t.matmul(ps_c[:], sel_sb[:, e*128:(e+1)*128], coefP[0:NE, cols(tb)],
                       start=True, stop=True).then_inc(s_mm, 1)
            mmc[0] += 1
            p.v.wait_ge(s_mm, mmc[0])
            p.v.tensor_copy(cb_sb[:], ps_c[:]).then_inc(s_cb, 1)
            for stage, wbuf, src, dst, bcol in (
                    (1, w1_sb, None, h1, B1c), (2, w2_sb, h1, h2, B2c)):
                for oc in range(NCH):
                    bank = ps_h[oc % 2]
                    gh = evc[0]
                    if stage == 2 and oc == 0:
                        p.t.wait_ge(s_ev, evc[0])
                    elif gh >= 2:
                        p.t.wait_ge(s_ev, gh - 1)
                    for d in range(NCH):
                        rhs = xnbf_sb[d][:, cols(tb)] if stage == 1 else src[d][:]
                        h = p.t.matmul(bank[:], wbuf[d][:, oc*128:(oc+1)*128],
                                       rhs, start=(d == 0), stop=(d == NCH - 1))
                    h.then_inc(s_mm, 1)
                    mmc[0] += 1
                    p.s.wait_ge(s_mm, mmc[0])
                    p.s.activation(dst[oc][:], bank[:], AF.Relu,
                                   bias=bcol[:, e:e+1]).then_inc(s_ev, 1)
                    evc[0] += 1
            for dd in range(NCH):
                bank = ps_o[dd % 2]
                om = omc[0]
                if dd == 0:
                    p.t.wait_ge(s_ev, evc[0])
                if om >= 2:
                    p.t.wait_ge(s_acc, accc[0] - 1)
                for f in range(NCH):
                    h = p.t.matmul(bank[:], w3_sb[f][:, dd*128:(dd+1)*128],
                                   h2[f][:], start=(f == 0), stop=(f == NCH - 1))
                h.then_inc(s_mm, 1)
                mmc[0] += 1
                p.v.wait_ge(s_mm, mmc[0])
                p.v.tensor_tensor(tmp_sb[:], bank[:], cb_sb[:], OP.mult)
                p.v.tensor_tensor(gacc[dd][:, cols(tb)], gacc[dd][:, cols(tb)],
                                  tmp_sb[:], OP.add).then_inc(s_acc, 1)
                accc[0] += 1
                omc[0] += 1
            p.v.nop().then_inc(s_cb, 1)
            cbc[0] += 1
    wscope.close()

    fscope = ExitStack()
    xb = [sb(fscope, nc, f"xb{i}", [128, NTOK], F32, side="right") for i in range(2)]
    sb_ = [sb(fscope, nc, f"sbb{i}", [128, NTOK], F32, side="right") for i in range(2)]
    gb = sb(fscope, nc, "gb", [128, NTOK], F32, side="right")
    x1b = [sb(fscope, nc, f"x1b{i}", [128, NTOK], F32, side="right") for i in range(2)]
    echo = sb(fscope, nc, "echo", [128, 1], F32, side="right")
    s_silu = sem(st, nc, "silu")
    s_x1 = sem(st, nc, "x1")
    s_od = sem(st, nc, "od")
    p.y.wait_ge(s_acc, accc[0])
    for d in range(NCH):
        if d >= 2:
            p.y.wait_ge(s_x1, d - 1)
        p.y.dma_start(xb[d % 2][:], x_h[d]).then_inc(s_ld, 16)
        p.y.dma_start(sb_[d % 2][:], s_h[d]).then_inc(s_ld, 16)
        ld += 32
        p.s.wait_ge(s_ld, ld)
        if d >= 1:
            p.s.wait_ge(s_x1, d)
        p.s.activation(gb[:], gacc[d][:], AF.Silu).then_inc(s_silu, 1)
        p.v.wait_ge(s_silu, d + 1)
        p.v.wait_ge(s_ld, ld)
        if d >= 2:
            p.v.wait_ge(s_od, 16 * (d - 1))
        p.v.tensor_tensor(gb[:], gb[:], sb_[d % 2][:], OP.mult)
        p.v.tensor_tensor(x1b[d % 2][:], gb[:], xb[d % 2][:], OP.add)
        p.v.tensor_copy(echo[:], x1b[d % 2][:, NTOK-1:NTOK]).then_inc(s_x1, 1)
        p.y.wait_ge(s_x1, d + 1)
        p.y.dma_start(o_x2[d], x1b[d % 2][:]).then_inc(s_od, 16)
    p.y.wait_ge(s_od, 16 * NCH)
    p.build()
    fscope.close()
    st.close()
    return nc


def host_ffn1_inputs(d, core, xnbf, logitsP, x1T, s1):
    return {
        "xnbf": xnbf, "logitsP": logitsP,
        "w1": np.ascontiguousarray(d["L1_ffn_w1"].reshape(16, 8, 128, 1024)),
        "w2": np.ascontiguousarray(d["L1_ffn_w2"].reshape(16, 8, 128, 1024)),
        "w3": np.ascontiguousarray(d["L1_ffn_w3"].reshape(16, 8, 128, 1024)),
        "b1r": np.ascontiguousarray(d["L1_ffn_b1"].reshape(1, 16)),
        "b2r": np.ascontiguousarray(d["L1_ffn_b2"].reshape(1, 16)),
        "b3c": np.ascontiguousarray(d["L1_ffn_b3"].reshape(16, 1)),
        "sel": np.ascontiguousarray(
            np.repeat(np.eye(16, dtype=np.float32), 128, axis=1)),
        "x1T": x1T, "s1": s1,
    }




X = mybir.AxisListType.X


def barrier(p, st, nc, name):
    s_b = sem(st, nc, name)
    p.y.nop().then_inc(s_b, 1)
    for e in (p.t, p.s, p.v, p.g, p.y):
        e.wait_ge(s_b, 1)


def emit_prep(p, st, nc, layer, io, do_main, do_kv, do_gate=None, x_is_bf16=False):
    NE = 14 if layer == 0 else 16
    if do_gate is None:
        do_gate = do_main
    xT, wqkvT, gwT, ident = io["xT"], io["wqkvT"], io["gwT"], io["ident"]

    x_scope = ExitStack()
    x_sb = [sb(x_scope, nc, f"x{d}", [128, NTOK], F32, side="right") for d in range(NCH)]
    xn_sb = [sb(st, nc, f"xn{d}", [128, NTOK], F32) for d in range(NCH)]
    xnbf_sb = [sb(st, nc, f"xnbf{d}", [128, NTOK], BF16) for d in range(NCH)]
    id_sb = sb(st, nc, "idf", [128, 128], F32)
    s_ld = sem(st, nc, "ld")
    s_fin = sem(st, nc, "fin")
    fin = 0
    ld = 0
    for d in range(NCH):
        eng = p.g if x_is_bf16 else p.y
        eng.dma_start(x_sb[d][:], xT[d]).then_inc(s_ld, 16)
        ld += 16
    p.g.dma_start(id_sb[:], ident).then_inc(s_ld, 16)
    ld += 16
    if layer == 0:
        gc_sb = [sb(st, nc, f"gc{d}", [128, 1], F32) for d in range(NCH)]
        bc_sb = [sb(st, nc, f"bc{d}", [128, 1], F32) for d in range(NCH)]
        for d in range(NCH):
            p.g.dma_start(gc_sb[d][:], io["gcol"][d]).then_inc(s_ld, 16)
            p.g.dma_start(bc_sb[d][:], io["bcol"][d]).then_inc(s_ld, 16)
            ld += 32
    elif do_gate:
        bg_sb = sb(st, nc, "bgsb", [NE, 1], F32)
        p.g.dma_start(bg_sb[:], io["bgc"]).then_inc(s_ld, 16)
        ld += 16
    for e in (p.s, p.v, p.t):
        e.wait_ge(s_ld, ld)

    ln1_scope = ExitStack()
    s_n1, n1 = emit_ln(p, ln1_scope, st, "ln1", [x_sb[d][:] for d in range(NCH)],
                       dst_f32=xn_sb, dst_bf=xnbf_sb)
    if do_main and ("o_xn" in io or "o_xnbf" in io):
        p.y.wait_ge(s_n1, n1)
        for d in range(NCH):
            if "o_xn" in io:
                p.y.dma_start(io["o_xn"][d], xn_sb[d][:]).then_inc(s_fin, 16)
                fin += 16
            if "o_xnbf" in io:
                p.y.dma_start(io["o_xnbf"][d], xnbf_sb[d][:]).then_inc(s_fin, 16)
                fin += 16

    if layer == 0:
        for e in (p.s, p.v, p.t):
            e.wait_ge(s_n1, n1)
        ln1_scope.close()
        x_scope.close()
        xn2bf_sb = [sb(st, nc, f"xn2bf{d}", [128, NTOK], BF16) for d in range(NCH)]
        ln2_scope = ExitStack()
        s_n2, n2 = emit_ln(p, ln2_scope, st, "ln2", [xn_sb[d][:] for d in range(NCH)],
                           dst_bf=xn2bf_sb, gamma=[gc_sb[d][:] for d in range(NCH)],
                           beta=[bc_sb[d][:] for d in range(NCH)])
        qkv_src = xn2bf_sb
        s_rdy, nrdy = s_n2, n2
        end_scope = ln2_scope
    else:
        qkv_src = xnbf_sb
        s_rdy, nrdy = s_n1, n1
        end_scope = ln1_scope

    for e in (p.s, p.v, p.t):
        e.wait_ge(s_rdy, nrdy)
    end_scope.close()
    if layer != 0:
        x_scope.close()

    w_sb = [sb(st, nc, f"w{d}", [128, 3072], BF16) for d in range(NCH)]
    gw_sb = [sb(st, nc, f"gw{d}", [128, NE], F32) for d in range(NCH)]
    p.g.wait_ge(s_rdy, nrdy)
    for d in range(NCH):
        p.g.dma_start(w_sb[d][:], wqkvT[d]).then_inc(s_ld, 16)
        ld += 16
        if do_gate:
            p.g.dma_start(gw_sb[d][:], gwT[d]).then_inc(s_ld, 16)
            ld += 16
    LD_W = ld
    for e in (p.s, p.v, p.t):
        e.wait_ge(s_ld, LD_W)

    qkv_scope = ExitStack()
    ev_qk = StagedEvac(p, qkv_scope, st, "eqk")

    if do_main:
        def evac_q(oc, blk, bank, s_mm, mmcnt):
            return ev_qk(bank[:], s_mm, mmcnt, io["o_q"][oc % 8, :, cols(blk)],
                         scale=0.125)
        with ExitStack() as fm1:
            lf = emit_fm_matmul(p, fm1, st, "q", list(range(8)), NCH,
                           lambda d, oc: w_sb[d][:, oc*128:(oc+1)*128],
                           lambda d, blk: qkv_src[d][:, cols(blk)], evac_q)
            p.t.wait_ge(lf[0], lf[1])
    last_free = None
    if do_kv:
        def evac_k(oc, blk, bank, s_mm, mmcnt):
            return ev_qk(bank[:], s_mm, mmcnt, io["o_k"][oc % 8, :, cols(blk)])
        with ExitStack() as fm2:
            last_free = emit_fm_matmul(p, fm2, st, "k", list(range(8, 16)), NCH,
                           lambda d, oc: w_sb[d][:, oc*128:(oc+1)*128],
                           lambda d, blk: qkv_src[d][:, cols(blk)], evac_k)
            p.t.wait_ge(last_free[0], last_free[1])

        ones_v = sb(qkv_scope, nc, "onesv", [128, 1040], BF16)
        s_ov = sem(st, nc, "ov")
        p.v.memset(ones_v[:], 1.0).then_inc(s_ov, 1)
        p.y.wait_ge(s_ov, 1)
        for tt in range(NTILE):
            p.y.dma_start(io["o_v"][tt], ones_v[:]).then_inc(s_ov, 16)
        p.y.wait_ge(s_ov, 1 + 16 * NTILE)

        def evac_v(tt, vb, bank, s_mm, mmcnt):
            dst = io["o_v"][tt].rearrange("p (h c) -> p h c", c=65)[:, vb*8:(vb+1)*8, 0:64]
            return ev_qk(bank[:], s_mm, mmcnt, dst)
        with ExitStack() as fm3:
            last_free = emit_fm_matmul(p, fm3, st, "v", list(range(NTILE)), NCH,
                           lambda d, tt: qkv_src[d][:, tt*128:(tt+1)*128],
                           lambda d, vb: w_sb[d][:, 2048+vb*TB:2048+(vb+1)*TB],
                           evac_v, nblk=2)
            p.t.wait_ge(last_free[0], last_free[1])
    ev_qk.flush_wait()
    p.v.wait_ge(ev_qk.s_dma, 16 * ev_qk.g)
    p.s.wait_ge(ev_qk.s_dma, 16 * ev_qk.g)
    qkv_scope.close()

    if do_gate:
        gscope = ExitStack()
        ps_g = [ps(gscope, nc, f"gps{i}", [128, NE]) for i in range(2)]
        ps_t = [ps(gscope, nc, f"gpt{i}", [128, 128]) for i in range(2)]
        s_gmm = sem(st, nc, "gmm")
        s_gv = sem(st, nc, "gv")
        s_gt = sem(st, nc, "gt")
        gateT_sb = sb(gscope, nc, "gateT", [NE, NTOK], F32)
        coef = [sb(gscope, nc, f"coef{i}", [128, NE], F32) for i in range(2)]
        work = {nm: sb(gscope, nc, f"gw_{nm}", [128, NE], F32)
                for nm in ("e", "pp", "pm", "ge2", "pw")}
        colw = {nm: sb(gscope, nc, f"gc_{nm}", [128, 1], F32)
                for nm in ("m", "negm", "sum", "rsum", "m1", "m2", "den", "rden")}
        VPT = 3 if layer == 0 else 1
        for tt in range(NTILE):
            bank = ps_g[tt % 2]
            if tt >= 2:
                p.t.wait_ge(s_gv, VPT * (tt - 2) + VPT)
            for d in range(NCH):
                h = p.t.matmul(bank[:], xn_sb[d][:, tt*128:(tt+1)*128],
                               gw_sb[d][:], start=(d == 0), stop=(d == NCH - 1))
            h.then_inc(s_gmm, 1)
            cf = coef[tt % 2]
            p.v.wait_ge(s_gmm, 2 * tt + 1)
            if layer == 0:
                p.v.reduce_max(colw["m"][:], bank[:], axis=X)
                p.v.tensor_scalar(colw["negm"][:], colw["m"][:], -1.0, None, OP.mult)\
                    .then_inc(s_gv, 1)
                p.s.wait_ge(s_gv, VPT * tt + 1)
                p.s.activation(work["e"][:], bank[:], AF.Exp, bias=colw["negm"][:])\
                    .then_inc(s_gv, 1)
                p.v.wait_ge(s_gv, VPT * tt + 2)
                p.v.reduce_sum(colw["sum"][:], work["e"][:], axis=X)
                p.v.reciprocal(colw["rsum"][:], colw["sum"][:])
                p.v.tensor_scalar(work["pp"][:], work["e"][:], colw["rsum"][:], None, OP.mult)
                p.v.reduce_max(colw["m1"][:], work["pp"][:], axis=X)
                p.v.tensor_scalar(work["ge2"][:], work["pp"][:], colw["m1"][:], None, OP.is_ge)
                p.v.tensor_tensor(work["pm"][:], work["pp"][:], work["ge2"][:], OP.mult)
                p.v.tensor_tensor(work["pm"][:], work["pp"][:], work["pm"][:], OP.subtract)
                p.v.reduce_max(colw["m2"][:], work["pm"][:], axis=X)
                p.v.tensor_tensor(colw["den"][:], colw["m1"][:], colw["m2"][:], OP.add)
                p.v.reciprocal(colw["rden"][:], colw["den"][:])
                p.v.tensor_scalar(work["ge2"][:], work["pp"][:], colw["m2"][:], None, OP.is_ge)
                p.v.tensor_scalar(work["pw"][:], work["pp"][:], colw["rden"][:], None, OP.mult)
                p.v.tensor_tensor(cf[:], work["pw"][:], work["ge2"][:], OP.mult)\
                    .then_inc(s_gv, 1)
            else:
                p.v.tensor_copy(cf[:], bank[:]).then_inc(s_gv, 1)
            tbk = ps_t[tt % 2]
            p.t.wait_ge(s_gv, VPT * tt + VPT)
            if tt >= 2:
                p.t.wait_ge(s_gt, tt - 1)
            p.t.transpose(tbk[:NE, :], cf[:], id_sb[:]).then_inc(s_gmm, 1)
            p.v.wait_ge(s_gmm, 2 * tt + 2)
            if layer == 0:
                hh = p.v.tensor_copy(gateT_sb[:, tt*128:(tt+1)*128], tbk[:NE, :])
            else:
                hh = p.v.tensor_scalar(gateT_sb[:, tt*128:(tt+1)*128], tbk[:NE, :],
                                       bg_sb[:], None, OP.add)
            hh.then_inc(s_gt, 1)
        p.y.wait_ge(s_gt, NTILE)
        p.y.dma_start(io["o_gate"], gateT_sb[:]).then_inc(s_fin, 16)
        fin += 16
        p.y.wait_ge(s_fin, fin)
        gscope.close()
    else:
        if fin:
            p.y.wait_ge(s_fin, fin)


def emit_attn(p, st, nc, layer, io):
    alibi = layer == 1
    moffs, mcols = mask_offsets(alibi)
    ncols = sum(KT for (_, _, _, _, _, KT, _) in attn_schedule(alibi))
    qj_sb = [[sb(st, nc, f"q{i}_{d}", [128, TB], BF16) for d in range(NCH)]
             for i in range(2)]
    kH_sb = [[sb(st, nc, f"k{hf}_{d}", [128, NTOK], BF16) for d in range(NCH)]
             for hf in range(2)]
    v_sb = [sb(st, nc, f"v{t}", [128, 1040], BF16) for t in range(32)]
    import os as _os2
    WRES = _os2.environ.get("ATTN_WRES")
    w_sb = None
    wsl = None
    if WRES:
        w_sb = [sb(st, nc, f"w{d}", [128, D], BF16) for d in range(NCH)]
    else:
        wsl = [[sb(st, nc, f"wsl{i}_{d}", [128, 128], BF16) for d in range(NCH)]
               for i in range(2)]
    bo_sb = [sb(st, nc, f"bo{d}", [128, 1], F32) for d in range(NCH)]
    bc_sb = sb(st, nc, "bc", [128, ncols], F32) if alibi else None
    mk_sb = sb(st, nc, "mk", [128, mcols], BF16)
    ones_f = sb(st, nc, "onesf", [1, 64], F32)
    abuf = [sb(st, nc, f"ab{i}", [128, TB], BF16) for i in range(4)]
    rrow = sb(st, nc, "rrow", [1, TB], F32)
    rb_sb = sb(st, nc, "rb", [64, TB], F32)
    s_buf = [sb(st, nc, f"sb{d}", [128, TB], BF16) for d in range(NCH)]

    s_ld = sem(st, nc, "ld")
    s_sc = sem(st, nc, "sc")
    s_exp = sem(st, nc, "exp")
    s_msk = sem(st, nc, "msk")
    s_av = sem(st, nc, "av")
    s_fz = sem(st, nc, "fz")
    s_pj = sem(st, nc, "pj")
    s_vm = sem(st, nc, "vm")

    s_qld = sem(st, nc, "qld")
    qld = [0]
    def load_qJ(Jn):
        for d in range(NCH):
            p.y.dma_start(qj_sb[Jn % 2][d][:], io["q"][d, :, Jn*TB:(Jn+1)*TB])\
                .then_inc(s_qld, 16)
            qld[0] += 16
    import os as _os
    NOK = _os.environ.get("ATTN_NO_K"); NOV = _os.environ.get("ATTN_NO_V")
    ld = 0
    load_qJ(0)
    load_qJ(1)
    for d in range(NCH):
        for hf, srct in ((0, io["kA"]), (1, io["kB"])):
            p.y.dma_start(kH_sb[hf][d][:], srct[d]).then_inc(s_ld, 16)
            ld += 16
    p.v.memset(ones_f[:], 1.0).then_inc(s_vm, 1)
    for t in range(32):
        half = 0 if (t % 4) in (0, 3) else 1
        src = io["vA"] if half == 0 else io["vB"]
        lt = tmap(half).index(t)
        p.g.dma_start(v_sb[t][:], src[lt]).then_inc(s_ld, 16)
        ld += 16
    s_wld = sem(st, nc, "wld")
    wldc = [0]
    if WRES:
        for d in range(NCH):
            p.g.dma_start(w_sb[d][:], io["woutT"][d]).then_inc(s_ld, 16); ld += 16
    for d in range(NCH):
        p.g.dma_start(bo_sb[d][:], io["boutcol"][d]).then_inc(s_ld, 16); ld += 16
    if alibi:
        p.g.dma_start(bc_sb[:], io["biascols"]).then_inc(s_ld, 16); ld += 16
    p.g.dma_start(mk_sb[:], io["maskbank"]).then_inc(s_ld, 16); ld += 16
    for e in (p.t, p.s, p.v):
        e.wait_ge(s_ld, ld)
    import os as _os
    if _os.environ.get("ATTN_LOADS_ONLY"):
        p.y.wait_ge(s_ld, ld)
        p.y.wait_ge(s_qld, 256)
        return

    pscope = ExitStack()
    ps_sc = [ps(pscope, nc, f"psc{i}", [128, TB]) for i in range(3)]
    ps_o = ps(pscope, nc, "pso", [65, TB])
    ps_b = ps(pscope, nc, "psb", [64, TB])
    ps_pj = [ps(pscope, nc, f"pspj{i}", [128, TB]) for i in range(2)]
    ev = StagedEvac(p, pscope, st, "es", n_stages=2, dtype=F32)

    C = dict(sc=0, exp=0, msk=0, av=0, fz=0, pj=0, g=0)
    frees_pj = []
    colix = 0
    sched = list(attn_schedule(alibi))
    si = 0
    for J in range(4):
        p.t.wait_ge(s_qld, 128 * (J + 1))
        items = [it for it in sched if it[0] == J]
        for (J_, h, qb, j0, off, KT, NM) in items:
            hp, hh = h // 2, h % 2
            rows = slice(hh * 64, hh * 64 + 64)
            qcols = slice(off - J * 512, off - J * 512 + qb)
            for kt in range(KT):
                g = C["g"]
                bank = ps_sc[g % 3]
                if g >= 3:
                    p.t.wait_ge(s_exp, g - 2)
                khalf = 0 if (kt % 4) in (0, 3) else 1
                klt = tmap(khalf).index(kt)
                p.t.matmul(bank[:, :qb], kH_sb[khalf][hp][rows, klt*128:(klt+1)*128],
                           qj_sb[J % 2][hp][rows, qcols], start=True, stop=True)\
                    .then_inc(s_sc, 1)
                C["sc"] += 1
                ab = abuf[g % 4]
                p.s.wait_ge(s_sc, C["sc"])
                if g >= 4:
                    p.s.wait_ge(s_av, g - 3)
                bias = bc_sb[:, colix:colix+1] if alibi else 0.0
                p.s.activation(ab[:, :qb], bank[:, :qb], AF.Exp, bias=bias)\
                    .then_inc(s_exp, 1)
                C["exp"] += 1
                colix += 1
                masked = kt >= KT - NM
                if masked:
                    rel = kt - (KT - NM)
                    par = j0 % 2 if qb <= 128 else 0
                    h64 = (off % 128) // 64 if qb == 64 else 0
                    mo = moffs[(qb, rel, par, h64)]
                    p.v.wait_ge(s_exp, C["exp"])
                    p.v.tensor_tensor(ab[:, :qb], ab[:, :qb],
                                      mk_sb[:, mo:mo+qb], OP.mult)\
                        .then_inc(s_msk, 1)
                    C["msk"] += 1
                if masked:
                    p.t.wait_ge(s_msk, C["msk"])
                else:
                    p.t.wait_ge(s_exp, C["exp"])
                if kt == 0 and si > 0:
                    p.t.wait_ge(s_fz, 3 * si)
                p.t.matmul(ps_o[:, :qb], v_sb[kt][:, h*65:(h+1)*65],
                           ab[:, :qb], start=(kt == 0), stop=(kt == KT - 1))\
                    .then_inc(s_av, 1)
                C["av"] += 1
                C["g"] += 1
            p.v.wait_ge(s_av, C["av"])
            p.v.reciprocal(rrow[:, :qb], ps_o[64:65, :qb]).then_inc(s_fz, 1)
            C["fz"] += 1
            p.t.wait_ge(s_fz, C["fz"])
            if si > 0:
                p.t.wait_ge(s_fz, 3 * si)
            p.t.matmul(ps_b[:, :qb], ones_f[:], rrow[:, :qb], start=True,
                       stop=True).then_inc(s_sc, 1)
            C["sc"] += 1
            p.v.wait_ge(s_sc, C["sc"])
            p.v.tensor_copy(rb_sb[:, :qb], ps_b[:, :qb]).then_inc(s_fz, 1)
            C["fz"] += 1
            scols = slice(off - J * 512, off - J * 512 + qb)
            p.v.tensor_tensor(s_buf[hp][rows, scols], ps_o[0:64, :qb],
                              rb_sb[:, :qb], OP.mult).then_inc(s_fz, 1)
            C["fz"] += 1
            si += 1
        p.t.wait_ge(s_fz, 3 * si)
        for oc in range(NCH):
            bank = ps_pj[oc % 2]
            gp = J * NCH + oc
            if not WRES:
                if gp >= 2:
                    p.g.wait_ge(s_pj, gp - 1)
                for e in range(NCH):
                    p.g.dma_start(wsl[gp % 2][e][:],
                                  io["woutT"][e, :, oc*128:(oc+1)*128]).then_inc(s_wld, 16)
                    wldc[0] += 16
            if gp >= 2:
                fs, fv = frees_pj[gp - 2]
                p.t.wait_ge(fs, fv)
            if not WRES:
                p.t.wait_ge(s_wld, wldc[0])
            for e in range(NCH):
                lhsw = w_sb[e][:, oc*128:(oc+1)*128] if WRES else wsl[gp % 2][e][:]
                hmm = p.t.matmul(bank[:], lhsw,
                                 s_buf[e][:], start=(e == 0), stop=(e == NCH - 1))
            hmm.then_inc(s_pj, 1)
            C["pj"] += 1
            frees_pj.append(ev(bank[:], s_pj, C["pj"], io["o_s"][oc, :, J*512:(J+1)*512],
                               bias=bo_sb[oc][:]))
        p.v.wait_ge(s_pj, C["pj"])
        if J + 2 < 4:
            p.y.wait_ge(s_pj, C["pj"])
            load_qJ(J + 2)
    ev.flush_wait()
    pscope.close()


def emit_moe0(p, st, nc, io):
    NE = 14
    xnbf_sb = [sb(st, nc, f"xnbf{d}", [128, NTOK], BF16) for d in range(NCH)]
    gacc = [sb(st, nc, f"ga{d}", [128, NTOK], F32) for d in range(NCH)]
    cf_sb = sb(st, nc, "cfsb", [NE, NTOK], F32)
    b1_sb = sb(st, nc, "b1sb", [128, NE * 16], F32)
    b2_sb = sb(st, nc, "b2sb", [NE, D], F32)
    ones_f = sb(st, nc, "onesf", [1, 128], F32)
    sel_sb = sb(st, nc, "selsb", [14, 128 * 14], F32)
    cb_sb = sb(st, nc, "cb", [128, TB], F32)
    tmp_sb = sb(st, nc, "tmp", [128, TB], F32)
    s_ld = sem(st, nc, "ld")
    s_w = sem(st, nc, "w")
    s_mm = sem(st, nc, "mm")
    s_ev = sem(st, nc, "ev")
    s_acc = sem(st, nc, "acc")
    s_cb = sem(st, nc, "cbk")
    p.v.memset(ones_f[:], 1.0)
    ld = 0
    for d in range(NCH):
        p.y.dma_start(xnbf_sb[d][:], io["xnbf"][d]).then_inc(s_ld, 16); ld += 16
    p.y.dma_start(cf_sb[:], io["coeffT"]).then_inc(s_ld, 16); ld += 16
    p.y.dma_start(b1_sb[:], io["b1c"]).then_inc(s_ld, 16); ld += 16
    p.y.dma_start(b2_sb[:], io["b2r"]).then_inc(s_ld, 16); ld += 16
    p.y.dma_start(sel_sb[:], io["sel"]).then_inc(s_ld, 16); ld += 16
    for e in (p.t, p.s, p.v):
        e.wait_ge(s_ld, ld)

    iscope = ExitStack()
    xnb = [sb(iscope, nc, f"xnb{i}", [128, NTOK], F32, side="right") for i in range(2)]
    with ExitStack() as sti:
        ps_i = [ps(sti, nc, f"pi{i}", [128, TB]) for i in range(4)]
        g = 0
        for d in range(NCH):
            if d >= 2:
                p.y.wait_ge(s_acc, (d - 1) * NBLK)
            p.y.dma_start(xnb[d % 2][:], io["xn"][d]).then_inc(s_ld, 16)
            ld += 16
            p.v.wait_ge(s_ld, ld)
            for blk in range(NBLK):
                bank = ps_i[g % 4]
                if g >= 4:
                    p.t.wait_ge(s_acc, g - 3)
                p.t.matmul(bank[:], b2_sb[:, d*128:(d+1)*128], cf_sb[:, cols(blk)],
                           start=True, stop=True).then_inc(s_mm, 1)
                p.v.wait_ge(s_mm, g + 1)
                p.v.tensor_tensor(gacc[d][:, cols(blk)], xnb[d % 2][:, cols(blk)],
                                  bank[:], OP.add).then_inc(s_acc, 1)
                g += 1
        NB_INIT = g
    iscope.close()

    wscope = ExitStack()
    w1_sb = [sb(wscope, nc, f"w1_{d}", [128, 2048], BF16) for d in range(NCH)]
    w2_sb = [sb(wscope, nc, f"w2_{f}", [128, D], BF16) for f in range(16)]
    h_sb = [sb(wscope, nc, f"h{f}", [128, TB], BF16) for f in range(16)]
    ps_h = [ps(wscope, nc, f"ph{i}", [128, TB]) for i in range(2)]
    ps_o = [ps(wscope, nc, f"po{i}", [128, TB]) for i in range(2)]
    ps_c = ps(wscope, nc, "pc", [128, TB])
    wld = [0]; ghc = [0]; omc = [0]; acc2 = [NB_INIT]; cbc = [0]; mmc = [NB_INIT]

    for e in range(NE):
        p.g.wait_ge(s_acc, NB_INIT + e * 32)
        for d in range(NCH):
            p.g.dma_start(w1_sb[d][:], io["w1"][e, d]).then_inc(s_w, 16); wld[0] += 16
        for f in range(16):
            p.g.dma_start(w2_sb[f][:], io["w2"][e, f]).then_inc(s_w, 16); wld[0] += 16
        p.t.wait_ge(s_w, wld[0])
        for tb in range(NBLK):
            if cbc[0] > 0:
                p.t.wait_ge(s_cb, 2 * cbc[0])
            p.t.matmul(ps_c[:], sel_sb[:, e*128:(e+1)*128], cf_sb[:, cols(tb)],
                       start=True, stop=True).then_inc(s_mm, 1)
            mmc[0] += 1
            p.v.wait_ge(s_mm, mmc[0])
            p.v.tensor_copy(cb_sb[:], ps_c[:]).then_inc(s_cb, 1)
            for fc in range(16):
                bank = ps_h[fc % 2]
                gh = ghc[0]
                if gh >= 2:
                    p.t.wait_ge(s_ev, gh - 1)
                for d in range(NCH):
                    h = p.t.matmul(bank[:], w1_sb[d][:, fc*128:(fc+1)*128],
                                   xnbf_sb[d][:, cols(tb)],
                                   start=(d == 0), stop=(d == NCH - 1))
                h.then_inc(s_mm, 1)
                mmc[0] += 1
                p.s.wait_ge(s_mm, mmc[0])
                p.s.activation(h_sb[fc][:], bank[:], AF.Gelu,
                               bias=b1_sb[:, e*16+fc:e*16+fc+1]).then_inc(s_ev, 1)
                ghc[0] += 1
            for dd in range(NCH):
                bank = ps_o[dd % 2]
                om = omc[0]
                if dd == 0:
                    p.t.wait_ge(s_ev, ghc[0])
                if om >= 2:
                    p.t.wait_ge(s_acc, acc2[0] - 1)
                for f in range(16):
                    h = p.t.matmul(bank[:], w2_sb[f][:, dd*128:(dd+1)*128],
                                   h_sb[f][:], start=(f == 0), stop=(f == 15))
                h.then_inc(s_mm, 1)
                mmc[0] += 1
                p.v.wait_ge(s_mm, mmc[0])
                p.v.tensor_tensor(tmp_sb[:], bank[:], cb_sb[:], OP.mult)
                p.v.tensor_tensor(gacc[dd][:, cols(tb)], gacc[dd][:, cols(tb)],
                                  tmp_sb[:], OP.add).then_inc(s_acc, 1)
                acc2[0] += 1
                omc[0] += 1
            p.v.nop().then_inc(s_cb, 1)
            cbc[0] += 1
    wscope.close()

    fscope = ExitStack()
    xb = [sb(fscope, nc, f"xb{i}", [128, NTOK], F32, side="right") for i in range(2)]
    sb_ = [sb(fscope, nc, f"sbb{i}", [128, NTOK], F32, side="right") for i in range(2)]
    gb = sb(fscope, nc, "gb", [128, NTOK], F32, side="right")
    x1b = [sb(fscope, nc, f"x1b{i}", [128, NTOK], F32, side="right") for i in range(2)]
    echo = sb(fscope, nc, "echo", [128, 1], F32, side="right")
    s_clip = sem(st, nc, "clip")
    s_silu = sem(st, nc, "silu")
    s_x1 = sem(st, nc, "x1")
    s_od = sem(st, nc, "od")
    p.y.wait_ge(s_acc, acc2[0])
    for d in range(NCH):
        if d >= 2:
            p.y.wait_ge(s_x1, d - 1)
        p.y.dma_start(xb[d % 2][:], io["xT"][d]).then_inc(s_ld, 16)
        p.y.dma_start(sb_[d % 2][:], io["s0"][d]).then_inc(s_ld, 16)
        ld += 32
        p.v.wait_ge(s_ld, ld)
        p.v.tensor_scalar(gacc[d][:], gacc[d][:], 100.0, -100.0, OP.min, OP.max)\
            .then_inc(s_clip, 1)
        p.s.wait_ge(s_clip, d + 1)
        if d >= 1:
            p.s.wait_ge(s_x1, d)
        p.s.activation(gb[:], gacc[d][:], AF.Silu).then_inc(s_silu, 1)
        p.v.wait_ge(s_silu, d + 1)
        if d >= 2:
            p.v.wait_ge(s_od, 16 * (d - 1))
        p.v.tensor_tensor(gb[:], gb[:], sb_[d % 2][:], OP.mult)
        p.v.tensor_tensor(x1b[d % 2][:], gb[:], xb[d % 2][:], OP.add)
        p.v.tensor_copy(echo[:], x1b[d % 2][:, NTOK-1:NTOK]).then_inc(s_x1, 1)
        p.y.wait_ge(s_x1, d + 1)
        p.y.dma_start(io["o_x1"][d], x1b[d % 2][:]).then_inc(s_od, 16)
    p.y.wait_ge(s_od, 16 * NCH)
    fscope.close()


def emit_exchange(p, st, nc, io):
    xscope = ExitStack()
    x1bf = [sb(xscope, nc, f"x1bf{d}", [128, NTOK], BF16) for d in range(NCH)]
    stg = [sb(xscope, nc, f"stg{i}", [128, NTOK], BF16) for i in range(2)]
    acc = sb(xscope, nc, "xacc", [128, NTOK], F32)
    msl = sb(xscope, nc, "msl", [128, 8], F32)
    mpr = sb(xscope, nc, "mpr", [128, 8], F32)
    s_ld = sem(st, nc, "ld")
    s_st = sem(st, nc, "stg")
    s_cc = sem(st, nc, "cc")
    s_rd = sem(st, nc, "rd")
    ld = 0
    p.g.dma_start(msl[:], io["mslot"]).then_inc(s_ld, 16); ld += 16
    p.g.dma_start(mpr[:], io["mpair"]).then_inc(s_ld, 16); ld += 16
    for d in range(NCH):
        p.g.dma_start(x1bf[d][:], io["x1"][d]).then_inc(s_ld, 16); ld += 16
    p.v.wait_ge(s_ld, ld)
    nst = 0
    for s in range(8):
        for d in range(NCH):
            p.v.wait_ge(s_st, 0 if nst < 2 else 16 * (nst - 1))
            p.v.tensor_scalar(stg[nst % 2][:], x1bf[d][:], msl[:, s:s+1], None,
                              OP.mult).then_inc(s_cc, 1)
            nst += 1
            p.y.wait_ge(s_cc, nst)
            p.y.dma_start(io["arin"][s, d], stg[(nst - 1) % 2][:]).then_inc(s_st, 16)
    p.g.wait_ge(s_st, 16 * nst)
    p.g.collective_compute("AllReduce", mybir.AluOpType.add,
                           replica_groups=[list(range(8))],
                           ins=[io["arin_t"].ap().opt()],
                           outs=[io["arout_t"].ap().opt()]).then_inc(s_cc, 16)
    p.g.wait_ge(s_cc, nst + 16)
    s_cs = sem(st, nc, "cs")
    nrd = 0
    for d in range(NCH):
        for s in range(8):
            if nrd >= 2:
                p.g.wait_ge(s_cs, nrd - 1)
            p.g.dma_start(stg[s % 2][:], io["arout"][s, d]).then_inc(s_ld, 16)
            ld += 16
            p.v.wait_ge(s_ld, ld)
            if s == 0:
                h = p.v.tensor_scalar(acc[:], stg[s % 2][:], mpr[:, s:s+1], None,
                                      OP.mult)
            else:
                h = p.v.scalar_tensor_tensor(acc[:], stg[s % 2][:], mpr[:, s:s+1],
                                             acc[:], OP.mult, OP.add)
            h.then_inc(s_cs, 1)
            if s == 7:
                p.v.tensor_copy(x1bf[d][:], acc[:]).then_inc(s_rd, 1)
            nrd += 1
        p.y.wait_ge(s_rd, d + 1)
        p.y.dma_start(io["o_x1pair"][d], x1bf[d][:]).then_inc(s_st, 16)
        nst += 1
    p.y.wait_ge(s_st, 16 * nst)
    xscope.close()


def emit_combine(p, st, nc, io):
    cscope = ExitStack()
    a = sb(cscope, nc, "ca", [128, NTOK], BF16)
    b = sb(cscope, nc, "cb2", [128, NTOK], BF16)
    o = sb(cscope, nc, "co", [128, NTOK], BF16)
    t1 = sb(cscope, nc, "ct", [128, NTOK], BF16)
    mh = sb(cscope, nc, "cmh", [128, 2], F32)
    s_ld = sem(st, nc, "ld")
    s_wr = sem(st, nc, "wr")
    s_cs = sem(st, nc, "cs")
    p.g.dma_start(mh[:], io["mh"]).then_inc(s_ld, 16)
    ld = [16]
    nio = [0]

    def one(dst, own_ap, pair_ap, mown_col, width):
        if nio[0] >= 1:
            p.g.wait_ge(s_cs, nio[0])
        p.g.dma_start(a[:, :width], own_ap).then_inc(s_ld, 16)
        p.g.dma_start(b[:, :width], pair_ap).then_inc(s_ld, 16)
        ld[0] += 32
        p.v.wait_ge(s_ld, ld[0])
        if nio[0] >= 1:
            p.v.wait_ge(s_wr, 16 * nio[0])
        p.v.tensor_scalar(t1[:, :width], b[:, :width], mh[:, 1-mown_col:2-mown_col],
                          None, OP.mult)
        p.v.scalar_tensor_tensor(o[:, :width], a[:, :width], mh[:, mown_col:mown_col+1],
                                 t1[:, :width], OP.mult, OP.add).then_inc(s_cs, 1)
        nio[0] += 1
        p.v.wait_ge(s_cs, nio[0])
        p.y.wait_ge(s_cs, nio[0])
        p.y.dma_start(dst, o[:, :width]).then_inc(s_wr, 16)

    for d in range(NCH):
        one(io["o_kA"][d], io["k_own"][d], io["k_pair"][d], 0, NTOK)
        one(io["o_kB"][d], io["k_own"][d], io["k_pair"][d], 1, NTOK)
    for t in range(NTILE):
        one(io["o_vA"][t], io["v_own"][t], io["v_pair"][t], 0, D)
        one(io["o_vB"][t], io["v_own"][t], io["v_pair"][t], 1, D)
    p.y.wait_ge(s_wr, 16 * nio[0])


def emit_ffn1(p, st, nc, io, BIS=40):
    NE = 16
    S = 4096
    xnbf_sb = [sb(st, nc, f"xnbf{d}", [128, NTOK], BF16) for d in range(NCH)]
    gacc = [sb(st, nc, f"ga{d}", [128, NTOK], F32) for d in range(NCH)]
    coefP = sb(st, nc, "coefP", [128, S], F32)
    sel_sb = sb(st, nc, "selsb", [NE, 128 * NE], F32)
    b3c_sb = sb(st, nc, "b3csb", [NE, 1], F32)
    ones_f = sb(st, nc, "onesf", [1, 128], F32)
    cb_sb = sb(st, nc, "cb", [128, TB], F32)
    tmp_sb = sb(st, nc, "tmp", [128, TB], F32)
    B1c = sb(st, nc, "B1c", [128, NE], F32)
    B2c = sb(st, nc, "B2c", [128, NE], F32)
    rowb = sb(st, nc, "rowb", [1, TB], F32)
    colw = {nm: sb(st, nc, f"c_{nm}", [128, 1], F32)
            for nm in ("mx", "negmx", "sum", "rsum", "lo", "hi", "u", "t",
                       "cnt", "sel1", "nsel", "dlt")}
    s_ld = sem(st, nc, "ld")
    s_w = sem(st, nc, "w")
    s_mm = sem(st, nc, "mm")
    s_ev = sem(st, nc, "ev")
    s_acc = sem(st, nc, "acc")
    s_cb = sem(st, nc, "cbk")
    s_gate = sem(st, nc, "gate")
    p.v.memset(ones_f[:], 1.0)
    ld = 0
    for d in range(NCH):
        p.y.dma_start(xnbf_sb[d][:], io["xnbf"][d]).then_inc(s_ld, 16); ld += 16
    p.y.dma_start(sel_sb[:], io["sel"]).then_inc(s_ld, 16); ld += 16
    p.y.dma_start(b3c_sb[:], io["b3c"]).then_inc(s_ld, 16); ld += 16

    gscope = ExitStack()
    lv_sb = sb(gscope, nc, "lv", [128, S], F32, side="right")
    pv_sb = sb(gscope, nc, "pv", [128, S], F32, side="right")
    ge_sb = sb(gscope, nc, "ge", [128, S], F32, side="right")
    b1r_sb = sb(gscope, nc, "b1rsb", [1, NE], F32, side="right")
    b2r_sb = sb(gscope, nc, "b2rsb", [1, NE], F32, side="right")
    lgA = sb(gscope, nc, "lgA", [NE, NTOK], F32, side="right")
    lgB = sb(gscope, nc, "lgB", [NE, NTOK], F32, side="right")
    mh2 = sb(gscope, nc, "mh2", [128, 2], F32, side="right")
    lgt = sb(gscope, nc, "lgt", [NE, NTOK], F32, side="right")
    p.v.memset(lv_sb[:], 0.0).then_inc(s_gate, 100)
    p.y.dma_start(lgA[:], io["lgA"]).then_inc(s_ld, 16); ld += 16
    p.y.dma_start(lgB[:], io["lgB"]).then_inc(s_ld, 16); ld += 16
    p.y.dma_start(mh2[:], io["mh"]).then_inc(s_ld, 16); ld += 16
    p.y.dma_start(b1r_sb[:], io["b1r"]).then_inc(s_ld, 16); ld += 16
    p.y.dma_start(b2r_sb[:], io["b2r"]).then_inc(s_ld, 16); ld += 16
    for e in (p.t, p.s, p.v):
        e.wait_ge(s_ld, ld)
    p.v.wait_ge(s_gate, 100)
    p.v.tensor_scalar(lgt[:], lgB[:], mh2[0:NE, 1:2], None, OP.mult)
    p.v.scalar_tensor_tensor(lv_sb[0:NE, 0:NTOK], lgA[:], mh2[0:NE, 0:1],
                             lgt[:], OP.mult, OP.add)
    p.v.tensor_scalar(lgt[:], lgA[:], mh2[0:NE, 1:2], None, OP.mult)
    p.v.scalar_tensor_tensor(lv_sb[0:NE, NTOK:2*NTOK], lgB[:], mh2[0:NE, 0:1],
                             lgt[:], OP.mult, OP.add)
    with ExitStack() as stb:
        ps_bc = [ps(stb, nc, f"pbc{i}", [128, NE]) for i in range(2)]
        p.t.matmul(ps_bc[0][:], ones_f[:], b1r_sb[:], start=True, stop=True)
        p.t.matmul(ps_bc[1][:], ones_f[:], b2r_sb[:], start=True, stop=True)\
            .then_inc(s_mm, 1)
        p.v.wait_ge(s_mm, 1)
        p.v.tensor_copy(B1c[:], ps_bc[0][:])
        p.v.tensor_copy(B2c[:], ps_bc[1][:]).then_inc(s_gate, 1)
    p.v.reduce_max(colw["mx"][:], lv_sb[:], axis=X)
    p.v.tensor_scalar(colw["negmx"][:], colw["mx"][:], -1.0, None, OP.mult)\
        .then_inc(s_gate, 1)
    p.s.wait_ge(s_gate, 102)
    p.s.activation(pv_sb[:], lv_sb[:], AF.Exp, bias=colw["negmx"][:])\
        .then_inc(s_gate, 1)
    p.v.wait_ge(s_gate, 103)
    p.v.reduce_sum(colw["sum"][:], pv_sb[:], axis=X)
    p.v.reciprocal(colw["rsum"][:], colw["sum"][:])
    p.v.tensor_scalar(pv_sb[:], pv_sb[:], colw["rsum"][:], None, OP.mult)
    p.v.memset(colw["lo"][:], 0.0)
    p.v.memset(colw["hi"][:], 1.0)
    for it in range(BIS):
        p.v.tensor_tensor(colw["u"][:], colw["lo"][:], colw["hi"][:], OP.add)
        p.v.tensor_scalar(colw["t"][:], colw["u"][:], 0.5, None, OP.mult)
        p.v.tensor_scalar(ge_sb[:], pv_sb[:], colw["t"][:], None, OP.is_ge)
        p.v.reduce_sum(colw["cnt"][:], ge_sb[:], axis=X)
        p.v.tensor_scalar(colw["sel1"][:], colw["cnt"][:], 256.0, None, OP.is_ge)
        p.v.tensor_scalar(colw["nsel"][:], colw["sel1"][:], -1.0, 1.0, OP.mult, OP.add)
        p.v.tensor_tensor(colw["dlt"][:], colw["t"][:], colw["lo"][:], OP.subtract)
        p.v.tensor_tensor(colw["dlt"][:], colw["dlt"][:], colw["sel1"][:], OP.mult)
        p.v.tensor_tensor(colw["lo"][:], colw["lo"][:], colw["dlt"][:], OP.add)
        p.v.tensor_tensor(colw["dlt"][:], colw["t"][:], colw["hi"][:], OP.subtract)
        p.v.tensor_tensor(colw["dlt"][:], colw["dlt"][:], colw["nsel"][:], OP.mult)
        p.v.tensor_tensor(colw["hi"][:], colw["hi"][:], colw["dlt"][:], OP.add)
    p.v.tensor_scalar(ge_sb[:], pv_sb[:], colw["lo"][:], None, OP.is_ge)
    p.v.tensor_tensor(coefP[:], pv_sb[:], ge_sb[:], OP.mult).then_inc(s_gate, 1)

    with ExitStack() as sti:
        ps_r = ps(sti, nc, "pr", [1, TB])
        ps_b = [ps(sti, nc, f"pb{i}", [128, TB]) for i in range(2)]
        p.t.wait_ge(s_gate, 104)
        acc0 = 0
        for tb in range(NBLK):
            if tb >= 1:
                p.t.wait_ge(s_gate, 104 + tb)
            p.t.matmul(ps_r[:], b3c_sb[:], coefP[0:NE, cols(tb)], start=True,
                       stop=True).then_inc(s_mm, 1)
            p.v.wait_ge(s_mm, 2 + 3 * tb)
            p.v.tensor_copy(rowb[:], ps_r[:]).then_inc(s_gate, 1)
            bank = ps_b[tb % 2]
            p.t.wait_ge(s_gate, 105 + tb)
            if tb >= 2:
                p.t.wait_ge(s_acc, (tb - 1) * NCH)
            p.t.matmul(bank[:], ones_f[:], rowb[:], start=True, stop=True)\
                .then_inc(s_mm, 2)
            p.v.wait_ge(s_mm, 4 + 3 * tb)
            for d in range(NCH):
                p.v.tensor_copy(gacc[d][:, cols(tb)], bank[:]).then_inc(s_acc, 1)
                acc0 += 1
        NB_INIT = acc0
        MM_INIT = 1 + 3 * NBLK
    gscope.close()

    wscope = ExitStack()
    w1_sb = [sb(wscope, nc, f"w1_{d}", [128, D], BF16) for d in range(NCH)]
    w2_sb = [sb(wscope, nc, f"w2_{d}", [128, D], BF16) for d in range(NCH)]
    w3_sb = [sb(wscope, nc, f"w3_{d}", [128, D], BF16) for d in range(NCH)]
    h1 = [sb(wscope, nc, f"h1_{d}", [128, TB], BF16) for d in range(NCH)]
    h2 = [sb(wscope, nc, f"h2_{d}", [128, TB], BF16) for d in range(NCH)]
    ps_h = [ps(wscope, nc, f"ph{i}", [128, TB]) for i in range(2)]
    ps_o = [ps(wscope, nc, f"po{i}", [128, TB]) for i in range(2)]
    ps_c = ps(wscope, nc, "pc", [128, TB])
    wld = [0]; evc = [0]; omc = [0]; accc = [NB_INIT]; cbc = [0]; mmc = [MM_INIT]

    for e in range(NE):
        p.g.wait_ge(s_acc, NB_INIT + e * 32)
        for d in range(NCH):
            p.g.dma_start(w1_sb[d][:], io["w1"][e, d]).then_inc(s_w, 16); wld[0] += 16
            p.g.dma_start(w2_sb[d][:], io["w2"][e, d]).then_inc(s_w, 16); wld[0] += 16
            p.g.dma_start(w3_sb[d][:], io["w3"][e, d]).then_inc(s_w, 16); wld[0] += 16
        p.t.wait_ge(s_w, wld[0])
        for tb in range(NBLK):
            if cbc[0] > 0:
                p.t.wait_ge(s_cb, 2 * cbc[0])
            p.t.matmul(ps_c[:], sel_sb[:, e*128:(e+1)*128], coefP[0:NE, cols(tb)],
                       start=True, stop=True).then_inc(s_mm, 1)
            mmc[0] += 1
            p.v.wait_ge(s_mm, mmc[0])
            p.v.tensor_copy(cb_sb[:], ps_c[:]).then_inc(s_cb, 1)
            for stage, wbuf, srcb, dst, bcol in (
                    (1, w1_sb, None, h1, B1c), (2, w2_sb, h1, h2, B2c)):
                for oc in range(NCH):
                    bank = ps_h[oc % 2]
                    gh = evc[0]
                    if stage == 2 and oc == 0:
                        p.t.wait_ge(s_ev, evc[0])
                    elif gh >= 2:
                        p.t.wait_ge(s_ev, gh - 1)
                    for d in range(NCH):
                        rhs = xnbf_sb[d][:, cols(tb)] if stage == 1 else srcb[d][:]
                        h = p.t.matmul(bank[:], wbuf[d][:, oc*128:(oc+1)*128],
                                       rhs, start=(d == 0), stop=(d == NCH - 1))
                    h.then_inc(s_mm, 1)
                    mmc[0] += 1
                    p.s.wait_ge(s_mm, mmc[0])
                    p.s.activation(dst[oc][:], bank[:], AF.Relu,
                                   bias=bcol[:, e:e+1]).then_inc(s_ev, 1)
                    evc[0] += 1
            for dd in range(NCH):
                bank = ps_o[dd % 2]
                om = omc[0]
                if dd == 0:
                    p.t.wait_ge(s_ev, evc[0])
                if om >= 2:
                    p.t.wait_ge(s_acc, accc[0] - 1)
                for f in range(NCH):
                    h = p.t.matmul(bank[:], w3_sb[f][:, dd*128:(dd+1)*128],
                                   h2[f][:], start=(f == 0), stop=(f == NCH - 1))
                h.then_inc(s_mm, 1)
                mmc[0] += 1
                p.v.wait_ge(s_mm, mmc[0])
                p.v.tensor_tensor(tmp_sb[:], bank[:], cb_sb[:], OP.mult)
                p.v.tensor_tensor(gacc[dd][:, cols(tb)], gacc[dd][:, cols(tb)],
                                  tmp_sb[:], OP.add).then_inc(s_acc, 1)
                accc[0] += 1
                omc[0] += 1
            p.v.nop().then_inc(s_cb, 1)
            cbc[0] += 1
    wscope.close()

    fscope = ExitStack()
    xb = [sb(fscope, nc, f"xb{i}", [128, NTOK], F32, side="right") for i in range(2)]
    sb_ = [sb(fscope, nc, f"sbb{i}", [128, NTOK], F32, side="right") for i in range(2)]
    gb = sb(fscope, nc, "gb", [128, NTOK], F32, side="right")
    x1b = [sb(fscope, nc, f"x1b{i}", [128, NTOK], F32, side="right") for i in range(2)]
    echo = sb(fscope, nc, "echo", [128, 1], F32, side="right")
    s_silu = sem(st, nc, "silu")
    s_x1 = sem(st, nc, "x1")
    s_od = sem(st, nc, "od")
    p.y.wait_ge(s_acc, accc[0])
    for d in range(NCH):
        if d >= 2:
            p.y.wait_ge(s_x1, d - 1)
        p.y.dma_start(xb[d % 2][:], io["x1T"][d]).then_inc(s_ld, 16)
        p.y.dma_start(sb_[d % 2][:], io["s1"][d]).then_inc(s_ld, 16)
        ld += 32
        p.s.wait_ge(s_ld, ld)
        if d >= 1:
            p.s.wait_ge(s_x1, d)
        p.s.activation(gb[:], gacc[d][:], AF.Silu).then_inc(s_silu, 1)
        p.v.wait_ge(s_silu, d + 1)
        p.v.wait_ge(s_ld, ld)
        if d >= 2:
            p.v.wait_ge(s_od, 16 * (d - 1))
        p.v.tensor_tensor(gb[:], gb[:], sb_[d % 2][:], OP.mult)
        p.v.tensor_tensor(x1b[d % 2][:], gb[:], xb[d % 2][:], OP.add)
        p.v.tensor_copy(echo[:], x1b[d % 2][:, NTOK-1:NTOK]).then_inc(s_x1, 1)
        p.y.wait_ge(s_x1, d + 1)
        p.y.dma_start(io["o_x2"][d], x1b[d % 2][:]).then_inc(s_od, 16)
    p.y.wait_ge(s_od, 16 * NCH)
    fscope.close()


class PhaseStack:
    def __init__(self, launch_stack):
        self.mem = ExitStack()
        self.launch = launch_stack
    def enter_context(self, cm):
        if type(cm).__name__ == "_GeneratorContextManager":
            return self.launch.enter_context(cm)
        return self.mem.enter_context(cm)
    def close(self, p):
        self.mem.close()


def build_l0():
    nc = bass.Bass("TRN2", target_bir_lowering=False)
    E = lambda n, s, dt: nc.dram_tensor(n, s, dt, kind="ExternalInput").ap()
    I = lambda n, s, dt: nc.dram_tensor(n, s, dt)
    NE0 = 14
    _, mc0 = mask_offsets(False)
    ins = dict(
        x_own=E("x_own", [NCH, 128, NTOK], F32),
        x_absA=E("x_absA", [NCH, 128, NTOK], F32),
        x_absB=E("x_absB", [NCH, 128, NTOK], F32),
        ident=E("ident", [128, 128], F32),
        wqkvT0=E("wqkvT0", [NCH, 128, 3072], F32),
        gwT0=E("gwT0", [NCH, 128, NE0], F32),
        gcol=E("gcol", [NCH, 128, 1], F32),
        bcol=E("bcol", [NCH, 128, 1], F32),
        wout0=E("wout0", [NCH, 128, D], F32),
        bout0=E("bout0", [NCH, 128, 1], F32),
        mask0=E("mask0", [128, mc0], BF16),
        w1m=E("w1m", [NE0, NCH, 128, 2048], F32),
        b1c=E("b1c", [128, NE0 * 16], F32),
        w2m=E("w2m", [NE0, 16, 128, D], F32),
        b2r=E("b2r", [NE0, D], F32),
        sel14=E("sel14", [14, 128 * 14], F32),
    )
    o_x1 = nc.dram_tensor("o_x1", [NCH, 128, NTOK], F32, kind="ExternalOutput").ap()
    xn0 = I("xn0", [NCH, 128, NTOK], F32)
    xnbf0 = I("xnbf0", [NCH, 128, NTOK], BF16)
    q0 = I("q0", [NCH, 128, NTOK], BF16)
    gate0 = I("gate0", [NE0, NTOK], F32)
    kA0 = I("kA0", [NCH, 128, NTOK], BF16)
    kB0 = I("kB0", [NCH, 128, NTOK], BF16)
    vA0 = I("vA0", [NTILE, 128, 1040], BF16)
    vB0 = I("vB0", [NTILE, 128, 1040], BF16)
    s0 = I("s0", [NCH, 128, NTOK], F32)

    p = Plan(nc)
    gstack = ExitStack()
    s_bar = gstack.enter_context(nc.semaphore("sbar"))
    barn = [0]

    def phase(fn):
        stp = PhaseStack(gstack)
        fn(stp)
        barn[0] += 1
        p.y.nop().then_inc(s_bar, 1)
        for e in (p.t, p.s, p.v, p.g, p.y):
            e.wait_ge(s_bar, barn[0])
        stp.close(p)

    A = lambda t: t.ap()
    phase(lambda stp: emit_prep(p, stp, nc, 0, dict(
        xT=ins["x_own"], wqkvT=ins["wqkvT0"], gwT=ins["gwT0"], ident=ins["ident"],
        gcol=ins["gcol"], bcol=ins["bcol"], o_xn=A(xn0), o_xnbf=A(xnbf0),
        o_q=A(q0), o_gate=A(gate0)), do_main=True, do_kv=False))
    phase(lambda stp: emit_prep(p, stp, nc, 0, dict(
        xT=ins["x_absA"], wqkvT=ins["wqkvT0"], gwT=ins["gwT0"], ident=ins["ident"],
        gcol=ins["gcol"], bcol=ins["bcol"], o_k=A(kA0), o_v=A(vA0)),
        do_main=False, do_kv=True))
    phase(lambda stp: emit_prep(p, stp, nc, 0, dict(
        xT=ins["x_absB"], wqkvT=ins["wqkvT0"], gwT=ins["gwT0"], ident=ins["ident"],
        gcol=ins["gcol"], bcol=ins["bcol"], o_k=A(kB0), o_v=A(vB0)),
        do_main=False, do_kv=True))
    phase(lambda stp: emit_attn(p, stp, nc, 0, dict(
        q=A(q0), kA=A(kA0), kB=A(kB0), vA=A(vA0), vB=A(vB0),
        woutT=ins["wout0"], boutcol=ins["bout0"], maskbank=ins["mask0"],
        o_s=A(s0))))
    phase(lambda stp: emit_moe0(p, stp, nc, dict(
        xnbf=A(xnbf0), xn=A(xn0), coeffT=A(gate0), w1=ins["w1m"], b1c=ins["b1c"],
        w2=ins["w2m"], b2r=ins["b2r"], sel=ins["sel14"], xT=ins["x_own"],
        s0=A(s0), o_x1=o_x1)))
    p.build()
    gstack.close()
    return nc


def build_l1():
    nc = bass.Bass("TRN2", target_bir_lowering=False)
    E = lambda n, s, dt: nc.dram_tensor(n, s, dt, kind="ExternalInput").ap()
    I = lambda n, s, dt: nc.dram_tensor(n, s, dt)
    NE1 = 16
    _, mc1 = mask_offsets(True)
    nb1 = sum(KT for (_, _, _, _, _, KT, _) in attn_schedule(True))
    ins = dict(
        x_own=E("x_own", [NCH, 128, NTOK], F32),
        x_absA=E("x_absA", [NCH, 128, NTOK], F32),
        x_absB=E("x_absB", [NCH, 128, NTOK], F32),
        ident=E("ident", [128, 128], F32),
        wqkvT1=E("wqkvT1", [NCH, 128, 3072], F32),
        gwT1=E("gwT1", [NCH, 128, NE1], F32),
        bgc=E("bgc", [NE1, 1], F32),
        wout1=E("wout1", [NCH, 128, D], F32),
        bout1=E("bout1", [NCH, 128, 1], F32),
        mask1=E("mask1", [128, mc1], BF16),
        bias1=E("bias1", [128, nb1], F32),
        mh=E("mh", [128, 2], F32),
        w1f=E("w1f", [NE1, NCH, 128, D], F32),
        w2f=E("w2f", [NE1, NCH, 128, D], F32),
        w3f=E("w3f", [NE1, NCH, 128, D], F32),
        b1r1=E("b1r1", [1, NE1], F32),
        b2r1=E("b2r1", [1, NE1], F32),
        b3c1=E("b3c1", [NE1, 1], F32),
        sel16=E("sel16", [NE1, 128 * NE1], F32),
    )
    o_x2 = nc.dram_tensor("o_x2", [NCH, 128, NTOK], F32, kind="ExternalOutput").ap()
    xnbf1 = I("xnbf1", [NCH, 128, NTOK], BF16)
    q1 = I("q1", [NCH, 128, NTOK], BF16)
    lgA = I("lgA", [NE1, NTOK], F32)
    lgB = I("lgB", [NE1, NTOK], F32)
    kA1 = I("kA1", [NCH, 128, NTOK], BF16)
    kB1 = I("kB1", [NCH, 128, NTOK], BF16)
    vA1 = I("vA1", [NTILE, 128, 1040], BF16)
    vB1 = I("vB1", [NTILE, 128, 1040], BF16)
    s1 = I("s1", [NCH, 128, NTOK], F32)

    p = Plan(nc)
    gstack = ExitStack()
    s_bar = gstack.enter_context(nc.semaphore("sbar"))
    barn = [0]

    def phase(fn):
        stp = PhaseStack(gstack)
        fn(stp)
        barn[0] += 1
        p.y.nop().then_inc(s_bar, 1)
        for e in (p.t, p.s, p.v, p.g, p.y):
            e.wait_ge(s_bar, barn[0])
        stp.close(p)

    A = lambda t: t.ap()
    phase(lambda stp: emit_prep(p, stp, nc, 1, dict(
        xT=ins["x_own"], wqkvT=ins["wqkvT1"], gwT=ins["gwT1"], ident=ins["ident"],
        bgc=ins["bgc"], o_xnbf=A(xnbf1), o_q=A(q1), o_gate=None),
        do_main=True, do_kv=False, do_gate=False))
    phase(lambda stp: emit_prep(p, stp, nc, 1, dict(
        xT=ins["x_absA"], wqkvT=ins["wqkvT1"], gwT=ins["gwT1"], ident=ins["ident"],
        bgc=ins["bgc"], o_k=A(kA1), o_v=A(vA1), o_gate=A(lgA)),
        do_main=False, do_kv=True, do_gate=True))
    phase(lambda stp: emit_prep(p, stp, nc, 1, dict(
        xT=ins["x_absB"], wqkvT=ins["wqkvT1"], gwT=ins["gwT1"], ident=ins["ident"],
        bgc=ins["bgc"], o_k=A(kB1), o_v=A(vB1), o_gate=A(lgB)),
        do_main=False, do_kv=True, do_gate=True))
    phase(lambda stp: emit_attn(p, stp, nc, 1, dict(
        q=A(q1), kA=A(kA1), kB=A(kB1), vA=A(vA1), vB=A(vB1),
        woutT=ins["wout1"], boutcol=ins["bout1"], maskbank=ins["mask1"],
        biascols=ins["bias1"], o_s=A(s1))))
    phase(lambda stp: emit_ffn1(p, stp, nc, dict(
        xnbf=A(xnbf1), lgA=A(lgA), lgB=A(lgB), mh=ins["mh"],
        w1=ins["w1f"], w2=ins["w2f"], w3=ins["w3f"],
        b1r=ins["b1r1"], b2r=ins["b2r1"], b3c=ins["b3c1"],
        sel=ins["sel16"], x1T=ins["x_own"], s1=A(s1), o_x2=o_x2)))
    p.build()
    gstack.close()
    return nc


from concourse.bass_utils import run_bass_kernel_spmd
import ml_dtypes


def _alibi_biascols(half):
    ncols = sum(KT for (_, _, _, _, _, KT, _) in attn_schedule(True))
    biascols = np.zeros((ncols, 128), np.float32)
    i = 0
    for (J, h, qb, j0, off, KT, NM) in attn_schedule(True):
        lts = range(j0, j0 + max(qb // 128, 1))
        if qb == 64:
            a_ = 2 * j0 + ((j0 % 2) if half == 0 else (1 - j0 % 2))
            h64 = (off % 128) // 64
            maxpos = a_ * 128 + h64 * 64 + 63
        else:
            maxpos = max(2 * lt + (((lt % 2) if half == 0 else (1 - lt % 2)))
                         for lt in lts) * 128 + 127
        q_end = maxpos + 1
        own_max_tile = maxpos // 128
        sl = ALIBI_SLOPES[h]
        for kt in range(KT):
            if kt > own_max_tile:
                biascols[i] = -60000.0
            else:
                biascols[i] = sl * ((kt * 128 + np.arange(128)) - q_end)
            i += 1
    return np.ascontiguousarray(biascols.T)


def kernel(**d):
    cores = list(range(8))
    f32 = np.float32
    ac = np.ascontiguousarray
    x0 = np.asarray(d["L0_attn_x"], dtype=f32)
    eye128 = np.eye(128, dtype=f32)
    idxh = [own_token_idx(0), own_token_idx(1)]

    sh0 = dict(
        ident=eye128,
        wqkvT0=ac(d["L0_attn_w_qkv"].T.astype(f32)).reshape(NCH, 128, 3072),
        gwT0=ac(d["L0_ffn_gate_w"].T.astype(f32)).reshape(NCH, 128, 14),
        gcol=ac(d["L0_attn_ln_gamma"].astype(f32)).reshape(NCH, 128, 1),
        bcol=ac(d["L0_attn_ln_beta"].astype(f32)).reshape(NCH, 128, 1),
        wout0=ac(d["L0_attn_w_out"].T.astype(f32)).reshape(NCH, 128, D),
        bout0=ac(d["L0_attn_b_out"].astype(f32)).reshape(NCH, 128, 1),
        w1m=ac(d["L0_ffn_w1"].astype(f32)).reshape(14, 8, 128, 2048),
        b1c=ac(d["L0_ffn_b1"].astype(f32).reshape(14 * 16, 128).T),
        w2m=ac(d["L0_ffn_w2"].astype(f32)).reshape(14, 16, 128, 1024),
        b2r=ac(d["L0_ffn_b2"].astype(f32)),
        sel14=ac(np.repeat(np.eye(14, dtype=f32), 128, axis=1)),
    )
    maskb = [None, None]
    for half in (0, 1):
        mb0, _ = mask_patterns(half, False)
        maskb[half] = ac(mb0).astype(ml_dtypes.bfloat16)
    in_maps = []
    for c in cores:
        b, half = c // 2, c % 2
        m = dict(sh0)
        m["x_own"] = ac(x0[b][idxh[half]].T).reshape(NCH, 128, NTOK)
        m["x_absA"] = ac(x0[b][idxh[0]].T).reshape(NCH, 128, NTOK)
        m["x_absB"] = ac(x0[b][idxh[1]].T).reshape(NCH, 128, NTOK)
        m["mask0"] = maskb[half]
        in_maps.append(m)
    nc0 = build_l0()
    r0 = run_bass_kernel_spmd(nc0, in_maps, core_ids=cores).results

    x1 = np.empty((4, 4096, 1024), f32)
    for c in cores:
        b, half = c // 2, c % 2
        x1[b][idxh[half]] = r0[c]["o_x1"].reshape(1024, NTOK).T

    sh1 = dict(
        ident=eye128,
        wqkvT1=ac(d["L1_attn_W_qkv"].astype(f32)).reshape(NCH, 128, 3072),
        gwT1=ac(d["L1_ffn_Wg"].astype(f32)).reshape(NCH, 128, 16),
        bgc=ac(d["L1_ffn_bg"].astype(f32)).reshape(16, 1),
        wout1=ac(d["L1_attn_W_out"].astype(f32)).reshape(NCH, 128, D),
        bout1=np.zeros((NCH, 128, 1), f32),
        w1f=ac(d["L1_ffn_w1"].astype(f32)).reshape(16, 8, 128, 1024),
        w2f=ac(d["L1_ffn_w2"].astype(f32)).reshape(16, 8, 128, 1024),
        w3f=ac(d["L1_ffn_w3"].astype(f32)).reshape(16, 8, 128, 1024),
        b1r1=ac(d["L1_ffn_b1"].astype(f32).reshape(1, 16)),
        b2r1=ac(d["L1_ffn_b2"].astype(f32).reshape(1, 16)),
        b3c1=ac(d["L1_ffn_b3"].astype(f32).reshape(16, 1)),
        sel16=ac(np.repeat(np.eye(16, dtype=f32), 128, axis=1)),
    )
    maskb1 = [None, None]
    bias1 = [None, None]
    for half in (0, 1):
        mb1, _ = mask_patterns(half, True)
        maskb1[half] = ac(mb1).astype(ml_dtypes.bfloat16)
        bias1[half] = _alibi_biascols(half)
    in_maps = []
    for c in cores:
        b, half = c // 2, c % 2
        m = dict(sh1)
        m["x_own"] = ac(x1[b][idxh[half]].T).reshape(NCH, 128, NTOK)
        m["x_absA"] = ac(x1[b][idxh[0]].T).reshape(NCH, 128, NTOK)
        m["x_absB"] = ac(x1[b][idxh[1]].T).reshape(NCH, 128, NTOK)
        m["mask1"] = maskb1[half]
        m["bias1"] = bias1[half]
        mhh = np.zeros((128, 2), f32); mhh[:, half] = 1.0
        m["mh"] = mhh
        in_maps.append(m)
    nc1 = build_l1()
    r1 = run_bass_kernel_spmd(nc1, in_maps, core_ids=cores).results

    out = np.empty((4, 4096, 1024), f32)
    for c in cores:
        b, half = c // 2, c % 2
        out[b][idxh[half]] = r1[c]["o_x2"].reshape(1024, NTOK).T
    return out


# version 05
# speedup vs baseline: 1.0456x; 1.0456x over previous
from contextlib import ExitStack
import numpy as np
import concourse.bass as bass
import concourse.mybir as mybir
from concourse.alu_op_type import AluOpType as OP

F32 = mybir.dt.float32
BF16 = mybir.dt.bfloat16
AF = mybir.ActivationFunctionType

D = 1024
NTOK = 2048
NCH = 8
NTILE = 16
TB = 512
NBLK = NTOK // TB

def tmap(half):
    return [t for t in range(32) if ((t % 4) in (0, 3)) == (half == 0)]

def own_token_idx(half):
    return np.concatenate([np.arange(t * 128, (t + 1) * 128) for t in tmap(half)])

ALIBI_SLOPES = np.asarray([2.0 ** (-(i + 1) / 2.0) for i in range(16)], np.float32)


class _Handle:
    def __init__(self):
        self.incs = []
    def then_inc(self, sem, n=1):
        self.incs.append((sem, n))
        return self


class _EngProxy:
    def __init__(self, plan, eng):
        self._plan = plan
        self._eng = eng
    def __getattr__(self, name):
        def rec(*args, **kwargs):
            h = _Handle()
            self._plan.ops[self._eng].append((name, args, kwargs, h))
            return h
        return rec


class Plan:
    ENGINES = ("sync", "gpsimd", "tensor", "vector", "scalar")
    def __init__(self, nc):
        self.nc = nc
        self.ops = {e: [] for e in self.ENGINES}
        self._vsem = nc.semaphore("vchain").__enter__()
        self._vscr = nc.sbuf_tensor("vscratch", [1, 2], mybir.dt.float32).__enter__()
        self.t = _EngProxy(self, "tensor")
        self.v = _EngProxy(self, "vector")
        self.s = _EngProxy(self, "scalar")
        self.g = _EngProxy(self, "gpsimd")
        self.y = _EngProxy(self, "sync")

    def build(self):
        nc = self.nc
        vsem = self._vsem
        vscr = self._vscr
        with nc.Block() as block:
            for eng in self.ENGINES:
                lst = self.ops[eng]
                if not lst:
                    continue
                def make(lst, eng=eng):
                    def body(e):
                        cnt = 0
                        for name, args, kwargs, h in lst:
                            if eng != "vector" or name == "wait_ge":
                                r = getattr(e, name)(*args, **kwargs)
                                for sm, n in h.incs:
                                    r = r.then_inc(sm, n)
                                continue
                            if cnt > 0:
                                e.wait_ge(vsem, cnt)
                            r = getattr(e, name)(*args, **kwargs)
                            r.then_inc(vsem, 1)
                            cnt += 1
                            for sm, n in h.incs:
                                e.wait_ge(vsem, cnt)
                                e.tensor_copy(vscr[:, 0:1], vscr[:, 1:2])\
                                    .then_inc(sm, n)
                                cnt += 0
                    return body
                getattr(block, eng)(make(lst))


_UID = [0]
def _u(name):
    _UID[0] += 1
    return f"{name}_u{_UID[0]}"

def sb(st, nc, name, shape, dtype, side=None):
    kw = {"side": side} if side else {}
    return st.enter_context(nc.sbuf_tensor(_u(name), shape, dtype, **kw))

def ps(st, nc, name, shape, dtype=F32):
    return st.enter_context(nc.psum_tensor(_u(name), shape, dtype))

def sem(st, nc, name):
    return st.enter_context(nc.semaphore(_u(name)))

def cols(blk, tb=TB):
    return slice(blk * tb, (blk + 1) * tb)


def emit_ln(p, stm, sts, name, src, dst_f32=None, dst_bf=None, gamma=None, beta=None):
    nc = p.nc
    sq = [sb(stm, nc, f"{name}_sq{i}", [128, NTOK], BF16, side="right") for i in range(2)]
    xbf = [sb(stm, nc, f"{name}_xbf{i}", [128, NTOK], BF16, side="right") for i in range(2)]
    srow = [sb(stm, nc, f"{name}_sr{i}", [1, TB], F32, side="right") for i in range(8)]
    R = [sb(stm, nc, f"{name}_R{i}", [1, TB], F32, side="right") for i in range(5)]
    tmp = [sb(stm, nc, f"{name}_tmp{i}", [128, TB], F32, side="right") for i in range(2)]
    ones_bf = sb(stm, nc, f"{name}_1bf", [128, 1], BF16, side="right")
    ones_f = sb(stm, nc, f"{name}_1f", [1, 128], F32, side="right")
    s_pre = sem(sts, nc, f"{name}_pre")
    s_mm = sem(sts, nc, f"{name}_mm")
    s_row = sem(sts, nc, f"{name}_row")
    s_nrm = sem(sts, nc, f"{name}_nrm")

    p.v.memset(ones_bf[:], 1.0)
    p.v.memset(ones_f[:], 1.0)
    pst_scope = ExitStack()
    ps_st = [ps(pst_scope, nc, f"{name}_ps{i}", [1, TB]) for i in range(8)]
    for d in range(NCH):
        if d >= 2:
            p.s.wait_ge(s_mm, d - 1)
            p.v.wait_ge(s_mm, d - 1)
        p.s.activation(sq[d % 2][:], src[d], AF.Square).then_inc(s_pre, 1)
        p.v.tensor_copy(xbf[d % 2][:], src[d]).then_inc(s_pre, 1)
        p.t.wait_ge(s_pre, 2 * d + 2)
        for blk in range(NBLK):
            p.t.matmul(ps_st[blk][:], ones_bf[:], xbf[d % 2][:, cols(blk)],
                       start=(d == 0), stop=(d == NCH - 1))
        for blk in range(NBLK):
            h = p.t.matmul(ps_st[4 + blk][:], ones_bf[:], sq[d % 2][:, cols(blk)],
                           start=(d == 0), stop=(d == NCH - 1))
        h.then_inc(s_mm, 1)

    p.v.wait_ge(s_mm, NCH)
    for blk in range(NBLK):
        p.v.tensor_copy(srow[blk][:], ps_st[blk][:])
        h = p.v.tensor_copy(srow[4 + blk][:], ps_st[4 + blk][:])
    h.then_inc(s_row, 1)
    pst_scope.close()
    ps_b = [ps(stm, nc, f"{name}_pb{i}", [128, TB]) for i in range(4)]

    for blk in range(NBLK):
        base = 1 + 5 * blk
        p.v.tensor_scalar(R[0][:], srow[blk][:], 1.0 / D, None, OP.mult)
        p.v.tensor_scalar(R[1][:], srow[4 + blk][:], 1.0 / D, None, OP.mult)\
            .then_inc(s_row, 1)
        p.s.wait_ge(s_row, base + 1)
        p.s.activation(R[2][:], R[0][:], AF.Square).then_inc(s_row, 1)
        p.v.wait_ge(s_row, base + 2)
        p.v.scalar_tensor_tensor(R[1][:], R[1][:], 1e-5, R[2][:], OP.add, OP.subtract)\
            .then_inc(s_row, 1)
        p.s.wait_ge(s_row, base + 3)
        p.s.activation(R[2][:], R[1][:], AF.Sqrt).then_inc(s_row, 1)
        p.v.wait_ge(s_row, base + 4)
        p.v.reciprocal(R[3][:], R[2][:])
        p.v.tensor_tensor(R[4][:], R[0][:], R[3][:], OP.mult)\
            .then_inc(s_row, 1)
        br, bm = ps_b[2 * (blk % 2)], ps_b[2 * (blk % 2) + 1]
        p.t.wait_ge(s_row, base + 5)
        if blk >= 2:
            p.t.wait_ge(s_nrm, (blk - 1) * NCH)
        p.t.matmul(br[:], ones_f[:], R[3][:], start=True, stop=True)
        p.t.matmul(bm[:], ones_f[:], R[4][:], start=True, stop=True)\
            .then_inc(s_mm, 1)
        p.v.wait_ge(s_mm, NCH + 1 + blk)
        for d in range(NCH):
            t0, t1 = tmp[0], tmp[1]
            p.v.tensor_tensor(t0[:], src[d][:, cols(blk)], br[:], OP.mult)
            hxn = p.v.tensor_tensor(t1[:], t0[:], bm[:], OP.subtract)
            if gamma is not None:
                if dst_f32 is not None:
                    p.v.tensor_scalar(dst_f32[d][:, cols(blk)], t1[:],
                                      gamma[d], beta[d], OP.mult, OP.add)
                if dst_bf is not None:
                    hxn = p.v.tensor_scalar(dst_bf[d][:, cols(blk)], t1[:],
                                            gamma[d], beta[d], OP.mult, OP.add)
            else:
                if dst_f32 is not None:
                    hxn = p.v.tensor_copy(dst_f32[d][:, cols(blk)], t1[:])
                if dst_bf is not None:
                    hxn = p.v.tensor_copy(dst_bf[d][:, cols(blk)], t1[:])
            hxn.then_inc(s_nrm, 1)
    return s_nrm, NBLK * NCH


class StagedEvac:
    def __init__(self, p, stm, sts, name, n_stages=4, tb=TB, dtype=BF16):
        self.p = p
        self.stages = [sb(stm, p.nc, f"{name}_stg{i}", [128, tb], dtype)
                       for i in range(n_stages)]
        self.n_stages = n_stages
        self.s_st = sem(sts, p.nc, f"{name}_sst")
        self.s_dma = sem(sts, p.nc, f"{name}_sdma")
        self.g = 0

    def __call__(self, bank_ap, s_mm, mmcnt, dst_ap, scale=1.0, bias=0.0):
        p = self.p
        g = self.g
        stage = self.stages[g % self.n_stages]
        p.s.wait_ge(s_mm, mmcnt)
        if g >= self.n_stages:
            p.s.wait_ge(self.s_dma, 16 * (g - self.n_stages + 1))
        func = AF.Copy if isinstance(bias, float) else AF.Identity
        h = p.s.activation(stage[:], bank_ap, func, scale=scale, bias=bias)
        h.then_inc(self.s_st, 1)
        p.y.wait_ge(self.s_st, g + 1)
        p.y.dma_start(dst_ap, stage[:]).then_inc(self.s_dma, 16)
        self.g += 1
        return (self.s_st, self.g)

    def flush_wait(self):
        self.p.y.wait_ge(self.s_dma, 16 * self.g)


def emit_fm_matmul(p, stm, sts, name, oc_list, nch, get_lhsT, get_rhs, evac,
                   nblk=NBLK, tb=TB):
    nc = p.nc
    pss = [ps(stm, nc, f"{name}_ps{i}", [128, tb]) for i in range(4)]
    s_mm = sem(sts, nc, f"{name}_mm")
    frees = []
    g = 0
    for oc in oc_list:
        for blk in range(nblk):
            bank = pss[g % 4]
            if g >= 4:
                fs, fv = frees[g - 4]
                p.t.wait_ge(fs, fv)
            for d in range(nch):
                h = p.t.matmul(bank[:], get_lhsT(d, oc), get_rhs(d, blk),
                               start=(d == 0), stop=(d == nch - 1))
            h.then_inc(s_mm, 1)
            frees.append(evac(oc, blk, bank, s_mm, g + 1))
            g += 1
    return frees[-1]


def build_prep(layer):
    nc = bass.Bass("TRN2", target_bir_lowering=False)
    NE = 14 if layer == 0 else 16
    xT = nc.dram_tensor("xT", [NCH, 128, NTOK], F32, kind="ExternalInput").ap()
    wqkvT = nc.dram_tensor("wqkvT", [NCH, 128, 3072], F32, kind="ExternalInput").ap()
    gwT = nc.dram_tensor("gwT", [NCH, 128, NE], F32, kind="ExternalInput").ap()
    ident = nc.dram_tensor("ident", [128, 128], F32, kind="ExternalInput").ap()
    if layer == 0:
        gcol = nc.dram_tensor("gcol", [NCH, 128, 1], F32, kind="ExternalInput").ap()
        bcol = nc.dram_tensor("bcol", [NCH, 128, 1], F32, kind="ExternalInput").ap()
    else:
        bgc = nc.dram_tensor("bgc", [NE, 1], F32, kind="ExternalInput").ap()
    o_xn = nc.dram_tensor("o_xn", [NCH, 128, NTOK], F32, kind="ExternalOutput").ap()
    o_xnbf = nc.dram_tensor("o_xnbf", [NCH, 128, NTOK], BF16, kind="ExternalOutput").ap()
    o_q = nc.dram_tensor("o_q", [NCH, 128, NTOK], BF16, kind="ExternalOutput").ap()
    o_k = nc.dram_tensor("o_k", [NCH, 128, NTOK], BF16, kind="ExternalOutput").ap()
    o_v = nc.dram_tensor("o_v", [NTILE, 128, D], BF16, kind="ExternalOutput").ap()
    o_gate = nc.dram_tensor("o_gate", [NE, NTOK], F32, kind="ExternalOutput").ap()

    p = Plan(nc)
    st = ExitStack()
    x_scope = ExitStack()
    x_sb = [sb(x_scope, nc, f"x{d}", [128, NTOK], F32, side="right") for d in range(NCH)]
    xn_sb = [sb(st, nc, f"xn{d}", [128, NTOK], F32) for d in range(NCH)]
    xnbf_sb = [sb(st, nc, f"xnbf{d}", [128, NTOK], BF16) for d in range(NCH)]
    id_sb = sb(st, nc, "idf", [128, 128], F32)
    s_ld = sem(st, nc, "ld")
    s_fin = sem(st, nc, "fin")
    fin = 0
    ld = 0

    for d in range(NCH):
        p.y.dma_start(x_sb[d][:], xT[d]).then_inc(s_ld, 16)
        ld += 16
    p.g.dma_start(id_sb[:], ident).then_inc(s_ld, 16)
    ld += 16
    if layer == 0:
        gc_sb = [sb(st, nc, f"gc{d}", [128, 1], F32) for d in range(NCH)]
        bc_sb = [sb(st, nc, f"bc{d}", [128, 1], F32) for d in range(NCH)]
        for d in range(NCH):
            p.g.dma_start(gc_sb[d][:], gcol[d]).then_inc(s_ld, 16)
            p.g.dma_start(bc_sb[d][:], bcol[d]).then_inc(s_ld, 16)
            ld += 32
    else:
        bg_sb = sb(st, nc, "bgsb", [NE, 1], F32)
        p.g.dma_start(bg_sb[:], bgc).then_inc(s_ld, 16)
        ld += 16
    LD_ALL = ld

    for e in (p.s, p.v, p.t):
        e.wait_ge(s_ld, LD_ALL)

    ln1_scope = ExitStack()
    s_n1, n1 = emit_ln(p, ln1_scope, st, "ln1", [x_sb[d][:] for d in range(NCH)],
                       dst_f32=xn_sb, dst_bf=xnbf_sb)
    p.y.wait_ge(s_n1, n1)
    for d in range(NCH):
        p.y.dma_start(o_xn[d], xn_sb[d][:]).then_inc(s_fin, 16)
        p.y.dma_start(o_xnbf[d], xnbf_sb[d][:]).then_inc(s_fin, 16)
        fin += 32

    if layer == 0:
        for e in (p.s, p.v, p.t):
            e.wait_ge(s_n1, n1)
        ln1_scope.close()
        x_scope.close()
        xn2bf_sb = [sb(st, nc, f"xn2bf{d}", [128, NTOK], BF16) for d in range(NCH)]
        ln2_scope = ExitStack()
        s_n2, n2 = emit_ln(p, ln2_scope, st, "ln2", [xn_sb[d][:] for d in range(NCH)],
                           dst_bf=xn2bf_sb, gamma=[gc_sb[d][:] for d in range(NCH)],
                           beta=[bc_sb[d][:] for d in range(NCH)])
        qkv_src = xn2bf_sb
        s_rdy, nrdy = s_n2, n2
        end_scope = ln2_scope
    else:
        qkv_src = xnbf_sb
        s_rdy, nrdy = s_n1, n1
        end_scope = ln1_scope

    for e in (p.s, p.v, p.t):
        e.wait_ge(s_rdy, nrdy)
    end_scope.close()
    if layer != 0:
        x_scope.close()

    w_sb = [sb(st, nc, f"w{d}", [128, 3072], BF16) for d in range(NCH)]
    gw_sb = [sb(st, nc, f"gw{d}", [128, NE], F32) for d in range(NCH)]
    p.g.wait_ge(s_rdy, nrdy)
    for d in range(NCH):
        p.g.dma_start(w_sb[d][:], wqkvT[d]).then_inc(s_ld, 16)
        p.g.dma_start(gw_sb[d][:], gwT[d]).then_inc(s_ld, 16)
        ld += 32
    LD_W = ld
    for e in (p.s, p.v, p.t):
        e.wait_ge(s_ld, LD_W)

    qkv_scope = ExitStack()
    ev_qk = StagedEvac(p, qkv_scope, st, "eqk")

    def evac_qk(oc, blk, bank, s_mm, mmcnt):
        dst = o_q if oc < 8 else o_k
        return ev_qk(bank[:], s_mm, mmcnt, dst[oc % 8, :, cols(blk)],
                     scale=(0.125 if oc < 8 else 1.0))

    emit_fm_matmul(p, qkv_scope, st, "qk", list(range(16)), NCH,
                   lambda d, oc: w_sb[d][:, oc*128:(oc+1)*128],
                   lambda d, blk: qkv_src[d][:, cols(blk)], evac_qk)

    def evac_v(tt, vb, bank, s_mm, mmcnt):
        return ev_qk(bank[:], s_mm, mmcnt, o_v[tt, :, vb*TB:(vb+1)*TB])

    s_ev_v, n_ev_v = emit_fm_matmul(p, qkv_scope, st, "v", list(range(NTILE)), NCH,
                   lambda d, tt: qkv_src[d][:, tt*128:(tt+1)*128],
                   lambda d, vb: w_sb[d][:, 2048+vb*TB:2048+(vb+1)*TB],
                   evac_v, nblk=2)
    ev_qk.flush_wait()
    p.t.wait_ge(s_ev_v, n_ev_v)
    p.v.wait_ge(ev_qk.s_dma, 16 * ev_qk.g)
    p.s.wait_ge(ev_qk.s_dma, 16 * ev_qk.g)
    qkv_scope.close()

    gscope = ExitStack()
    ps_g = [ps(gscope, nc, f"gps{i}", [128, NE]) for i in range(2)]
    ps_t = [ps(gscope, nc, f"gpt{i}", [128, 128]) for i in range(2)]
    s_gmm = sem(st, nc, "gmm")
    s_gv = sem(st, nc, "gv")
    s_gt = sem(st, nc, "gt")
    gateT_sb = sb(gscope, nc, "gateT", [NE, NTOK], F32)
    zeroNE = sb(gscope, nc, "zeroNE", [128, NE], F32)
    p.v.memset(zeroNE[:], 0.0)
    coef = [sb(gscope, nc, f"coef{i}", [128, NE], F32) for i in range(2)]
    work = {nm: sb(gscope, nc, f"gw_{nm}", [128, NE], F32)
            for nm in ("e", "pp", "pm", "ge2", "pw")}
    colw = {nm: sb(gscope, nc, f"gc_{nm}", [128, 1], F32)
            for nm in ("m", "negm", "sum", "rsum", "m1", "m2", "den", "rden")}
    VPT = 3 if layer == 0 else 1
    for tt in range(NTILE):
        bank = ps_g[tt % 2]
        if tt >= 2:
            p.t.wait_ge(s_gv, VPT * (tt - 2) + VPT)
        for d in range(NCH):
            h = p.t.matmul(bank[:], xn_sb[d][:, tt*128:(tt+1)*128],
                           gw_sb[d][:], start=(d == 0), stop=(d == NCH - 1))
        h.then_inc(s_gmm, 1)
        cf = coef[tt % 2]
        p.v.wait_ge(s_gmm, 2 * tt + 1)
        if layer == 0:
            p.v.reduce_max(colw["m"][:], bank[:], axis=mybir.AxisListType.X)
            p.v.tensor_scalar(colw["negm"][:], colw["m"][:], -1.0, None, OP.mult)\
                .then_inc(s_gv, 1)
            p.s.wait_ge(s_gv, VPT * tt + 1)
            p.s.activation(work["e"][:], bank[:], AF.Exp, bias=colw["negm"][:])\
                .then_inc(s_gv, 1)
            p.v.wait_ge(s_gv, VPT * tt + 2)
            p.v.reduce_sum(colw["sum"][:], work["e"][:], axis=mybir.AxisListType.X)
            p.v.reciprocal(colw["rsum"][:], colw["sum"][:])
            p.v.tensor_scalar(work["pp"][:], work["e"][:], colw["rsum"][:], None, OP.mult)
            p.v.reduce_max(colw["m1"][:], work["pp"][:], axis=mybir.AxisListType.X)
            p.v.tensor_scalar(work["ge2"][:], work["pp"][:], colw["m1"][:], None, OP.is_ge)
            p.v.tensor_tensor(work["pm"][:], work["pp"][:], work["ge2"][:], OP.mult)
            p.v.tensor_tensor(work["pm"][:], work["pp"][:], work["pm"][:], OP.subtract)
            p.v.reduce_max(colw["m2"][:], work["pm"][:], axis=mybir.AxisListType.X)
            p.v.tensor_tensor(colw["den"][:], colw["m1"][:], colw["m2"][:], OP.add)
            p.v.reciprocal(colw["rden"][:], colw["den"][:])
            p.v.tensor_scalar(work["ge2"][:], work["pp"][:], colw["m2"][:], None, OP.is_ge)
            p.v.tensor_scalar(work["pw"][:], work["pp"][:], colw["rden"][:], None, OP.mult)
            p.v.tensor_tensor(cf[:], work["pw"][:], work["ge2"][:], OP.mult)\
                .then_inc(s_gv, 1)
        else:
            p.v.tensor_copy(cf[:], bank[:]).then_inc(s_gv, 1)
        tbk = ps_t[tt % 2]
        p.t.wait_ge(s_gv, VPT * tt + VPT)
        if tt >= 2:
            p.t.wait_ge(s_gt, tt - 1)
        p.t.transpose(tbk[:NE, :], cf[:], id_sb[:]).then_inc(s_gmm, 1)
        p.v.wait_ge(s_gmm, 2 * tt + 2)
        if layer == 0:
            hh = p.v.tensor_copy(gateT_sb[:, tt*128:(tt+1)*128], tbk[:NE, :])
        else:
            hh = p.v.tensor_scalar(gateT_sb[:, tt*128:(tt+1)*128], tbk[:NE, :],
                                   bg_sb[:], None, OP.add)
        hh.then_inc(s_gt, 1)
    p.y.wait_ge(s_gt, NTILE)
    p.y.dma_start(o_gate, gateT_sb[:]).then_inc(s_fin, 16)
    fin += 16

    p.y.wait_ge(s_fin, fin)
    p.build()
    gscope.close()
    qkv_scope.close()
    st.close()
    return nc


def host_prep_inputs(layer, d, core):
    half = core % 2
    b = core // 2
    idx = own_token_idx(half)
    if layer == 0:
        x = d["L0_attn_x"][b][idx]
        wq = d["L0_attn_w_qkv"]
        wqkvT = np.ascontiguousarray(wq.T).reshape(NCH, 128, 3072)
        gwT = np.ascontiguousarray(d["L0_ffn_gate_w"].T).reshape(NCH, 128, 14)
        m = {
            "xT": np.ascontiguousarray(x.T).reshape(NCH, 128, NTOK),
            "wqkvT": wqkvT,
            "gwT": gwT,
            "ident": np.eye(128, dtype=np.float32),
            "gcol": d["L0_attn_ln_gamma"].reshape(NCH, 128, 1).astype(np.float32),
            "bcol": d["L0_attn_ln_beta"].reshape(NCH, 128, 1).astype(np.float32),
        }
    else:
        x = d["x1"][b][idx]
        wqkvT = np.ascontiguousarray(d["L1_attn_W_qkv"]).reshape(NCH, 128, 3072)
        gwT = np.ascontiguousarray(d["L1_ffn_Wg"]).reshape(NCH, 128, 16)
        m = {
            "xT": np.ascontiguousarray(x.T).reshape(NCH, 128, NTOK),
            "wqkvT": wqkvT,
            "gwT": gwT,
            "ident": np.eye(128, dtype=np.float32),
            "bgc": d["L1_ffn_bg"].reshape(16, 1).astype(np.float32),
        }
    return m


def attn_schedule(alibi):
    for J in range(4):
        for h in range(16):
            qb = qblk_of_head(h, alibi)
            for s_ in range(512 // qb):
                off = J * 512 + s_ * qb
                j0 = off // 128
                QB = max(qb // 128, 1)
                KT = 2 * (j0 + QB)
                NM = 2 * QB
                yield (J, h, qb, j0, off, KT, NM)

def qblk_of_head(h, alibi):
    if not alibi:
        return 512
    return {0: 64, 1: 128, 2: 128, 3: 128, 4: 128, 5: 256, 6: 256}.get(h, 512)

MASK_CLASSES_L0 = ((512, 8, [0], [0]),)
MASK_CLASSES_L1 = ((512, 8, [0], [0]), (256, 4, [0], [0]),
                   (128, 2, [0, 1], [0]), (64, 2, [0, 1], [0, 1]))

def mask_patterns(half, alibi):
    pats = {}
    def par_fn(lt):
        return (lt % 2) if half == 0 else (1 - lt % 2)
    for qb, rels, pars, h64s in (MASK_CLASSES_L1 if alibi else MASK_CLASSES_L0):
        QB = max(qb // 128, 1)
        for par in pars:
            for h64 in h64s:
                j0 = par
                for rel in range(rels):
                    m = np.zeros((128, qb), np.float32)
                    for c in range(qb):
                        lt = j0 + (c + (h64 * 64 if qb == 64 else 0)) // 128
                        a = 2 * lt + par_fn(lt)
                        qpos = a * 128 + (c + h64 * 64) % 128
                        kbase = (2 * j0 + rel) * 128
                        i = np.arange(128)
                        m[:, c] = (kbase + i <= qpos).astype(np.float32)
                    pats[(qb, rel, par, h64)] = m
    keys = sorted(pats.keys())
    offs, bank, o = {}, [], 0
    for k in keys:
        offs[k] = o
        bank.append(pats[k])
        o += k[0]
    return np.concatenate(bank, axis=1), offs

def mask_offsets(alibi):
    offs, o = {}, 0
    for k in sorted([(qb, rel, par, h64)
                     for qb, rels, pars, h64s in (MASK_CLASSES_L1 if alibi else MASK_CLASSES_L0)
                     for par in pars for h64 in h64s for rel in range(rels)]):
        offs[k] = o
        o += k[0]
    return offs, o

def host_attn_inputs(layer, core, q, k_full, vpad, woutT, boutcol):
    half = core % 2
    alibi = layer == 1
    tm = tmap(half)
    maskbank, _ = mask_patterns(half, alibi)
    ncols = sum(KT for (_, _, _, _, _, KT, _) in attn_schedule(alibi))
    biascols = np.zeros((ncols, 128), np.float32)
    i = 0
    for (J, h, qb, j0, off, KT, NM) in attn_schedule(alibi):
        lts = range(j0, j0 + max(qb // 128, 1))
        if qb == 64:
            a = 2 * j0 + ((j0 % 2) if half == 0 else (1 - j0 % 2))
            h64 = (off % 128) // 64
            maxpos = a * 128 + h64 * 64 + 63
        else:
            maxpos = max(2 * lt + (((lt % 2) if half == 0 else (1 - lt % 2)))
                         for lt in lts) * 128 + 127
        q_end = maxpos + 1
        own_max_tile = maxpos // 128
        sl = ALIBI_SLOPES[h] if alibi else 0.0
        for kt in range(KT):
            if kt > own_max_tile:
                biascols[i] = -60000.0
            elif alibi:
                biascols[i] = sl * ((kt * 128 + np.arange(128)) - q_end)
            i += 1
    import ml_dtypes
    m = {
        "q": q, "k": k_full, "vpad": vpad, "woutT": woutT, "boutcol": boutcol,
        "maskbank": np.ascontiguousarray(maskbank).astype(ml_dtypes.bfloat16),
    }
    if alibi:
        m["biascols"] = np.ascontiguousarray(biascols.T)
    return m


def build_attn(layer):
    alibi = layer == 1
    nc = bass.Bass("TRN2", target_bir_lowering=False)
    moffs, mcols = mask_offsets(alibi)
    ncols = sum(KT for (_, _, _, _, _, KT, _) in attn_schedule(alibi))
    q_h = nc.dram_tensor("q", [NCH, 128, NTOK], BF16, kind="ExternalInput").ap()
    k_h = nc.dram_tensor("k", [NCH, 128, 4096], BF16, kind="ExternalInput").ap()
    v_h = nc.dram_tensor("vpad", [32, 128, 1040], BF16, kind="ExternalInput").ap()
    w_h = nc.dram_tensor("woutT", [NCH, 128, D], F32, kind="ExternalInput").ap()
    bo_h = nc.dram_tensor("boutcol", [NCH, 128, 1], F32, kind="ExternalInput").ap()
    if alibi:
        bc_h = nc.dram_tensor("biascols", [128, ncols], F32, kind="ExternalInput").ap()
    mk_h = nc.dram_tensor("maskbank", [128, mcols], BF16, kind="ExternalInput").ap()
    o_s = nc.dram_tensor("o_s", [NCH, 128, NTOK], F32, kind="ExternalOutput").ap()

    p = Plan(nc)
    st = ExitStack()
    qj_sb = [[sb(st, nc, f"q{i}_{d}", [128, TB], BF16) for d in range(NCH)]
             for i in range(2)]
    k_sb = [sb(st, nc, f"k{d}", [128, 4096], BF16) for d in range(NCH)]
    v_sb = [sb(st, nc, f"v{t}", [128, 1040], BF16) for t in range(32)]
    w_sb = [sb(st, nc, f"w{d}", [128, D], BF16) for d in range(NCH)]
    bo_sb = [sb(st, nc, f"bo{d}", [128, 1], F32) for d in range(NCH)]
    bc_sb = sb(st, nc, "bc", [128, ncols], F32) if alibi else None
    mk_sb = sb(st, nc, "mk", [128, mcols], BF16)
    ones_f = sb(st, nc, "onesf", [1, 64], F32)
    abuf = [sb(st, nc, f"ab{i}", [128, TB], BF16) for i in range(4)]
    rrow = sb(st, nc, "rrow", [1, TB], F32)
    rb_sb = sb(st, nc, "rb", [64, TB], F32)
    s_buf = [sb(st, nc, f"sb{d}", [128, TB], BF16) for d in range(NCH)]

    s_ld = sem(st, nc, "ld")
    s_sc = sem(st, nc, "sc")
    s_exp = sem(st, nc, "exp")
    s_msk = sem(st, nc, "msk")
    s_av = sem(st, nc, "av")
    s_fz = sem(st, nc, "fz")
    s_pj = sem(st, nc, "pj")
    s_fin = sem(st, nc, "fin")

    s_qld = sem(st, nc, "qld")
    qld = [0]
    def load_qJ(Jn):
        for d in range(NCH):
            p.y.dma_start(qj_sb[Jn % 2][d][:], q_h[d, :, Jn*TB:(Jn+1)*TB])\
                .then_inc(s_qld, 16)
            qld[0] += 16
    ld = 0
    load_qJ(0)
    load_qJ(1)
    for d in range(NCH):
        p.y.dma_start(k_sb[d][:], k_h[d]).then_inc(s_ld, 16); ld += 16
    for t in range(32):
        p.y.dma_start(v_sb[t][:], v_h[t]).then_inc(s_ld, 16); ld += 16
    for d in range(NCH):
        p.g.dma_start(w_sb[d][:], w_h[d]).then_inc(s_ld, 16); ld += 16
        p.g.dma_start(bo_sb[d][:], bo_h[d]).then_inc(s_ld, 16); ld += 16
    if alibi:
        p.g.dma_start(bc_sb[:], bc_h).then_inc(s_ld, 16); ld += 16
    p.g.dma_start(mk_sb[:], mk_h).then_inc(s_ld, 16); ld += 16
    p.v.memset(ones_f[:], 1.0)
    for e in (p.t, p.s, p.v):
        e.wait_ge(s_ld, ld)

    ps_sc = [ps(st, nc, f"psc{i}", [128, TB]) for i in range(3)]
    ps_o = ps(st, nc, "pso", [65, TB])
    ps_b = ps(st, nc, "psb", [64, TB])
    ps_pj = [ps(st, nc, f"pspj{i}", [128, TB]) for i in range(2)]
    ev = StagedEvac(p, st, st, "es", n_stages=2, dtype=F32)

    C = dict(sc=0, exp=0, msk=0, av=0, fz=0, pj=0, g=0)
    frees_pj = []
    colix = 0
    sched = list(attn_schedule(alibi))
    si = 0
    for J in range(4):
        p.t.wait_ge(s_qld, 128 * (J + 1))
        items = [it for it in sched if it[0] == J]
        for (J_, h, qb, j0, off, KT, NM) in items:
            hp, hh = h // 2, h % 2
            rows = slice(hh * 64, hh * 64 + 64)
            qcols = slice(off - J * 512, off - J * 512 + qb)
            for kt in range(KT):
                g = C["g"]
                bank = ps_sc[g % 3]
                if g >= 3:
                    p.t.wait_ge(s_exp, g - 2)
                p.t.matmul(bank[:, :qb], k_sb[hp][rows, kt*128:(kt+1)*128],
                           qj_sb[J % 2][hp][rows, qcols], start=True, stop=True)\
                    .then_inc(s_sc, 1)
                C["sc"] += 1
                ab = abuf[g % 4]
                p.s.wait_ge(s_sc, C["sc"])
                if g >= 4:
                    p.s.wait_ge(s_av, g - 3)
                bias = bc_sb[:, colix:colix+1] if alibi else 0.0
                p.s.activation(ab[:, :qb], bank[:, :qb], AF.Exp, bias=bias)\
                    .then_inc(s_exp, 1)
                C["exp"] += 1
                colix += 1
                masked = kt >= KT - NM
                if masked:
                    rel = kt - (KT - NM)
                    par = j0 % 2 if qb <= 128 else 0
                    h64 = (off % 128) // 64 if qb == 64 else 0
                    mo = moffs[(qb, rel, par, h64)]
                    p.v.wait_ge(s_exp, C["exp"])
                    p.v.tensor_tensor(ab[:, :qb], ab[:, :qb],
                                      mk_sb[:, mo:mo+qb], OP.mult)\
                        .then_inc(s_msk, 1)
                    C["msk"] += 1
                if masked:
                    p.t.wait_ge(s_msk, C["msk"])
                else:
                    p.t.wait_ge(s_exp, C["exp"])
                if kt == 0:
                    if si > 0:
                        p.t.wait_ge(s_fz, 3 * si)
                p.t.matmul(ps_o[:, :qb], v_sb[kt][:, h*65:(h+1)*65],
                           ab[:, :qb], start=(kt == 0), stop=(kt == KT - 1))\
                    .then_inc(s_av, 1)
                C["av"] += 1
                C["g"] += 1
            p.v.wait_ge(s_av, C["av"])
            p.v.reciprocal(rrow[:, :qb], ps_o[64:65, :qb]).then_inc(s_fz, 1)
            C["fz"] += 1
            p.t.wait_ge(s_fz, C["fz"])
            if si > 0:
                p.t.wait_ge(s_fz, 3 * si)
            p.t.matmul(ps_b[:, :qb], ones_f[:], rrow[:, :qb], start=True,
                       stop=True).then_inc(s_sc, 1)
            C["sc"] += 1
            p.v.wait_ge(s_sc, C["sc"])
            p.v.tensor_copy(rb_sb[:, :qb], ps_b[:, :qb]).then_inc(s_fz, 1)
            C["fz"] += 1
            scols = slice(off - J * 512, off - J * 512 + qb)
            p.v.tensor_tensor(s_buf[hp][rows, scols], ps_o[0:64, :qb],
                              rb_sb[:, :qb], OP.mult).then_inc(s_fz, 1)
            C["fz"] += 1
            si += 1
        p.t.wait_ge(s_fz, 3 * si)
        for oc in range(NCH):
            bank = ps_pj[oc % 2]
            gp = J * NCH + oc
            if gp >= 2:
                fs, fv = frees_pj[gp - 2]
                p.t.wait_ge(fs, fv)
            for e in range(NCH):
                hmm = p.t.matmul(bank[:], w_sb[e][:, oc*128:(oc+1)*128],
                                 s_buf[e][:], start=(e == 0), stop=(e == NCH - 1))
            hmm.then_inc(s_pj, 1)
            C["pj"] += 1
            frees_pj.append(ev(bank[:], s_pj, C["pj"], o_s[oc, :, J*512:(J+1)*512],
                               bias=bo_sb[oc][:]))
        p.v.wait_ge(s_pj, C["pj"])
        if J + 2 < 4:
            p.y.wait_ge(s_pj, C["pj"])
            load_qJ(J + 2)
    ev.flush_wait()
    p.build()
    st.close()
    return nc


def build_moe0(NEXP=14):
    nc = bass.Bass("TRN2", target_bir_lowering=False)
    NE = 14
    xnbf_h = nc.dram_tensor("xnbf", [NCH, 128, NTOK], BF16, kind="ExternalInput").ap()
    xn_h = nc.dram_tensor("xn", [NCH, 128, NTOK], F32, kind="ExternalInput").ap()
    cf_h = nc.dram_tensor("coeffT", [NE, NTOK], F32, kind="ExternalInput").ap()
    w1_h = nc.dram_tensor("w1", [NE, NCH, 128, 2048], F32, kind="ExternalInput").ap()
    b1_h = nc.dram_tensor("b1c", [128, NE * 16], F32, kind="ExternalInput").ap()
    w2_h = nc.dram_tensor("w2", [NE, 16, 128, D], F32, kind="ExternalInput").ap()
    b2_h = nc.dram_tensor("b2r", [NE, D], F32, kind="ExternalInput").ap()
    x_h = nc.dram_tensor("xT", [NCH, 128, NTOK], F32, kind="ExternalInput").ap()
    s_h = nc.dram_tensor("s0", [NCH, 128, NTOK], F32, kind="ExternalInput").ap()
    sel_h = nc.dram_tensor("sel", [14, 128 * 14], F32, kind="ExternalInput").ap()
    o_x1 = nc.dram_tensor("o_x1", [NCH, 128, NTOK], F32, kind="ExternalOutput").ap()

    p = Plan(nc)
    st = ExitStack()
    xnbf_sb = [sb(st, nc, f"xnbf{d}", [128, NTOK], BF16) for d in range(NCH)]
    gacc = [sb(st, nc, f"ga{d}", [128, NTOK], F32) for d in range(NCH)]
    cf_sb = sb(st, nc, "cfsb", [NE, NTOK], F32)
    b1_sb = sb(st, nc, "b1sb", [128, NE * 16], F32)
    b2_sb = sb(st, nc, "b2sb", [NE, D], F32)
    ones_f = sb(st, nc, "onesf", [1, 128], F32)
    sel_sb = sb(st, nc, "selsb", [14, 128 * 14], F32)
    cb_sb = sb(st, nc, "cb", [128, TB], F32)
    tmp_sb = sb(st, nc, "tmp", [128, TB], F32)
    s_ld = sem(st, nc, "ld")
    s_w = sem(st, nc, "w")
    s_mm = sem(st, nc, "mm")
    s_ev = sem(st, nc, "ev")
    s_acc = sem(st, nc, "acc")
    s_cb = sem(st, nc, "cbk")
    s_fin = sem(st, nc, "fin")
    p.v.memset(ones_f[:], 1.0)

    ld = 0
    for d in range(NCH):
        p.y.dma_start(xnbf_sb[d][:], xnbf_h[d]).then_inc(s_ld, 16); ld += 16
    p.y.dma_start(cf_sb[:], cf_h).then_inc(s_ld, 16); ld += 16
    p.y.dma_start(b1_sb[:], b1_h).then_inc(s_ld, 16); ld += 16
    p.y.dma_start(b2_sb[:], b2_h).then_inc(s_ld, 16); ld += 16
    p.y.dma_start(sel_sb[:], sel_h).then_inc(s_ld, 16); ld += 16
    for e in (p.t, p.s, p.v):
        e.wait_ge(s_ld, ld)

    iscope = ExitStack()
    xnb = [sb(iscope, nc, f"xnb{i}", [128, NTOK], F32, side="right") for i in range(2)]
    with ExitStack() as sti:
        ps_i = [ps(sti, nc, f"pi{i}", [128, TB]) for i in range(4)]
        g = 0
        for d in range(NCH):
            if d >= 2:
                p.y.wait_ge(s_acc, (d - 1) * NBLK)
            p.y.dma_start(xnb[d % 2][:], xn_h[d]).then_inc(s_ld, 16)
            ld += 16
            p.v.wait_ge(s_ld, ld)
            for blk in range(NBLK):
                bank = ps_i[g % 4]
                if g >= 4:
                    p.t.wait_ge(s_acc, g - 3)
                p.t.matmul(bank[:], b2_sb[:, d*128:(d+1)*128], cf_sb[:, cols(blk)],
                           start=True, stop=True).then_inc(s_mm, 1)
                p.v.wait_ge(s_mm, g + 1)
                p.v.tensor_tensor(gacc[d][:, cols(blk)], xnb[d % 2][:, cols(blk)],
                                  bank[:], OP.add).then_inc(s_acc, 1)
                g += 1
        NB_INIT = g

    iscope.close()
    wscope = ExitStack()
    w1_sb = [sb(wscope, nc, f"w1_{d}", [128, 2048], BF16) for d in range(NCH)]
    w2_sb = [sb(wscope, nc, f"w2_{f}", [128, D], BF16) for f in range(16)]
    h_sb = [sb(wscope, nc, f"h{f}", [128, TB], BF16) for f in range(16)]
    ps_h = [ps(wscope, nc, f"ph{i}", [128, TB]) for i in range(2)]
    ps_o = [ps(wscope, nc, f"po{i}", [128, TB]) for i in range(2)]
    ps_c = ps(wscope, nc, "pc", [128, TB])
    wld = [0]
    ghc = [0]
    omc = [0]
    acc2 = [NB_INIT]
    cbc = [0]
    mmc = [NB_INIT]

    def load_we(e):
        p.g.wait_ge(s_acc, NB_INIT + e * 32)
        for d in range(NCH):
            p.g.dma_start(w1_sb[d][:], w1_h[e, d]).then_inc(s_w, 16); wld[0] += 16
        for f in range(16):
            p.g.dma_start(w2_sb[f][:], w2_h[e, f]).then_inc(s_w, 16); wld[0] += 16

    for e in range(NEXP):
        load_we(e)
        p.t.wait_ge(s_w, wld[0])
        for tb in range(NBLK):
            if cbc[0] > 0:
                p.t.wait_ge(s_cb, 2 * cbc[0])
            p.t.matmul(ps_c[:], sel_sb[:, e*128:(e+1)*128], cf_sb[:, cols(tb)],
                       start=True, stop=True).then_inc(s_mm, 1)
            mmc[0] += 1
            p.v.wait_ge(s_mm, mmc[0])
            p.v.tensor_copy(cb_sb[:], ps_c[:]).then_inc(s_cb, 1)
            for fc in range(16):
                bank = ps_h[fc % 2]
                gh = ghc[0]
                if gh >= 2:
                    p.t.wait_ge(s_ev, gh - 1)
                for d in range(NCH):
                    h = p.t.matmul(bank[:], w1_sb[d][:, fc*128:(fc+1)*128],
                                   xnbf_sb[d][:, cols(tb)],
                                   start=(d == 0), stop=(d == NCH - 1))
                h.then_inc(s_mm, 1)
                mmc[0] += 1
                p.s.wait_ge(s_mm, mmc[0])
                p.s.activation(h_sb[fc][:], bank[:], AF.Gelu,
                               bias=b1_sb[:, e*16+fc:e*16+fc+1]).then_inc(s_ev, 1)
                ghc[0] += 1
            for dd in range(NCH):
                bank = ps_o[dd % 2]
                om = omc[0]
                if dd == 0:
                    p.t.wait_ge(s_ev, ghc[0])
                if om >= 2:
                    p.t.wait_ge(s_acc, acc2[0] - 1)
                for f in range(16):
                    h = p.t.matmul(bank[:], w2_sb[f][:, dd*128:(dd+1)*128],
                                   h_sb[f][:], start=(f == 0), stop=(f == 15))
                h.then_inc(s_mm, 1)
                mmc[0] += 1
                p.v.wait_ge(s_mm, mmc[0])
                p.v.tensor_tensor(tmp_sb[:], bank[:], cb_sb[:], OP.mult)
                p.v.tensor_tensor(gacc[dd][:, cols(tb)], gacc[dd][:, cols(tb)],
                                  tmp_sb[:], OP.add).then_inc(s_acc, 1)
                acc2[0] += 1
                omc[0] += 1
            p.v.nop().then_inc(s_cb, 1)
            cbc[0] += 1
    wscope.close()

    fscope = ExitStack()
    xb = [sb(fscope, nc, f"xb{i}", [128, NTOK], F32, side="right") for i in range(2)]
    sb_ = [sb(fscope, nc, f"sbb{i}", [128, NTOK], F32, side="right") for i in range(2)]
    gb = sb(fscope, nc, "gb", [128, NTOK], F32, side="right")
    x1b = [sb(fscope, nc, f"x1b{i}", [128, NTOK], F32, side="right") for i in range(2)]
    s_clip = sem(st, nc, "clip")
    s_silu = sem(st, nc, "silu")
    s_x1 = sem(st, nc, "x1")
    s_od = sem(st, nc, "od")
    echo = sb(fscope, nc, "echo", [128, 1], F32, side="right")
    p.y.wait_ge(s_acc, acc2[0])
    for d in range(NCH):
        if d >= 2:
            p.y.wait_ge(s_x1, d - 1)
        p.y.dma_start(xb[d % 2][:], x_h[d]).then_inc(s_ld, 16)
        p.y.dma_start(sb_[d % 2][:], s_h[d]).then_inc(s_ld, 16)
        ld += 32
        p.v.wait_ge(s_ld, ld)
        p.v.tensor_scalar(gacc[d][:], gacc[d][:], 100.0, -100.0, OP.min, OP.max)\
            .then_inc(s_clip, 1)
        p.s.wait_ge(s_clip, d + 1)
        if d >= 1:
            p.s.wait_ge(s_x1, d)
        p.s.activation(gb[:], gacc[d][:], AF.Silu).then_inc(s_silu, 1)
        p.v.wait_ge(s_silu, d + 1)
        if d >= 2:
            p.v.wait_ge(s_od, 16 * (d - 1))
        p.v.tensor_tensor(gb[:], gb[:], sb_[d % 2][:], OP.mult)
        p.v.tensor_tensor(x1b[d % 2][:], gb[:], xb[d % 2][:], OP.add)
        p.v.tensor_copy(echo[:], x1b[d % 2][:, NTOK-1:NTOK]).then_inc(s_x1, 1)
        p.y.wait_ge(s_x1, d + 1)
        p.y.dma_start(o_x1[d], x1b[d % 2][:]).then_inc(s_od, 16)
    p.y.wait_ge(s_od, 16 * NCH)
    p.build()
    fscope.close()
    st.close()
    return nc


def host_moe0_inputs(d, core, xn, xnbf, coeffT, s0, ri=None):
    half = core % 2
    b = core // 2
    return {
        "xnbf": xnbf, "xn": xn, "coeffT": coeffT,
        "w1": np.ascontiguousarray(d["L0_ffn_w1"].reshape(14, 8, 128, 2048)),
        "b1c": np.ascontiguousarray(d["L0_ffn_b1"].reshape(14 * 16, 128).T),
        "w2": np.ascontiguousarray(d["L0_ffn_w2"].reshape(14, 16, 128, 1024)),
        "b2r": np.ascontiguousarray(d["L0_ffn_b2"]),
        "xT": np.ascontiguousarray(d["L0_attn_x"][b][own_token_idx(half)].T)
              .reshape(NCH, 128, NTOK),
        "s0": s0,
        "sel": np.ascontiguousarray(
            np.repeat(np.eye(14, dtype=np.float32), 128, axis=1)),
    }


def build_ffn1(NEXP=16, BIS=40):
    nc = bass.Bass("TRN2", target_bir_lowering=False)
    NE = 16
    S = 4096
    xnbf_h = nc.dram_tensor("xnbf", [NCH, 128, NTOK], BF16, kind="ExternalInput").ap()
    lg_h = nc.dram_tensor("logitsP", [NE, S], F32, kind="ExternalInput").ap()
    w1_h = nc.dram_tensor("w1", [NE, NCH, 128, D], F32, kind="ExternalInput").ap()
    w2_h = nc.dram_tensor("w2", [NE, NCH, 128, D], F32, kind="ExternalInput").ap()
    w3_h = nc.dram_tensor("w3", [NE, NCH, 128, D], F32, kind="ExternalInput").ap()
    b1_h = nc.dram_tensor("b1r", [1, NE], F32, kind="ExternalInput").ap()
    b2_h = nc.dram_tensor("b2r", [1, NE], F32, kind="ExternalInput").ap()
    b3_h = nc.dram_tensor("b3c", [NE, 1], F32, kind="ExternalInput").ap()
    sel_h = nc.dram_tensor("sel", [NE, 128 * NE], F32, kind="ExternalInput").ap()
    x_h = nc.dram_tensor("x1T", [NCH, 128, NTOK], F32, kind="ExternalInput").ap()
    s_h = nc.dram_tensor("s1", [NCH, 128, NTOK], F32, kind="ExternalInput").ap()
    o_x2 = nc.dram_tensor("o_x2", [NCH, 128, NTOK], F32, kind="ExternalOutput").ap()
    o_cf = nc.dram_tensor("o_cf", [NE, S], F32, kind="ExternalOutput").ap()

    p = Plan(nc)
    st = ExitStack()
    xnbf_sb = [sb(st, nc, f"xnbf{d}", [128, NTOK], BF16) for d in range(NCH)]
    gacc = [sb(st, nc, f"ga{d}", [128, NTOK], F32) for d in range(NCH)]
    coefP = sb(st, nc, "coefP", [128, S], F32)
    sel_sb = sb(st, nc, "selsb", [NE, 128 * NE], F32)
    b3c_sb = sb(st, nc, "b3csb", [NE, 1], F32)
    ones_f = sb(st, nc, "onesf", [1, 128], F32)
    cb_sb = sb(st, nc, "cb", [128, TB], F32)
    tmp_sb = sb(st, nc, "tmp", [128, TB], F32)
    B1c = sb(st, nc, "B1c", [128, NE], F32)
    B2c = sb(st, nc, "B2c", [128, NE], F32)
    rowb = sb(st, nc, "rowb", [1, TB], F32)
    colw = {nm: sb(st, nc, f"c_{nm}", [128, 1], F32)
            for nm in ("mx", "negmx", "sum", "rsum", "lo", "hi", "u", "t",
                       "cnt", "sel1", "nsel", "dlt")}
    s_ld = sem(st, nc, "ld")
    s_w = sem(st, nc, "w")
    s_mm = sem(st, nc, "mm")
    s_ev = sem(st, nc, "ev")
    s_acc = sem(st, nc, "acc")
    s_cb = sem(st, nc, "cbk")
    s_gate = sem(st, nc, "gate")
    p.v.memset(ones_f[:], 1.0)

    ld = 0
    for d in range(NCH):
        p.y.dma_start(xnbf_sb[d][:], xnbf_h[d]).then_inc(s_ld, 16); ld += 16
    p.y.dma_start(sel_sb[:], sel_h).then_inc(s_ld, 16); ld += 16
    p.y.dma_start(b3c_sb[:], b3_h).then_inc(s_ld, 16); ld += 16

    gscope = ExitStack()
    lv_sb = sb(gscope, nc, "lv", [128, S], F32, side="right")
    pv_sb = sb(gscope, nc, "pv", [128, S], F32, side="right")
    ge_sb = sb(gscope, nc, "ge", [128, S], F32, side="right")
    b1r_sb = sb(gscope, nc, "b1rsb", [1, NE], F32, side="right")
    b2r_sb = sb(gscope, nc, "b2rsb", [1, NE], F32, side="right")
    p.v.memset(lv_sb[:], 0.0).then_inc(s_gate, 100)
    p.y.wait_ge(s_gate, 100)
    p.y.dma_start(lv_sb[0:NE, :], lg_h).then_inc(s_ld, 16); ld += 16
    p.y.dma_start(b1r_sb[:], b1_h).then_inc(s_ld, 16); ld += 16
    p.y.dma_start(b2r_sb[:], b2_h).then_inc(s_ld, 16); ld += 16
    for e in (p.t, p.s, p.v):
        e.wait_ge(s_ld, ld)
    with ExitStack() as stb:
        ps_bc = [ps(stb, nc, f"pbc{i}", [128, NE]) for i in range(2)]
        p.t.matmul(ps_bc[0][:], ones_f[:], b1r_sb[:], start=True, stop=True)
        p.t.matmul(ps_bc[1][:], ones_f[:], b2r_sb[:], start=True, stop=True)\
            .then_inc(s_mm, 1)
        p.v.wait_ge(s_mm, 1)
        p.v.tensor_copy(B1c[:], ps_bc[0][:])
        p.v.tensor_copy(B2c[:], ps_bc[1][:]).then_inc(s_gate, 1)
    X = mybir.AxisListType.X
    p.v.reduce_max(colw["mx"][:], lv_sb[:], axis=X)
    p.v.tensor_scalar(colw["negmx"][:], colw["mx"][:], -1.0, None, OP.mult)\
        .then_inc(s_gate, 1)
    p.s.wait_ge(s_gate, 102)
    p.s.activation(pv_sb[:], lv_sb[:], AF.Exp, bias=colw["negmx"][:])\
        .then_inc(s_gate, 1)
    p.v.wait_ge(s_gate, 103)
    p.v.reduce_sum(colw["sum"][:], pv_sb[:], axis=X)
    p.v.reciprocal(colw["rsum"][:], colw["sum"][:])
    p.v.tensor_scalar(pv_sb[:], pv_sb[:], colw["rsum"][:], None, OP.mult)
    p.v.memset(colw["lo"][:], 0.0)
    p.v.memset(colw["hi"][:], 1.0)
    for it in range(BIS):
        p.v.tensor_tensor(colw["u"][:], colw["lo"][:], colw["hi"][:], OP.add)
        p.v.tensor_scalar(colw["t"][:], colw["u"][:], 0.5, None, OP.mult)
        p.v.tensor_scalar(ge_sb[:], pv_sb[:], colw["t"][:], None, OP.is_ge)
        p.v.reduce_sum(colw["cnt"][:], ge_sb[:], axis=X)
        p.v.tensor_scalar(colw["sel1"][:], colw["cnt"][:], 256.0, None, OP.is_ge)
        p.v.tensor_scalar(colw["nsel"][:], colw["sel1"][:], -1.0, 1.0, OP.mult, OP.add)
        p.v.tensor_tensor(colw["dlt"][:], colw["t"][:], colw["lo"][:], OP.subtract)
        p.v.tensor_tensor(colw["dlt"][:], colw["dlt"][:], colw["sel1"][:], OP.mult)
        p.v.tensor_tensor(colw["lo"][:], colw["lo"][:], colw["dlt"][:], OP.add)
        p.v.tensor_tensor(colw["dlt"][:], colw["t"][:], colw["hi"][:], OP.subtract)
        p.v.tensor_tensor(colw["dlt"][:], colw["dlt"][:], colw["nsel"][:], OP.mult)
        p.v.tensor_tensor(colw["hi"][:], colw["hi"][:], colw["dlt"][:], OP.add)
    p.v.tensor_scalar(ge_sb[:], pv_sb[:], colw["lo"][:], None, OP.is_ge)
    p.v.tensor_tensor(coefP[:], pv_sb[:], ge_sb[:], OP.mult).then_inc(s_gate, 1)

    p.y.wait_ge(s_gate, 104)
    p.y.dma_start(o_cf, coefP[0:NE, :]).then_inc(s_ld, 16)
    ld += 16
    with ExitStack() as sti:
        ps_r = ps(sti, nc, "pr", [1, TB])
        ps_b = [ps(sti, nc, f"pb{i}", [128, TB]) for i in range(2)]
        p.t.wait_ge(s_gate, 104)
        g = 0
        acc0 = 0
        for tb in range(NBLK):
            if tb >= 1:
                p.t.wait_ge(s_gate, 104 + tb)
            p.t.matmul(ps_r[:], b3c_sb[:], coefP[0:NE, cols(tb)], start=True,
                       stop=True).then_inc(s_mm, 1)
            p.v.wait_ge(s_mm, 2 + 3 * tb)
            p.v.tensor_copy(rowb[:], ps_r[:]).then_inc(s_gate, 1)
            bank = ps_b[tb % 2]
            p.t.wait_ge(s_gate, 105 + tb)
            if tb >= 2:
                p.t.wait_ge(s_acc, (tb - 1) * NCH)
            p.t.matmul(bank[:], ones_f[:], rowb[:], start=True, stop=True)\
                .then_inc(s_mm, 2)
            p.v.wait_ge(s_mm, 4 + 3 * tb)
            for d in range(NCH):
                p.v.tensor_copy(gacc[d][:, cols(tb)], bank[:]).then_inc(s_acc, 1)
                acc0 += 1
        NB_INIT = acc0
        MM_INIT = 1 + 3 * NBLK
    gscope.close()

    wscope = ExitStack()
    w1_sb = [sb(wscope, nc, f"w1_{d}", [128, D], BF16) for d in range(NCH)]
    w2_sb = [sb(wscope, nc, f"w2_{d}", [128, D], BF16) for d in range(NCH)]
    w3_sb = [sb(wscope, nc, f"w3_{d}", [128, D], BF16) for d in range(NCH)]
    h1 = [sb(wscope, nc, f"h1_{d}", [128, TB], BF16) for d in range(NCH)]
    h2 = [sb(wscope, nc, f"h2_{d}", [128, TB], BF16) for d in range(NCH)]
    ps_h = [ps(wscope, nc, f"ph{i}", [128, TB]) for i in range(2)]
    ps_o = [ps(wscope, nc, f"po{i}", [128, TB]) for i in range(2)]
    ps_c = ps(wscope, nc, "pc", [128, TB])
    wld = [0]
    evc = [0]
    omc = [0]
    accc = [NB_INIT]
    cbc = [0]
    mmc = [MM_INIT]

    for e in range(NEXP):
        p.g.wait_ge(s_acc, NB_INIT + e * 32)
        for d in range(NCH):
            p.g.dma_start(w1_sb[d][:], w1_h[e, d]).then_inc(s_w, 16); wld[0] += 16
            p.g.dma_start(w2_sb[d][:], w2_h[e, d]).then_inc(s_w, 16); wld[0] += 16
            p.g.dma_start(w3_sb[d][:], w3_h[e, d]).then_inc(s_w, 16); wld[0] += 16
        p.t.wait_ge(s_w, wld[0])
        for tb in range(NBLK):
            if cbc[0] > 0:
                p.t.wait_ge(s_cb, 2 * cbc[0])
            p.t.matmul(ps_c[:], sel_sb[:, e*128:(e+1)*128], coefP[0:NE, cols(tb)],
                       start=True, stop=True).then_inc(s_mm, 1)
            mmc[0] += 1
            p.v.wait_ge(s_mm, mmc[0])
            p.v.tensor_copy(cb_sb[:], ps_c[:]).then_inc(s_cb, 1)
            for stage, wbuf, src, dst, bcol in (
                    (1, w1_sb, None, h1, B1c), (2, w2_sb, h1, h2, B2c)):
                for oc in range(NCH):
                    bank = ps_h[oc % 2]
                    gh = evc[0]
                    if stage == 2 and oc == 0:
                        p.t.wait_ge(s_ev, evc[0])
                    elif gh >= 2:
                        p.t.wait_ge(s_ev, gh - 1)
                    for d in range(NCH):
                        rhs = xnbf_sb[d][:, cols(tb)] if stage == 1 else src[d][:]
                        h = p.t.matmul(bank[:], wbuf[d][:, oc*128:(oc+1)*128],
                                       rhs, start=(d == 0), stop=(d == NCH - 1))
                    h.then_inc(s_mm, 1)
                    mmc[0] += 1
                    p.s.wait_ge(s_mm, mmc[0])
                    p.s.activation(dst[oc][:], bank[:], AF.Relu,
                                   bias=bcol[:, e:e+1]).then_inc(s_ev, 1)
                    evc[0] += 1
            for dd in range(NCH):
                bank = ps_o[dd % 2]
                om = omc[0]
                if dd == 0:
                    p.t.wait_ge(s_ev, evc[0])
                if om >= 2:
                    p.t.wait_ge(s_acc, accc[0] - 1)
                for f in range(NCH):
                    h = p.t.matmul(bank[:], w3_sb[f][:, dd*128:(dd+1)*128],
                                   h2[f][:], start=(f == 0), stop=(f == NCH - 1))
                h.then_inc(s_mm, 1)
                mmc[0] += 1
                p.v.wait_ge(s_mm, mmc[0])
                p.v.tensor_tensor(tmp_sb[:], bank[:], cb_sb[:], OP.mult)
                p.v.tensor_tensor(gacc[dd][:, cols(tb)], gacc[dd][:, cols(tb)],
                                  tmp_sb[:], OP.add).then_inc(s_acc, 1)
                accc[0] += 1
                omc[0] += 1
            p.v.nop().then_inc(s_cb, 1)
            cbc[0] += 1
    wscope.close()

    fscope = ExitStack()
    xb = [sb(fscope, nc, f"xb{i}", [128, NTOK], F32, side="right") for i in range(2)]
    sb_ = [sb(fscope, nc, f"sbb{i}", [128, NTOK], F32, side="right") for i in range(2)]
    gb = sb(fscope, nc, "gb", [128, NTOK], F32, side="right")
    x1b = [sb(fscope, nc, f"x1b{i}", [128, NTOK], F32, side="right") for i in range(2)]
    echo = sb(fscope, nc, "echo", [128, 1], F32, side="right")
    s_silu = sem(st, nc, "silu")
    s_x1 = sem(st, nc, "x1")
    s_od = sem(st, nc, "od")
    p.y.wait_ge(s_acc, accc[0])
    for d in range(NCH):
        if d >= 2:
            p.y.wait_ge(s_x1, d - 1)
        p.y.dma_start(xb[d % 2][:], x_h[d]).then_inc(s_ld, 16)
        p.y.dma_start(sb_[d % 2][:], s_h[d]).then_inc(s_ld, 16)
        ld += 32
        p.s.wait_ge(s_ld, ld)
        if d >= 1:
            p.s.wait_ge(s_x1, d)
        p.s.activation(gb[:], gacc[d][:], AF.Silu).then_inc(s_silu, 1)
        p.v.wait_ge(s_silu, d + 1)
        p.v.wait_ge(s_ld, ld)
        if d >= 2:
            p.v.wait_ge(s_od, 16 * (d - 1))
        p.v.tensor_tensor(gb[:], gb[:], sb_[d % 2][:], OP.mult)
        p.v.tensor_tensor(x1b[d % 2][:], gb[:], xb[d % 2][:], OP.add)
        p.v.tensor_copy(echo[:], x1b[d % 2][:, NTOK-1:NTOK]).then_inc(s_x1, 1)
        p.y.wait_ge(s_x1, d + 1)
        p.y.dma_start(o_x2[d], x1b[d % 2][:]).then_inc(s_od, 16)
    p.y.wait_ge(s_od, 16 * NCH)
    p.build()
    fscope.close()
    st.close()
    return nc


def host_ffn1_inputs(d, core, xnbf, logitsP, x1T, s1):
    return {
        "xnbf": xnbf, "logitsP": logitsP,
        "w1": np.ascontiguousarray(d["L1_ffn_w1"].reshape(16, 8, 128, 1024)),
        "w2": np.ascontiguousarray(d["L1_ffn_w2"].reshape(16, 8, 128, 1024)),
        "w3": np.ascontiguousarray(d["L1_ffn_w3"].reshape(16, 8, 128, 1024)),
        "b1r": np.ascontiguousarray(d["L1_ffn_b1"].reshape(1, 16)),
        "b2r": np.ascontiguousarray(d["L1_ffn_b2"].reshape(1, 16)),
        "b3c": np.ascontiguousarray(d["L1_ffn_b3"].reshape(16, 1)),
        "sel": np.ascontiguousarray(
            np.repeat(np.eye(16, dtype=np.float32), 128, axis=1)),
        "x1T": x1T, "s1": s1,
    }


import concourse.mybir as mybir
from concourse.alu_op_type import AluOpType as OP

X = mybir.AxisListType.X


def barrier(p, st, nc, name):
    s_b = sem(st, nc, name)
    p.y.nop().then_inc(s_b, 1)
    for e in (p.t, p.s, p.v, p.g, p.y):
        e.wait_ge(s_b, 1)


def emit_prep(p, st, nc, layer, io, do_main, do_kv, do_gate=None, x_is_bf16=False):
    NE = 14 if layer == 0 else 16
    if do_gate is None:
        do_gate = do_main
    xT, wqkvT, gwT, ident = io["xT"], io["wqkvT"], io["gwT"], io["ident"]

    x_scope = ExitStack()
    x_sb = [sb(x_scope, nc, f"x{d}", [128, NTOK], F32, side="right") for d in range(NCH)]
    xn_sb = [sb(st, nc, f"xn{d}", [128, NTOK], F32) for d in range(NCH)]
    xnbf_sb = [sb(st, nc, f"xnbf{d}", [128, NTOK], BF16) for d in range(NCH)]
    id_sb = sb(st, nc, "idf", [128, 128], F32)
    s_ld = sem(st, nc, "ld")
    s_fin = sem(st, nc, "fin")
    fin = 0
    ld = 0
    for d in range(NCH):
        eng = p.g if x_is_bf16 else p.y
        eng.dma_start(x_sb[d][:], xT[d]).then_inc(s_ld, 16)
        ld += 16
    p.g.dma_start(id_sb[:], ident).then_inc(s_ld, 16)
    ld += 16
    if layer == 0:
        gc_sb = [sb(st, nc, f"gc{d}", [128, 1], F32) for d in range(NCH)]
        bc_sb = [sb(st, nc, f"bc{d}", [128, 1], F32) for d in range(NCH)]
        for d in range(NCH):
            p.g.dma_start(gc_sb[d][:], io["gcol"][d]).then_inc(s_ld, 16)
            p.g.dma_start(bc_sb[d][:], io["bcol"][d]).then_inc(s_ld, 16)
            ld += 32
    elif do_gate:
        bg_sb = sb(st, nc, "bgsb", [NE, 1], F32)
        p.g.dma_start(bg_sb[:], io["bgc"]).then_inc(s_ld, 16)
        ld += 16
    for e in (p.s, p.v, p.t):
        e.wait_ge(s_ld, ld)

    ln1_scope = ExitStack()
    s_n1, n1 = emit_ln(p, ln1_scope, st, "ln1", [x_sb[d][:] for d in range(NCH)],
                       dst_f32=xn_sb, dst_bf=xnbf_sb)
    if do_main and ("o_xn" in io or "o_xnbf" in io):
        p.y.wait_ge(s_n1, n1)
        for d in range(NCH):
            if "o_xn" in io:
                p.y.dma_start(io["o_xn"][d], xn_sb[d][:]).then_inc(s_fin, 16)
                fin += 16
            if "o_xnbf" in io:
                p.y.dma_start(io["o_xnbf"][d], xnbf_sb[d][:]).then_inc(s_fin, 16)
                fin += 16

    if layer == 0:
        for e in (p.s, p.v, p.t):
            e.wait_ge(s_n1, n1)
        ln1_scope.close()
        x_scope.close()
        xn2bf_sb = [sb(st, nc, f"xn2bf{d}", [128, NTOK], BF16) for d in range(NCH)]
        ln2_scope = ExitStack()
        s_n2, n2 = emit_ln(p, ln2_scope, st, "ln2", [xn_sb[d][:] for d in range(NCH)],
                           dst_bf=xn2bf_sb, gamma=[gc_sb[d][:] for d in range(NCH)],
                           beta=[bc_sb[d][:] for d in range(NCH)])
        qkv_src = xn2bf_sb
        s_rdy, nrdy = s_n2, n2
        end_scope = ln2_scope
    else:
        qkv_src = xnbf_sb
        s_rdy, nrdy = s_n1, n1
        end_scope = ln1_scope

    for e in (p.s, p.v, p.t):
        e.wait_ge(s_rdy, nrdy)
    end_scope.close()
    if layer != 0:
        x_scope.close()

    w_sb = [sb(st, nc, f"w{d}", [128, 3072], BF16) for d in range(NCH)]
    gw_sb = [sb(st, nc, f"gw{d}", [128, NE], F32) for d in range(NCH)]
    p.g.wait_ge(s_rdy, nrdy)
    for d in range(NCH):
        p.g.dma_start(w_sb[d][:], wqkvT[d]).then_inc(s_ld, 16)
        ld += 16
        if do_gate:
            p.g.dma_start(gw_sb[d][:], gwT[d]).then_inc(s_ld, 16)
            ld += 16
    LD_W = ld
    for e in (p.s, p.v, p.t):
        e.wait_ge(s_ld, LD_W)

    qkv_scope = ExitStack()
    ev_qk = StagedEvac(p, qkv_scope, st, "eqk")

    if do_main:
        def evac_q(oc, blk, bank, s_mm, mmcnt):
            return ev_qk(bank[:], s_mm, mmcnt, io["o_q"][oc % 8, :, cols(blk)],
                         scale=0.125)
        with ExitStack() as fm1:
            lf = emit_fm_matmul(p, fm1, st, "q", list(range(8)), NCH,
                           lambda d, oc: w_sb[d][:, oc*128:(oc+1)*128],
                           lambda d, blk: qkv_src[d][:, cols(blk)], evac_q)
            p.t.wait_ge(lf[0], lf[1])
    last_free = None
    if do_kv:
        def evac_k(oc, blk, bank, s_mm, mmcnt):
            return ev_qk(bank[:], s_mm, mmcnt, io["o_k"][oc % 8, :, cols(blk)])
        with ExitStack() as fm2:
            last_free = emit_fm_matmul(p, fm2, st, "k", list(range(8, 16)), NCH,
                           lambda d, oc: w_sb[d][:, oc*128:(oc+1)*128],
                           lambda d, blk: qkv_src[d][:, cols(blk)], evac_k)
            p.t.wait_ge(last_free[0], last_free[1])

        ones_v = sb(qkv_scope, nc, "onesv", [128, 1040], BF16)
        s_ov = sem(st, nc, "ov")
        p.v.memset(ones_v[:], 1.0).then_inc(s_ov, 1)
        p.y.wait_ge(s_ov, 1)
        for tt in range(NTILE):
            p.y.dma_start(io["o_v"][tt], ones_v[:]).then_inc(s_ov, 16)
        p.y.wait_ge(s_ov, 1 + 16 * NTILE)

        def evac_v(tt, vb, bank, s_mm, mmcnt):
            dst = io["o_v"][tt].rearrange("p (h c) -> p h c", c=65)[:, vb*8:(vb+1)*8, 0:64]
            return ev_qk(bank[:], s_mm, mmcnt, dst)
        with ExitStack() as fm3:
            last_free = emit_fm_matmul(p, fm3, st, "v", list(range(NTILE)), NCH,
                           lambda d, tt: qkv_src[d][:, tt*128:(tt+1)*128],
                           lambda d, vb: w_sb[d][:, 2048+vb*TB:2048+(vb+1)*TB],
                           evac_v, nblk=2)
            p.t.wait_ge(last_free[0], last_free[1])
    ev_qk.flush_wait()
    p.v.wait_ge(ev_qk.s_dma, 16 * ev_qk.g)
    p.s.wait_ge(ev_qk.s_dma, 16 * ev_qk.g)
    qkv_scope.close()

    if do_gate:
        gscope = ExitStack()
        ps_g = [ps(gscope, nc, f"gps{i}", [128, NE]) for i in range(2)]
        ps_t = [ps(gscope, nc, f"gpt{i}", [128, 128]) for i in range(2)]
        s_gmm = sem(st, nc, "gmm")
        s_gv = sem(st, nc, "gv")
        s_gt = sem(st, nc, "gt")
        gateT_sb = sb(gscope, nc, "gateT", [NE, NTOK], F32)
        coef = [sb(gscope, nc, f"coef{i}", [128, NE], F32) for i in range(2)]
        work = {nm: sb(gscope, nc, f"gw_{nm}", [128, NE], F32)
                for nm in ("e", "pp", "pm", "ge2", "pw")}
        colw = {nm: sb(gscope, nc, f"gc_{nm}", [128, 1], F32)
                for nm in ("m", "negm", "sum", "rsum", "m1", "m2", "den", "rden")}
        VPT = 3 if layer == 0 else 1
        for tt in range(NTILE):
            bank = ps_g[tt % 2]
            if tt >= 2:
                p.t.wait_ge(s_gv, VPT * (tt - 2) + VPT)
            for d in range(NCH):
                h = p.t.matmul(bank[:], xn_sb[d][:, tt*128:(tt+1)*128],
                               gw_sb[d][:], start=(d == 0), stop=(d == NCH - 1))
            h.then_inc(s_gmm, 1)
            cf = coef[tt % 2]
            p.v.wait_ge(s_gmm, 2 * tt + 1)
            if layer == 0:
                p.v.reduce_max(colw["m"][:], bank[:], axis=X)
                p.v.tensor_scalar(colw["negm"][:], colw["m"][:], -1.0, None, OP.mult)\
                    .then_inc(s_gv, 1)
                p.s.wait_ge(s_gv, VPT * tt + 1)
                p.s.activation(work["e"][:], bank[:], AF.Exp, bias=colw["negm"][:])\
                    .then_inc(s_gv, 1)
                p.v.wait_ge(s_gv, VPT * tt + 2)
                p.v.reduce_sum(colw["sum"][:], work["e"][:], axis=X)
                p.v.reciprocal(colw["rsum"][:], colw["sum"][:])
                p.v.tensor_scalar(work["pp"][:], work["e"][:], colw["rsum"][:], None, OP.mult)
                p.v.reduce_max(colw["m1"][:], work["pp"][:], axis=X)
                p.v.tensor_scalar(work["ge2"][:], work["pp"][:], colw["m1"][:], None, OP.is_ge)
                p.v.tensor_tensor(work["pm"][:], work["pp"][:], work["ge2"][:], OP.mult)
                p.v.tensor_tensor(work["pm"][:], work["pp"][:], work["pm"][:], OP.subtract)
                p.v.reduce_max(colw["m2"][:], work["pm"][:], axis=X)
                p.v.tensor_tensor(colw["den"][:], colw["m1"][:], colw["m2"][:], OP.add)
                p.v.reciprocal(colw["rden"][:], colw["den"][:])
                p.v.tensor_scalar(work["ge2"][:], work["pp"][:], colw["m2"][:], None, OP.is_ge)
                p.v.tensor_scalar(work["pw"][:], work["pp"][:], colw["rden"][:], None, OP.mult)
                p.v.tensor_tensor(cf[:], work["pw"][:], work["ge2"][:], OP.mult)\
                    .then_inc(s_gv, 1)
            else:
                p.v.tensor_copy(cf[:], bank[:]).then_inc(s_gv, 1)
            tbk = ps_t[tt % 2]
            p.t.wait_ge(s_gv, VPT * tt + VPT)
            if tt >= 2:
                p.t.wait_ge(s_gt, tt - 1)
            p.t.transpose(tbk[:NE, :], cf[:], id_sb[:]).then_inc(s_gmm, 1)
            p.v.wait_ge(s_gmm, 2 * tt + 2)
            if layer == 0:
                hh = p.v.tensor_copy(gateT_sb[:, tt*128:(tt+1)*128], tbk[:NE, :])
            else:
                hh = p.v.tensor_scalar(gateT_sb[:, tt*128:(tt+1)*128], tbk[:NE, :],
                                       bg_sb[:], None, OP.add)
            hh.then_inc(s_gt, 1)
        p.y.wait_ge(s_gt, NTILE)
        p.y.dma_start(io["o_gate"], gateT_sb[:]).then_inc(s_fin, 16)
        fin += 16
        p.y.wait_ge(s_fin, fin)
        gscope.close()
    else:
        if fin:
            p.y.wait_ge(s_fin, fin)


def emit_attn(p, st, nc, layer, io):
    alibi = layer == 1
    moffs, mcols = mask_offsets(alibi)
    ncols = sum(KT for (_, _, _, _, _, KT, _) in attn_schedule(alibi))
    qj_sb = [[sb(st, nc, f"q{i}_{d}", [128, TB], BF16) for d in range(NCH)]
             for i in range(2)]
    kH_sb = [[sb(st, nc, f"k{hf}_{d}", [128, NTOK], BF16) for d in range(NCH)]
             for hf in range(2)]
    v_sb = [sb(st, nc, f"v{t}", [128, 1040], BF16) for t in range(32)]
    import os as _os2
    WRES = _os2.environ.get("ATTN_WRES")
    w_sb = None
    wsl = None
    if WRES:
        w_sb = [sb(st, nc, f"w{d}", [128, D], BF16) for d in range(NCH)]
    else:
        wsl = [[sb(st, nc, f"wsl{i}_{d}", [128, 128], BF16) for d in range(NCH)]
               for i in range(2)]
    bo_sb = [sb(st, nc, f"bo{d}", [128, 1], F32) for d in range(NCH)]
    bc_sb = sb(st, nc, "bc", [128, ncols], F32) if alibi else None
    mk_sb = sb(st, nc, "mk", [128, mcols], BF16)
    ones_f = sb(st, nc, "onesf", [1, 64], F32)
    abuf = [sb(st, nc, f"ab{i}", [128, TB], BF16) for i in range(4)]
    rrow = sb(st, nc, "rrow", [1, TB], F32)
    rb_sb = sb(st, nc, "rb", [64, TB], F32)
    s_buf = [sb(st, nc, f"sb{d}", [128, TB], BF16) for d in range(NCH)]

    s_ld = sem(st, nc, "ld")
    s_sc = sem(st, nc, "sc")
    s_exp = sem(st, nc, "exp")
    s_msk = sem(st, nc, "msk")
    s_av = sem(st, nc, "av")
    s_fz = sem(st, nc, "fz")
    s_pj = sem(st, nc, "pj")
    s_vm = sem(st, nc, "vm")

    s_qld = sem(st, nc, "qld")
    qld = [0]
    def load_qJ(Jn):
        for d in range(NCH):
            p.y.dma_start(qj_sb[Jn % 2][d][:], io["q"][d, :, Jn*TB:(Jn+1)*TB])\
                .then_inc(s_qld, 16)
            qld[0] += 16
    import os as _os
    NOK = _os.environ.get("ATTN_NO_K"); NOV = _os.environ.get("ATTN_NO_V")
    ld = 0
    load_qJ(0)
    load_qJ(1)
    for d in range(NCH):
        for hf, srct in ((0, io["kA"]), (1, io["kB"])):
            p.y.dma_start(kH_sb[hf][d][:], srct[d]).then_inc(s_ld, 16)
            ld += 16
    p.v.memset(ones_f[:], 1.0).then_inc(s_vm, 1)
    for t in range(32):
        half = 0 if (t % 4) in (0, 3) else 1
        src = io["vA"] if half == 0 else io["vB"]
        lt = tmap(half).index(t)
        p.g.dma_start(v_sb[t][:], src[lt]).then_inc(s_ld, 16)
        ld += 16
    s_wld = sem(st, nc, "wld")
    wldc = [0]
    if WRES:
        for d in range(NCH):
            p.g.dma_start(w_sb[d][:], io["woutT"][d]).then_inc(s_ld, 16); ld += 16
    for d in range(NCH):
        p.g.dma_start(bo_sb[d][:], io["boutcol"][d]).then_inc(s_ld, 16); ld += 16
    if alibi:
        p.g.dma_start(bc_sb[:], io["biascols"]).then_inc(s_ld, 16); ld += 16
    p.g.dma_start(mk_sb[:], io["maskbank"]).then_inc(s_ld, 16); ld += 16
    for e in (p.t, p.s, p.v):
        e.wait_ge(s_ld, ld)
    import os as _os
    if _os.environ.get("ATTN_LOADS_ONLY"):
        p.y.wait_ge(s_ld, ld)
        p.y.wait_ge(s_qld, 256)
        return

    pscope = ExitStack()
    ps_sc = [ps(pscope, nc, f"psc{i}", [128, TB]) for i in range(3)]
    ps_o = ps(pscope, nc, "pso", [65, TB])
    ps_b = ps(pscope, nc, "psb", [64, TB])
    ps_pj = [ps(pscope, nc, f"pspj{i}", [128, TB]) for i in range(2)]
    ev = StagedEvac(p, pscope, st, "es", n_stages=2, dtype=F32)

    C = dict(sc=0, exp=0, msk=0, av=0, fz=0, pj=0, g=0)
    frees_pj = []
    colix = 0
    sched = list(attn_schedule(alibi))
    si = 0
    for J in range(4):
        p.t.wait_ge(s_qld, 128 * (J + 1))
        items = [it for it in sched if it[0] == J]
        for (J_, h, qb, j0, off, KT, NM) in items:
            hp, hh = h // 2, h % 2
            rows = slice(hh * 64, hh * 64 + 64)
            qcols = slice(off - J * 512, off - J * 512 + qb)
            for kt in range(KT):
                g = C["g"]
                bank = ps_sc[g % 3]
                if g >= 3:
                    p.t.wait_ge(s_exp, g - 2)
                khalf = 0 if (kt % 4) in (0, 3) else 1
                klt = tmap(khalf).index(kt)
                p.t.matmul(bank[:, :qb], kH_sb[khalf][hp][rows, klt*128:(klt+1)*128],
                           qj_sb[J % 2][hp][rows, qcols], start=True, stop=True)\
                    .then_inc(s_sc, 1)
                C["sc"] += 1
                ab = abuf[g % 4]
                p.s.wait_ge(s_sc, C["sc"])
                if g >= 4:
                    p.s.wait_ge(s_av, g - 3)
                bias = bc_sb[:, colix:colix+1] if alibi else 0.0
                p.s.activation(ab[:, :qb], bank[:, :qb], AF.Exp, bias=bias)\
                    .then_inc(s_exp, 1)
                C["exp"] += 1
                colix += 1
                masked = kt >= KT - NM
                if masked:
                    rel = kt - (KT - NM)
                    par = j0 % 2 if qb <= 128 else 0
                    h64 = (off % 128) // 64 if qb == 64 else 0
                    mo = moffs[(qb, rel, par, h64)]
                    p.v.wait_ge(s_exp, C["exp"])
                    p.v.tensor_tensor(ab[:, :qb], ab[:, :qb],
                                      mk_sb[:, mo:mo+qb], OP.mult)\
                        .then_inc(s_msk, 1)
                    C["msk"] += 1
                if masked:
                    p.t.wait_ge(s_msk, C["msk"])
                else:
                    p.t.wait_ge(s_exp, C["exp"])
                if kt == 0 and si > 0:
                    p.t.wait_ge(s_fz, 3 * si)
                p.t.matmul(ps_o[:, :qb], v_sb[kt][:, h*65:(h+1)*65],
                           ab[:, :qb], start=(kt == 0), stop=(kt == KT - 1))\
                    .then_inc(s_av, 1)
                C["av"] += 1
                C["g"] += 1
            p.v.wait_ge(s_av, C["av"])
            p.v.reciprocal(rrow[:, :qb], ps_o[64:65, :qb]).then_inc(s_fz, 1)
            C["fz"] += 1
            p.t.wait_ge(s_fz, C["fz"])
            if si > 0:
                p.t.wait_ge(s_fz, 3 * si)
            p.t.matmul(ps_b[:, :qb], ones_f[:], rrow[:, :qb], start=True,
                       stop=True).then_inc(s_sc, 1)
            C["sc"] += 1
            p.v.wait_ge(s_sc, C["sc"])
            p.v.tensor_copy(rb_sb[:, :qb], ps_b[:, :qb]).then_inc(s_fz, 1)
            C["fz"] += 1
            scols = slice(off - J * 512, off - J * 512 + qb)
            p.v.tensor_tensor(s_buf[hp][rows, scols], ps_o[0:64, :qb],
                              rb_sb[:, :qb], OP.mult).then_inc(s_fz, 1)
            C["fz"] += 1
            si += 1
        p.t.wait_ge(s_fz, 3 * si)
        for oc in range(NCH):
            bank = ps_pj[oc % 2]
            gp = J * NCH + oc
            if not WRES:
                if gp >= 2:
                    p.g.wait_ge(s_pj, gp - 1)
                for e in range(NCH):
                    p.g.dma_start(wsl[gp % 2][e][:],
                                  io["woutT"][e, :, oc*128:(oc+1)*128]).then_inc(s_wld, 16)
                    wldc[0] += 16
            if gp >= 2:
                fs, fv = frees_pj[gp - 2]
                p.t.wait_ge(fs, fv)
            if not WRES:
                p.t.wait_ge(s_wld, wldc[0])
            for e in range(NCH):
                lhsw = w_sb[e][:, oc*128:(oc+1)*128] if WRES else wsl[gp % 2][e][:]
                hmm = p.t.matmul(bank[:], lhsw,
                                 s_buf[e][:], start=(e == 0), stop=(e == NCH - 1))
            hmm.then_inc(s_pj, 1)
            C["pj"] += 1
            frees_pj.append(ev(bank[:], s_pj, C["pj"], io["o_s"][oc, :, J*512:(J+1)*512],
                               bias=bo_sb[oc][:]))
        p.v.wait_ge(s_pj, C["pj"])
        if J + 2 < 4:
            p.y.wait_ge(s_pj, C["pj"])
            load_qJ(J + 2)
    ev.flush_wait()
    pscope.close()


def emit_moe0(p, st, nc, io):
    NE = 14
    xnbf_sb = [sb(st, nc, f"xnbf{d}", [128, NTOK], BF16) for d in range(NCH)]
    gacc = [sb(st, nc, f"ga{d}", [128, NTOK], F32) for d in range(NCH)]
    cf_sb = sb(st, nc, "cfsb", [NE, NTOK], F32)
    b1_sb = sb(st, nc, "b1sb", [128, NE * 16], F32)
    b2_sb = sb(st, nc, "b2sb", [NE, D], F32)
    ones_f = sb(st, nc, "onesf", [1, 128], F32)
    sel_sb = sb(st, nc, "selsb", [14, 128 * 14], F32)
    cb_sb = sb(st, nc, "cb", [128, TB], F32)
    tmp_sb = sb(st, nc, "tmp", [128, TB], F32)
    s_ld = sem(st, nc, "ld")
    s_w = sem(st, nc, "w")
    s_mm = sem(st, nc, "mm")
    s_ev = sem(st, nc, "ev")
    s_acc = sem(st, nc, "acc")
    s_cb = sem(st, nc, "cbk")
    p.v.memset(ones_f[:], 1.0)
    ld = 0
    for d in range(NCH):
        p.y.dma_start(xnbf_sb[d][:], io["xnbf"][d]).then_inc(s_ld, 16); ld += 16
    p.y.dma_start(cf_sb[:], io["coeffT"]).then_inc(s_ld, 16); ld += 16
    p.y.dma_start(b1_sb[:], io["b1c"]).then_inc(s_ld, 16); ld += 16
    p.y.dma_start(b2_sb[:], io["b2r"]).then_inc(s_ld, 16); ld += 16
    p.y.dma_start(sel_sb[:], io["sel"]).then_inc(s_ld, 16); ld += 16
    for e in (p.t, p.s, p.v):
        e.wait_ge(s_ld, ld)

    iscope = ExitStack()
    xnb = [sb(iscope, nc, f"xnb{i}", [128, NTOK], F32, side="right") for i in range(2)]
    with ExitStack() as sti:
        ps_i = [ps(sti, nc, f"pi{i}", [128, TB]) for i in range(4)]
        g = 0
        for d in range(NCH):
            if d >= 2:
                p.y.wait_ge(s_acc, (d - 1) * NBLK)
            p.y.dma_start(xnb[d % 2][:], io["xn"][d]).then_inc(s_ld, 16)
            ld += 16
            p.v.wait_ge(s_ld, ld)
            for blk in range(NBLK):
                bank = ps_i[g % 4]
                if g >= 4:
                    p.t.wait_ge(s_acc, g - 3)
                p.t.matmul(bank[:], b2_sb[:, d*128:(d+1)*128], cf_sb[:, cols(blk)],
                           start=True, stop=True).then_inc(s_mm, 1)
                p.v.wait_ge(s_mm, g + 1)
                p.v.tensor_tensor(gacc[d][:, cols(blk)], xnb[d % 2][:, cols(blk)],
                                  bank[:], OP.add).then_inc(s_acc, 1)
                g += 1
        NB_INIT = g
    iscope.close()

    wscope = ExitStack()
    w1_sb = [sb(wscope, nc, f"w1_{d}", [128, 2048], BF16) for d in range(NCH)]
    w2_sb = [sb(wscope, nc, f"w2_{f}", [128, D], BF16) for f in range(16)]
    h_sb = [sb(wscope, nc, f"h{f}", [128, TB], BF16) for f in range(16)]
    ps_h = [ps(wscope, nc, f"ph{i}", [128, TB]) for i in range(2)]
    ps_o = [ps(wscope, nc, f"po{i}", [128, TB]) for i in range(2)]
    ps_c = ps(wscope, nc, "pc", [128, TB])
    wld = [0]; ghc = [0]; omc = [0]; acc2 = [NB_INIT]; cbc = [0]; mmc = [NB_INIT]

    for e in range(NE):
        p.g.wait_ge(s_ev, 64 * e)
        if e == 0:
            p.g.wait_ge(s_acc, NB_INIT)
        for d in range(NCH):
            p.g.dma_start(w1_sb[d][:], io["w1"][e, d]).then_inc(s_w, 16); wld[0] += 16
        w1done = wld[0]
        p.g.wait_ge(s_acc, NB_INIT + e * 32)
        for f in range(16):
            p.g.dma_start(w2_sb[f][:], io["w2"][e, f]).then_inc(s_w, 16); wld[0] += 16
        walldone = wld[0]
        for tb in range(NBLK):
            if cbc[0] > 0:
                p.t.wait_ge(s_cb, 2 * cbc[0])
            p.t.matmul(ps_c[:], sel_sb[:, e*128:(e+1)*128], cf_sb[:, cols(tb)],
                       start=True, stop=True).then_inc(s_mm, 1)
            mmc[0] += 1
            p.v.wait_ge(s_mm, mmc[0])
            p.v.tensor_copy(cb_sb[:], ps_c[:]).then_inc(s_cb, 1)
            for fc in range(16):
                bank = ps_h[fc % 2]
                gh = ghc[0]
                if fc == 0 and tb == 0:
                    p.t.wait_ge(s_w, w1done)
                if gh >= 2:
                    p.t.wait_ge(s_ev, gh - 1)
                for d in range(NCH):
                    h = p.t.matmul(bank[:], w1_sb[d][:, fc*128:(fc+1)*128],
                                   xnbf_sb[d][:, cols(tb)],
                                   start=(d == 0), stop=(d == NCH - 1))
                h.then_inc(s_mm, 1)
                mmc[0] += 1
                p.s.wait_ge(s_mm, mmc[0])
                p.s.activation(h_sb[fc][:], bank[:], AF.Gelu,
                               bias=b1_sb[:, e*16+fc:e*16+fc+1]).then_inc(s_ev, 1)
                ghc[0] += 1
            for dd in range(NCH):
                bank = ps_o[dd % 2]
                om = omc[0]
                if dd == 0:
                    p.t.wait_ge(s_ev, ghc[0])
                    if tb == 0:
                        p.t.wait_ge(s_w, walldone)
                if om >= 2:
                    p.t.wait_ge(s_acc, acc2[0] - 1)
                for f in range(16):
                    h = p.t.matmul(bank[:], w2_sb[f][:, dd*128:(dd+1)*128],
                                   h_sb[f][:], start=(f == 0), stop=(f == 15))
                h.then_inc(s_mm, 1)
                mmc[0] += 1
                p.v.wait_ge(s_mm, mmc[0])
                p.v.tensor_tensor(tmp_sb[:], bank[:], cb_sb[:], OP.mult)
                p.v.tensor_tensor(gacc[dd][:, cols(tb)], gacc[dd][:, cols(tb)],
                                  tmp_sb[:], OP.add).then_inc(s_acc, 1)
                acc2[0] += 1
                omc[0] += 1
            p.v.nop().then_inc(s_cb, 1)
            cbc[0] += 1
    wscope.close()

    fscope = ExitStack()
    xb = [sb(fscope, nc, f"xb{i}", [128, NTOK], F32, side="right") for i in range(2)]
    sb_ = [sb(fscope, nc, f"sbb{i}", [128, NTOK], F32, side="right") for i in range(2)]
    gb = sb(fscope, nc, "gb", [128, NTOK], F32, side="right")
    x1b = [sb(fscope, nc, f"x1b{i}", [128, NTOK], F32, side="right") for i in range(2)]
    echo = sb(fscope, nc, "echo", [128, 1], F32, side="right")
    s_clip = sem(st, nc, "clip")
    s_silu = sem(st, nc, "silu")
    s_x1 = sem(st, nc, "x1")
    s_od = sem(st, nc, "od")
    p.y.wait_ge(s_acc, acc2[0])
    for d in range(NCH):
        if d >= 2:
            p.y.wait_ge(s_x1, d - 1)
        p.y.dma_start(xb[d % 2][:], io["xT"][d]).then_inc(s_ld, 16)
        p.y.dma_start(sb_[d % 2][:], io["s0"][d]).then_inc(s_ld, 16)
        ld += 32
        p.v.wait_ge(s_ld, ld)
        p.v.tensor_scalar(gacc[d][:], gacc[d][:], 100.0, -100.0, OP.min, OP.max)\
            .then_inc(s_clip, 1)
        p.s.wait_ge(s_clip, d + 1)
        if d >= 1:
            p.s.wait_ge(s_x1, d)
        p.s.activation(gb[:], gacc[d][:], AF.Silu).then_inc(s_silu, 1)
        p.v.wait_ge(s_silu, d + 1)
        if d >= 2:
            p.v.wait_ge(s_od, 16 * (d - 1))
        p.v.tensor_tensor(gb[:], gb[:], sb_[d % 2][:], OP.mult)
        p.v.tensor_tensor(x1b[d % 2][:], gb[:], xb[d % 2][:], OP.add)
        p.v.tensor_copy(echo[:], x1b[d % 2][:, NTOK-1:NTOK]).then_inc(s_x1, 1)
        p.y.wait_ge(s_x1, d + 1)
        p.y.dma_start(io["o_x1"][d], x1b[d % 2][:]).then_inc(s_od, 16)
    p.y.wait_ge(s_od, 16 * NCH)
    fscope.close()


def emit_exchange(p, st, nc, io):
    xscope = ExitStack()
    x1bf = [sb(xscope, nc, f"x1bf{d}", [128, NTOK], BF16) for d in range(NCH)]
    stg = [sb(xscope, nc, f"stg{i}", [128, NTOK], BF16) for i in range(2)]
    acc = sb(xscope, nc, "xacc", [128, NTOK], F32)
    msl = sb(xscope, nc, "msl", [128, 8], F32)
    mpr = sb(xscope, nc, "mpr", [128, 8], F32)
    s_ld = sem(st, nc, "ld")
    s_st = sem(st, nc, "stg")
    s_cc = sem(st, nc, "cc")
    s_rd = sem(st, nc, "rd")
    ld = 0
    p.g.dma_start(msl[:], io["mslot"]).then_inc(s_ld, 16); ld += 16
    p.g.dma_start(mpr[:], io["mpair"]).then_inc(s_ld, 16); ld += 16
    for d in range(NCH):
        p.g.dma_start(x1bf[d][:], io["x1"][d]).then_inc(s_ld, 16); ld += 16
    p.v.wait_ge(s_ld, ld)
    nst = 0
    for s in range(8):
        for d in range(NCH):
            p.v.wait_ge(s_st, 0 if nst < 2 else 16 * (nst - 1))
            p.v.tensor_scalar(stg[nst % 2][:], x1bf[d][:], msl[:, s:s+1], None,
                              OP.mult).then_inc(s_cc, 1)
            nst += 1
            p.y.wait_ge(s_cc, nst)
            p.y.dma_start(io["arin"][s, d], stg[(nst - 1) % 2][:]).then_inc(s_st, 16)
    p.g.wait_ge(s_st, 16 * nst)
    p.g.collective_compute("AllReduce", mybir.AluOpType.add,
                           replica_groups=[list(range(8))],
                           ins=[io["arin_t"].ap().opt()],
                           outs=[io["arout_t"].ap().opt()]).then_inc(s_cc, 16)
    p.g.wait_ge(s_cc, nst + 16)
    s_cs = sem(st, nc, "cs")
    nrd = 0
    for d in range(NCH):
        for s in range(8):
            if nrd >= 2:
                p.g.wait_ge(s_cs, nrd - 1)
            p.g.dma_start(stg[s % 2][:], io["arout"][s, d]).then_inc(s_ld, 16)
            ld += 16
            p.v.wait_ge(s_ld, ld)
            if s == 0:
                h = p.v.tensor_scalar(acc[:], stg[s % 2][:], mpr[:, s:s+1], None,
                                      OP.mult)
            else:
                h = p.v.scalar_tensor_tensor(acc[:], stg[s % 2][:], mpr[:, s:s+1],
                                             acc[:], OP.mult, OP.add)
            h.then_inc(s_cs, 1)
            if s == 7:
                p.v.tensor_copy(x1bf[d][:], acc[:]).then_inc(s_rd, 1)
            nrd += 1
        p.y.wait_ge(s_rd, d + 1)
        p.y.dma_start(io["o_x1pair"][d], x1bf[d][:]).then_inc(s_st, 16)
        nst += 1
    p.y.wait_ge(s_st, 16 * nst)
    xscope.close()


def emit_combine(p, st, nc, io):
    cscope = ExitStack()
    a = sb(cscope, nc, "ca", [128, NTOK], BF16)
    b = sb(cscope, nc, "cb2", [128, NTOK], BF16)
    o = sb(cscope, nc, "co", [128, NTOK], BF16)
    t1 = sb(cscope, nc, "ct", [128, NTOK], BF16)
    mh = sb(cscope, nc, "cmh", [128, 2], F32)
    s_ld = sem(st, nc, "ld")
    s_wr = sem(st, nc, "wr")
    s_cs = sem(st, nc, "cs")
    p.g.dma_start(mh[:], io["mh"]).then_inc(s_ld, 16)
    ld = [16]
    nio = [0]

    def one(dst, own_ap, pair_ap, mown_col, width):
        if nio[0] >= 1:
            p.g.wait_ge(s_cs, nio[0])
        p.g.dma_start(a[:, :width], own_ap).then_inc(s_ld, 16)
        p.g.dma_start(b[:, :width], pair_ap).then_inc(s_ld, 16)
        ld[0] += 32
        p.v.wait_ge(s_ld, ld[0])
        if nio[0] >= 1:
            p.v.wait_ge(s_wr, 16 * nio[0])
        p.v.tensor_scalar(t1[:, :width], b[:, :width], mh[:, 1-mown_col:2-mown_col],
                          None, OP.mult)
        p.v.scalar_tensor_tensor(o[:, :width], a[:, :width], mh[:, mown_col:mown_col+1],
                                 t1[:, :width], OP.mult, OP.add).then_inc(s_cs, 1)
        nio[0] += 1
        p.v.wait_ge(s_cs, nio[0])
        p.y.wait_ge(s_cs, nio[0])
        p.y.dma_start(dst, o[:, :width]).then_inc(s_wr, 16)

    for d in range(NCH):
        one(io["o_kA"][d], io["k_own"][d], io["k_pair"][d], 0, NTOK)
        one(io["o_kB"][d], io["k_own"][d], io["k_pair"][d], 1, NTOK)
    for t in range(NTILE):
        one(io["o_vA"][t], io["v_own"][t], io["v_pair"][t], 0, D)
        one(io["o_vB"][t], io["v_own"][t], io["v_pair"][t], 1, D)
    p.y.wait_ge(s_wr, 16 * nio[0])


def emit_ffn1(p, st, nc, io, BIS=40):
    NE = 16
    S = 4096
    xnbf_sb = [sb(st, nc, f"xnbf{d}", [128, NTOK], BF16) for d in range(NCH)]
    gacc = [sb(st, nc, f"ga{d}", [128, NTOK], F32) for d in range(NCH)]
    coefP = sb(st, nc, "coefP", [128, S], F32)
    sel_sb = sb(st, nc, "selsb", [NE, 128 * NE], F32)
    b3c_sb = sb(st, nc, "b3csb", [NE, 1], F32)
    ones_f = sb(st, nc, "onesf", [1, 128], F32)
    cb_sb = sb(st, nc, "cb", [128, TB], F32)
    tmp_sb = sb(st, nc, "tmp", [128, TB], F32)
    B1c = sb(st, nc, "B1c", [128, NE], F32)
    B2c = sb(st, nc, "B2c", [128, NE], F32)
    rowb = sb(st, nc, "rowb", [1, TB], F32)
    colw = {nm: sb(st, nc, f"c_{nm}", [128, 1], F32)
            for nm in ("mx", "negmx", "sum", "rsum", "lo", "hi", "u", "t",
                       "cnt", "sel1", "nsel", "dlt")}
    s_ld = sem(st, nc, "ld")
    s_w = sem(st, nc, "w")
    s_mm = sem(st, nc, "mm")
    s_ev = sem(st, nc, "ev")
    s_acc = sem(st, nc, "acc")
    s_cb = sem(st, nc, "cbk")
    s_gate = sem(st, nc, "gate")
    p.v.memset(ones_f[:], 1.0)
    ld = 0
    for d in range(NCH):
        p.y.dma_start(xnbf_sb[d][:], io["xnbf"][d]).then_inc(s_ld, 16); ld += 16
    p.y.dma_start(sel_sb[:], io["sel"]).then_inc(s_ld, 16); ld += 16
    p.y.dma_start(b3c_sb[:], io["b3c"]).then_inc(s_ld, 16); ld += 16

    gscope = ExitStack()
    lv_sb = sb(gscope, nc, "lv", [128, S], F32, side="right")
    pv_sb = sb(gscope, nc, "pv", [128, S], F32, side="right")
    ge_sb = sb(gscope, nc, "ge", [128, S], F32, side="right")
    b1r_sb = sb(gscope, nc, "b1rsb", [1, NE], F32, side="right")
    b2r_sb = sb(gscope, nc, "b2rsb", [1, NE], F32, side="right")
    lgA = sb(gscope, nc, "lgA", [NE, NTOK], F32, side="right")
    lgB = sb(gscope, nc, "lgB", [NE, NTOK], F32, side="right")
    mh2 = sb(gscope, nc, "mh2", [128, 2], F32, side="right")
    lgt = sb(gscope, nc, "lgt", [NE, NTOK], F32, side="right")
    p.v.memset(lv_sb[:], 0.0).then_inc(s_gate, 100)
    p.y.dma_start(lgA[:], io["lgA"]).then_inc(s_ld, 16); ld += 16
    p.y.dma_start(lgB[:], io["lgB"]).then_inc(s_ld, 16); ld += 16
    p.y.dma_start(mh2[:], io["mh"]).then_inc(s_ld, 16); ld += 16
    p.y.dma_start(b1r_sb[:], io["b1r"]).then_inc(s_ld, 16); ld += 16
    p.y.dma_start(b2r_sb[:], io["b2r"]).then_inc(s_ld, 16); ld += 16
    for e in (p.t, p.s, p.v):
        e.wait_ge(s_ld, ld)
    p.v.wait_ge(s_gate, 100)
    p.v.tensor_scalar(lgt[:], lgB[:], mh2[0:NE, 1:2], None, OP.mult)
    p.v.scalar_tensor_tensor(lv_sb[0:NE, 0:NTOK], lgA[:], mh2[0:NE, 0:1],
                             lgt[:], OP.mult, OP.add)
    p.v.tensor_scalar(lgt[:], lgA[:], mh2[0:NE, 1:2], None, OP.mult)
    p.v.scalar_tensor_tensor(lv_sb[0:NE, NTOK:2*NTOK], lgB[:], mh2[0:NE, 0:1],
                             lgt[:], OP.mult, OP.add)
    with ExitStack() as stb:
        ps_bc = [ps(stb, nc, f"pbc{i}", [128, NE]) for i in range(2)]
        p.t.matmul(ps_bc[0][:], ones_f[:], b1r_sb[:], start=True, stop=True)
        p.t.matmul(ps_bc[1][:], ones_f[:], b2r_sb[:], start=True, stop=True)\
            .then_inc(s_mm, 1)
        p.v.wait_ge(s_mm, 1)
        p.v.tensor_copy(B1c[:], ps_bc[0][:])
        p.v.tensor_copy(B2c[:], ps_bc[1][:]).then_inc(s_gate, 1)
    p.v.reduce_max(colw["mx"][:], lv_sb[:], axis=X)
    p.v.tensor_scalar(colw["negmx"][:], colw["mx"][:], -1.0, None, OP.mult)\
        .then_inc(s_gate, 1)
    p.s.wait_ge(s_gate, 102)
    p.s.activation(pv_sb[:], lv_sb[:], AF.Exp, bias=colw["negmx"][:])\
        .then_inc(s_gate, 1)
    p.v.wait_ge(s_gate, 103)
    p.v.reduce_sum(colw["sum"][:], pv_sb[:], axis=X)
    p.v.reciprocal(colw["rsum"][:], colw["sum"][:])
    p.v.tensor_scalar(pv_sb[:], pv_sb[:], colw["rsum"][:], None, OP.mult)
    p.v.memset(colw["lo"][:], 0.0)
    p.v.memset(colw["hi"][:], 1.0)
    for it in range(BIS):
        p.v.tensor_tensor(colw["u"][:], colw["lo"][:], colw["hi"][:], OP.add)
        p.v.tensor_scalar(colw["t"][:], colw["u"][:], 0.5, None, OP.mult)
        p.v.tensor_scalar(ge_sb[:], pv_sb[:], colw["t"][:], None, OP.is_ge)
        p.v.reduce_sum(colw["cnt"][:], ge_sb[:], axis=X)
        p.v.tensor_scalar(colw["sel1"][:], colw["cnt"][:], 256.0, None, OP.is_ge)
        p.v.tensor_scalar(colw["nsel"][:], colw["sel1"][:], -1.0, 1.0, OP.mult, OP.add)
        p.v.tensor_tensor(colw["dlt"][:], colw["t"][:], colw["lo"][:], OP.subtract)
        p.v.tensor_tensor(colw["dlt"][:], colw["dlt"][:], colw["sel1"][:], OP.mult)
        p.v.tensor_tensor(colw["lo"][:], colw["lo"][:], colw["dlt"][:], OP.add)
        p.v.tensor_tensor(colw["dlt"][:], colw["t"][:], colw["hi"][:], OP.subtract)
        p.v.tensor_tensor(colw["dlt"][:], colw["dlt"][:], colw["nsel"][:], OP.mult)
        p.v.tensor_tensor(colw["hi"][:], colw["hi"][:], colw["dlt"][:], OP.add)
    p.v.tensor_scalar(ge_sb[:], pv_sb[:], colw["lo"][:], None, OP.is_ge)
    p.v.tensor_tensor(coefP[:], pv_sb[:], ge_sb[:], OP.mult).then_inc(s_gate, 1)

    with ExitStack() as sti:
        ps_r = ps(sti, nc, "pr", [1, TB])
        ps_b = [ps(sti, nc, f"pb{i}", [128, TB]) for i in range(2)]
        p.t.wait_ge(s_gate, 104)
        acc0 = 0
        for tb in range(NBLK):
            if tb >= 1:
                p.t.wait_ge(s_gate, 104 + tb)
            p.t.matmul(ps_r[:], b3c_sb[:], coefP[0:NE, cols(tb)], start=True,
                       stop=True).then_inc(s_mm, 1)
            p.v.wait_ge(s_mm, 2 + 3 * tb)
            p.v.tensor_copy(rowb[:], ps_r[:]).then_inc(s_gate, 1)
            bank = ps_b[tb % 2]
            p.t.wait_ge(s_gate, 105 + tb)
            if tb >= 2:
                p.t.wait_ge(s_acc, (tb - 1) * NCH)
            p.t.matmul(bank[:], ones_f[:], rowb[:], start=True, stop=True)\
                .then_inc(s_mm, 2)
            p.v.wait_ge(s_mm, 4 + 3 * tb)
            for d in range(NCH):
                p.v.tensor_copy(gacc[d][:, cols(tb)], bank[:]).then_inc(s_acc, 1)
                acc0 += 1
        NB_INIT = acc0
        MM_INIT = 1 + 3 * NBLK
    gscope.close()

    wscope = ExitStack()
    w1_sb = [sb(wscope, nc, f"w1_{d}", [128, D], BF16) for d in range(NCH)]
    w2_sb = [sb(wscope, nc, f"w2_{d}", [128, D], BF16) for d in range(NCH)]
    w3_sb = [sb(wscope, nc, f"w3_{d}", [128, D], BF16) for d in range(NCH)]
    h1 = [sb(wscope, nc, f"h1_{d}", [128, TB], BF16) for d in range(NCH)]
    h2 = [sb(wscope, nc, f"h2_{d}", [128, TB], BF16) for d in range(NCH)]
    ps_h = [ps(wscope, nc, f"ph{i}", [128, TB]) for i in range(2)]
    ps_o = [ps(wscope, nc, f"po{i}", [128, TB]) for i in range(2)]
    ps_c = ps(wscope, nc, "pc", [128, TB])
    wld = [0]; evc = [0]; omc = [0]; accc = [NB_INIT]; cbc = [0]; mmc = [MM_INIT]

    for e in range(NE):
        p.g.wait_ge(s_ev, max(0, 64 * e - 8))
        if e == 0:
            p.g.wait_ge(s_acc, NB_INIT)
        for d in range(NCH):
            p.g.dma_start(w1_sb[d][:], io["w1"][e, d]).then_inc(s_w, 16); wld[0] += 16
        w1done = wld[0]
        p.g.wait_ge(s_ev, 64 * e)
        for d in range(NCH):
            p.g.dma_start(w2_sb[d][:], io["w2"][e, d]).then_inc(s_w, 16); wld[0] += 16
        w2done = wld[0]
        p.g.wait_ge(s_acc, NB_INIT + e * 32)
        for d in range(NCH):
            p.g.dma_start(w3_sb[d][:], io["w3"][e, d]).then_inc(s_w, 16); wld[0] += 16
        walldone = wld[0]
        for tb in range(NBLK):
            if cbc[0] > 0:
                p.t.wait_ge(s_cb, 2 * cbc[0])
            p.t.matmul(ps_c[:], sel_sb[:, e*128:(e+1)*128], coefP[0:NE, cols(tb)],
                       start=True, stop=True).then_inc(s_mm, 1)
            mmc[0] += 1
            p.v.wait_ge(s_mm, mmc[0])
            p.v.tensor_copy(cb_sb[:], ps_c[:]).then_inc(s_cb, 1)
            for stage, wbuf, srcb, dst, bcol in (
                    (1, w1_sb, None, h1, B1c), (2, w2_sb, h1, h2, B2c)):
                for oc in range(NCH):
                    bank = ps_h[oc % 2]
                    gh = evc[0]
                    if oc == 0 and tb == 0:
                        p.t.wait_ge(s_w, w1done if stage == 1 else w2done)
                    if stage == 2 and oc == 0:
                        p.t.wait_ge(s_ev, evc[0])
                    elif gh >= 2:
                        p.t.wait_ge(s_ev, gh - 1)
                    for d in range(NCH):
                        rhs = xnbf_sb[d][:, cols(tb)] if stage == 1 else srcb[d][:]
                        h = p.t.matmul(bank[:], wbuf[d][:, oc*128:(oc+1)*128],
                                       rhs, start=(d == 0), stop=(d == NCH - 1))
                    h.then_inc(s_mm, 1)
                    mmc[0] += 1
                    p.s.wait_ge(s_mm, mmc[0])
                    p.s.activation(dst[oc][:], bank[:], AF.Relu,
                                   bias=bcol[:, e:e+1]).then_inc(s_ev, 1)
                    evc[0] += 1
            for dd in range(NCH):
                bank = ps_o[dd % 2]
                om = omc[0]
                if dd == 0:
                    p.t.wait_ge(s_ev, evc[0])
                    if tb == 0:
                        p.t.wait_ge(s_w, walldone)
                if om >= 2:
                    p.t.wait_ge(s_acc, accc[0] - 1)
                for f in range(NCH):
                    h = p.t.matmul(bank[:], w3_sb[f][:, dd*128:(dd+1)*128],
                                   h2[f][:], start=(f == 0), stop=(f == NCH - 1))
                h.then_inc(s_mm, 1)
                mmc[0] += 1
                p.v.wait_ge(s_mm, mmc[0])
                p.v.tensor_tensor(tmp_sb[:], bank[:], cb_sb[:], OP.mult)
                p.v.tensor_tensor(gacc[dd][:, cols(tb)], gacc[dd][:, cols(tb)],
                                  tmp_sb[:], OP.add).then_inc(s_acc, 1)
                accc[0] += 1
                omc[0] += 1
            p.v.nop().then_inc(s_cb, 1)
            cbc[0] += 1
    wscope.close()

    fscope = ExitStack()
    xb = [sb(fscope, nc, f"xb{i}", [128, NTOK], F32, side="right") for i in range(2)]
    sb_ = [sb(fscope, nc, f"sbb{i}", [128, NTOK], F32, side="right") for i in range(2)]
    gb = sb(fscope, nc, "gb", [128, NTOK], F32, side="right")
    x1b = [sb(fscope, nc, f"x1b{i}", [128, NTOK], F32, side="right") for i in range(2)]
    echo = sb(fscope, nc, "echo", [128, 1], F32, side="right")
    s_silu = sem(st, nc, "silu")
    s_x1 = sem(st, nc, "x1")
    s_od = sem(st, nc, "od")
    p.y.wait_ge(s_acc, accc[0])
    for d in range(NCH):
        if d >= 2:
            p.y.wait_ge(s_x1, d - 1)
        p.y.dma_start(xb[d % 2][:], io["x1T"][d]).then_inc(s_ld, 16)
        p.y.dma_start(sb_[d % 2][:], io["s1"][d]).then_inc(s_ld, 16)
        ld += 32
        p.s.wait_ge(s_ld, ld)
        if d >= 1:
            p.s.wait_ge(s_x1, d)
        p.s.activation(gb[:], gacc[d][:], AF.Silu).then_inc(s_silu, 1)
        p.v.wait_ge(s_silu, d + 1)
        p.v.wait_ge(s_ld, ld)
        if d >= 2:
            p.v.wait_ge(s_od, 16 * (d - 1))
        p.v.tensor_tensor(gb[:], gb[:], sb_[d % 2][:], OP.mult)
        p.v.tensor_tensor(x1b[d % 2][:], gb[:], xb[d % 2][:], OP.add)
        p.v.tensor_copy(echo[:], x1b[d % 2][:, NTOK-1:NTOK]).then_inc(s_x1, 1)
        p.y.wait_ge(s_x1, d + 1)
        p.y.dma_start(io["o_x2"][d], x1b[d % 2][:]).then_inc(s_od, 16)
    p.y.wait_ge(s_od, 16 * NCH)
    fscope.close()


class PhaseStack:
    def __init__(self, launch_stack):
        self.mem = ExitStack()
        self.launch = launch_stack
    def enter_context(self, cm):
        if type(cm).__name__ == "_GeneratorContextManager":
            return self.launch.enter_context(cm)
        return self.mem.enter_context(cm)
    def close(self, p):
        self.mem.close()


def build_l0():
    nc = bass.Bass("TRN2", target_bir_lowering=False)
    E = lambda n, s, dt: nc.dram_tensor(n, s, dt, kind="ExternalInput").ap()
    I = lambda n, s, dt: nc.dram_tensor(n, s, dt)
    NE0 = 14
    _, mc0 = mask_offsets(False)
    ins = dict(
        x_own=E("x_own", [NCH, 128, NTOK], F32),
        x_absA=E("x_absA", [NCH, 128, NTOK], F32),
        x_absB=E("x_absB", [NCH, 128, NTOK], F32),
        ident=E("ident", [128, 128], F32),
        wqkvT0=E("wqkvT0", [NCH, 128, 3072], F32),
        gwT0=E("gwT0", [NCH, 128, NE0], F32),
        gcol=E("gcol", [NCH, 128, 1], F32),
        bcol=E("bcol", [NCH, 128, 1], F32),
        wout0=E("wout0", [NCH, 128, D], F32),
        bout0=E("bout0", [NCH, 128, 1], F32),
        mask0=E("mask0", [128, mc0], BF16),
        w1m=E("w1m", [NE0, NCH, 128, 2048], F32),
        b1c=E("b1c", [128, NE0 * 16], F32),
        w2m=E("w2m", [NE0, 16, 128, D], F32),
        b2r=E("b2r", [NE0, D], F32),
        sel14=E("sel14", [14, 128 * 14], F32),
    )
    o_x1 = nc.dram_tensor("o_x1", [NCH, 128, NTOK], F32, kind="ExternalOutput").ap()
    xn0 = I("xn0", [NCH, 128, NTOK], F32)
    xnbf0 = I("xnbf0", [NCH, 128, NTOK], BF16)
    q0 = I("q0", [NCH, 128, NTOK], BF16)
    gate0 = I("gate0", [NE0, NTOK], F32)
    kA0 = I("kA0", [NCH, 128, NTOK], BF16)
    kB0 = I("kB0", [NCH, 128, NTOK], BF16)
    vA0 = I("vA0", [NTILE, 128, 1040], BF16)
    vB0 = I("vB0", [NTILE, 128, 1040], BF16)
    s0 = I("s0", [NCH, 128, NTOK], F32)

    p = Plan(nc)
    gstack = ExitStack()
    s_bar = gstack.enter_context(nc.semaphore("sbar"))
    barn = [0]

    def phase(fn):
        stp = PhaseStack(gstack)
        fn(stp)
        barn[0] += 1
        p.y.nop().then_inc(s_bar, 1)
        for e in (p.t, p.s, p.v, p.g, p.y):
            e.wait_ge(s_bar, barn[0])
        stp.close(p)

    A = lambda t: t.ap()
    phase(lambda stp: emit_prep(p, stp, nc, 0, dict(
        xT=ins["x_own"], wqkvT=ins["wqkvT0"], gwT=ins["gwT0"], ident=ins["ident"],
        gcol=ins["gcol"], bcol=ins["bcol"], o_xn=A(xn0), o_xnbf=A(xnbf0),
        o_q=A(q0), o_gate=A(gate0)), do_main=True, do_kv=False))
    phase(lambda stp: emit_prep(p, stp, nc, 0, dict(
        xT=ins["x_absA"], wqkvT=ins["wqkvT0"], gwT=ins["gwT0"], ident=ins["ident"],
        gcol=ins["gcol"], bcol=ins["bcol"], o_k=A(kA0), o_v=A(vA0)),
        do_main=False, do_kv=True))
    phase(lambda stp: emit_prep(p, stp, nc, 0, dict(
        xT=ins["x_absB"], wqkvT=ins["wqkvT0"], gwT=ins["gwT0"], ident=ins["ident"],
        gcol=ins["gcol"], bcol=ins["bcol"], o_k=A(kB0), o_v=A(vB0)),
        do_main=False, do_kv=True))
    phase(lambda stp: emit_attn(p, stp, nc, 0, dict(
        q=A(q0), kA=A(kA0), kB=A(kB0), vA=A(vA0), vB=A(vB0),
        woutT=ins["wout0"], boutcol=ins["bout0"], maskbank=ins["mask0"],
        o_s=A(s0))))
    phase(lambda stp: emit_moe0(p, stp, nc, dict(
        xnbf=A(xnbf0), xn=A(xn0), coeffT=A(gate0), w1=ins["w1m"], b1c=ins["b1c"],
        w2=ins["w2m"], b2r=ins["b2r"], sel=ins["sel14"], xT=ins["x_own"],
        s0=A(s0), o_x1=o_x1)))
    p.build()
    gstack.close()
    return nc


def build_l1():
    nc = bass.Bass("TRN2", target_bir_lowering=False)
    E = lambda n, s, dt: nc.dram_tensor(n, s, dt, kind="ExternalInput").ap()
    I = lambda n, s, dt: nc.dram_tensor(n, s, dt)
    NE1 = 16
    _, mc1 = mask_offsets(True)
    nb1 = sum(KT for (_, _, _, _, _, KT, _) in attn_schedule(True))
    ins = dict(
        x_own=E("x_own", [NCH, 128, NTOK], F32),
        x_absA=E("x_absA", [NCH, 128, NTOK], F32),
        x_absB=E("x_absB", [NCH, 128, NTOK], F32),
        ident=E("ident", [128, 128], F32),
        wqkvT1=E("wqkvT1", [NCH, 128, 3072], F32),
        gwT1=E("gwT1", [NCH, 128, NE1], F32),
        bgc=E("bgc", [NE1, 1], F32),
        wout1=E("wout1", [NCH, 128, D], F32),
        bout1=E("bout1", [NCH, 128, 1], F32),
        mask1=E("mask1", [128, mc1], BF16),
        bias1=E("bias1", [128, nb1], F32),
        mh=E("mh", [128, 2], F32),
        w1f=E("w1f", [NE1, NCH, 128, D], F32),
        w2f=E("w2f", [NE1, NCH, 128, D], F32),
        w3f=E("w3f", [NE1, NCH, 128, D], F32),
        b1r1=E("b1r1", [1, NE1], F32),
        b2r1=E("b2r1", [1, NE1], F32),
        b3c1=E("b3c1", [NE1, 1], F32),
        sel16=E("sel16", [NE1, 128 * NE1], F32),
    )
    o_x2 = nc.dram_tensor("o_x2", [NCH, 128, NTOK], F32, kind="ExternalOutput").ap()
    xnbf1 = I("xnbf1", [NCH, 128, NTOK], BF16)
    q1 = I("q1", [NCH, 128, NTOK], BF16)
    lgA = I("lgA", [NE1, NTOK], F32)
    lgB = I("lgB", [NE1, NTOK], F32)
    kA1 = I("kA1", [NCH, 128, NTOK], BF16)
    kB1 = I("kB1", [NCH, 128, NTOK], BF16)
    vA1 = I("vA1", [NTILE, 128, 1040], BF16)
    vB1 = I("vB1", [NTILE, 128, 1040], BF16)
    s1 = I("s1", [NCH, 128, NTOK], F32)

    p = Plan(nc)
    gstack = ExitStack()
    s_bar = gstack.enter_context(nc.semaphore("sbar"))
    barn = [0]

    def phase(fn):
        stp = PhaseStack(gstack)
        fn(stp)
        barn[0] += 1
        p.y.nop().then_inc(s_bar, 1)
        for e in (p.t, p.s, p.v, p.g, p.y):
            e.wait_ge(s_bar, barn[0])
        stp.close(p)

    A = lambda t: t.ap()
    phase(lambda stp: emit_prep(p, stp, nc, 1, dict(
        xT=ins["x_own"], wqkvT=ins["wqkvT1"], gwT=ins["gwT1"], ident=ins["ident"],
        bgc=ins["bgc"], o_xnbf=A(xnbf1), o_q=A(q1), o_gate=None),
        do_main=True, do_kv=False, do_gate=False))
    phase(lambda stp: emit_prep(p, stp, nc, 1, dict(
        xT=ins["x_absA"], wqkvT=ins["wqkvT1"], gwT=ins["gwT1"], ident=ins["ident"],
        bgc=ins["bgc"], o_k=A(kA1), o_v=A(vA1), o_gate=A(lgA)),
        do_main=False, do_kv=True, do_gate=True))
    phase(lambda stp: emit_prep(p, stp, nc, 1, dict(
        xT=ins["x_absB"], wqkvT=ins["wqkvT1"], gwT=ins["gwT1"], ident=ins["ident"],
        bgc=ins["bgc"], o_k=A(kB1), o_v=A(vB1), o_gate=A(lgB)),
        do_main=False, do_kv=True, do_gate=True))
    phase(lambda stp: emit_attn(p, stp, nc, 1, dict(
        q=A(q1), kA=A(kA1), kB=A(kB1), vA=A(vA1), vB=A(vB1),
        woutT=ins["wout1"], boutcol=ins["bout1"], maskbank=ins["mask1"],
        biascols=ins["bias1"], o_s=A(s1))))
    phase(lambda stp: emit_ffn1(p, stp, nc, dict(
        xnbf=A(xnbf1), lgA=A(lgA), lgB=A(lgB), mh=ins["mh"],
        w1=ins["w1f"], w2=ins["w2f"], w3=ins["w3f"],
        b1r=ins["b1r1"], b2r=ins["b2r1"], b3c=ins["b3c1"],
        sel=ins["sel16"], x1T=ins["x_own"], s1=A(s1), o_x2=o_x2)))
    p.build()
    gstack.close()
    return nc


from concourse.bass_utils import run_bass_kernel_spmd
import ml_dtypes


def _alibi_biascols(half):
    ncols = sum(KT for (_, _, _, _, _, KT, _) in attn_schedule(True))
    biascols = np.zeros((ncols, 128), np.float32)
    i = 0
    for (J, h, qb, j0, off, KT, NM) in attn_schedule(True):
        lts = range(j0, j0 + max(qb // 128, 1))
        if qb == 64:
            a_ = 2 * j0 + ((j0 % 2) if half == 0 else (1 - j0 % 2))
            h64 = (off % 128) // 64
            maxpos = a_ * 128 + h64 * 64 + 63
        else:
            maxpos = max(2 * lt + (((lt % 2) if half == 0 else (1 - lt % 2)))
                         for lt in lts) * 128 + 127
        q_end = maxpos + 1
        own_max_tile = maxpos // 128
        sl = ALIBI_SLOPES[h]
        for kt in range(KT):
            if kt > own_max_tile:
                biascols[i] = -60000.0
            else:
                biascols[i] = sl * ((kt * 128 + np.arange(128)) - q_end)
            i += 1
    return np.ascontiguousarray(biascols.T)


def kernel(**d):
    cores = list(range(8))
    f32 = np.float32
    ac = np.ascontiguousarray
    x0 = np.asarray(d["L0_attn_x"], dtype=f32)
    eye128 = np.eye(128, dtype=f32)
    idxh = [own_token_idx(0), own_token_idx(1)]

    sh0 = dict(
        ident=eye128,
        wqkvT0=ac(d["L0_attn_w_qkv"].T.astype(f32)).reshape(NCH, 128, 3072),
        gwT0=ac(d["L0_ffn_gate_w"].T.astype(f32)).reshape(NCH, 128, 14),
        gcol=ac(d["L0_attn_ln_gamma"].astype(f32)).reshape(NCH, 128, 1),
        bcol=ac(d["L0_attn_ln_beta"].astype(f32)).reshape(NCH, 128, 1),
        wout0=ac(d["L0_attn_w_out"].T.astype(f32)).reshape(NCH, 128, D),
        bout0=ac(d["L0_attn_b_out"].astype(f32)).reshape(NCH, 128, 1),
        w1m=ac(d["L0_ffn_w1"].astype(f32)).reshape(14, 8, 128, 2048),
        b1c=ac(d["L0_ffn_b1"].astype(f32).reshape(14 * 16, 128).T),
        w2m=ac(d["L0_ffn_w2"].astype(f32)).reshape(14, 16, 128, 1024),
        b2r=ac(d["L0_ffn_b2"].astype(f32)),
        sel14=ac(np.repeat(np.eye(14, dtype=f32), 128, axis=1)),
    )
    maskb = [None, None]
    for half in (0, 1):
        mb0, _ = mask_patterns(half, False)
        maskb[half] = ac(mb0).astype(ml_dtypes.bfloat16)
    in_maps = []
    for c in cores:
        b, half = c // 2, c % 2
        m = dict(sh0)
        m["x_own"] = ac(x0[b][idxh[half]].T).reshape(NCH, 128, NTOK)
        m["x_absA"] = ac(x0[b][idxh[0]].T).reshape(NCH, 128, NTOK)
        m["x_absB"] = ac(x0[b][idxh[1]].T).reshape(NCH, 128, NTOK)
        m["mask0"] = maskb[half]
        in_maps.append(m)
    nc0 = build_l0()
    r0 = run_bass_kernel_spmd(nc0, in_maps, core_ids=cores).results

    x1 = np.empty((4, 4096, 1024), f32)
    for c in cores:
        b, half = c // 2, c % 2
        x1[b][idxh[half]] = r0[c]["o_x1"].reshape(1024, NTOK).T

    sh1 = dict(
        ident=eye128,
        wqkvT1=ac(d["L1_attn_W_qkv"].astype(f32)).reshape(NCH, 128, 3072),
        gwT1=ac(d["L1_ffn_Wg"].astype(f32)).reshape(NCH, 128, 16),
        bgc=ac(d["L1_ffn_bg"].astype(f32)).reshape(16, 1),
        wout1=ac(d["L1_attn_W_out"].astype(f32)).reshape(NCH, 128, D),
        bout1=np.zeros((NCH, 128, 1), f32),
        w1f=ac(d["L1_ffn_w1"].astype(f32)).reshape(16, 8, 128, 1024),
        w2f=ac(d["L1_ffn_w2"].astype(f32)).reshape(16, 8, 128, 1024),
        w3f=ac(d["L1_ffn_w3"].astype(f32)).reshape(16, 8, 128, 1024),
        b1r1=ac(d["L1_ffn_b1"].astype(f32).reshape(1, 16)),
        b2r1=ac(d["L1_ffn_b2"].astype(f32).reshape(1, 16)),
        b3c1=ac(d["L1_ffn_b3"].astype(f32).reshape(16, 1)),
        sel16=ac(np.repeat(np.eye(16, dtype=f32), 128, axis=1)),
    )
    maskb1 = [None, None]
    bias1 = [None, None]
    for half in (0, 1):
        mb1, _ = mask_patterns(half, True)
        maskb1[half] = ac(mb1).astype(ml_dtypes.bfloat16)
        bias1[half] = _alibi_biascols(half)
    in_maps = []
    for c in cores:
        b, half = c // 2, c % 2
        m = dict(sh1)
        m["x_own"] = ac(x1[b][idxh[half]].T).reshape(NCH, 128, NTOK)
        m["x_absA"] = ac(x1[b][idxh[0]].T).reshape(NCH, 128, NTOK)
        m["x_absB"] = ac(x1[b][idxh[1]].T).reshape(NCH, 128, NTOK)
        m["mask1"] = maskb1[half]
        m["bias1"] = bias1[half]
        mhh = np.zeros((128, 2), f32); mhh[:, half] = 1.0
        m["mh"] = mhh
        in_maps.append(m)
    nc1 = build_l1()
    r1 = run_bass_kernel_spmd(nc1, in_maps, core_ids=cores).results

    out = np.empty((4, 4096, 1024), f32)
    for c in cores:
        b, half = c // 2, c % 2
        out[b][idxh[half]] = r1[c]["o_x2"].reshape(1024, NTOK).T
    return out


# version 06 (final)
# speedup vs baseline: 1.3856x; 1.3252x over previous
from contextlib import ExitStack
import numpy as np
import concourse.bass as bass
import concourse.mybir as mybir
from concourse.alu_op_type import AluOpType as OP

F32 = mybir.dt.float32
BF16 = mybir.dt.bfloat16
AF = mybir.ActivationFunctionType

D = 1024
NTOK = 2048
NCH = 8
NTILE = 16
TB = 512
NBLK = NTOK // TB

def tmap(half):
    return [t for t in range(32) if ((t % 4) in (0, 3)) == (half == 0)]

def own_token_idx(half):
    return np.concatenate([np.arange(t * 128, (t + 1) * 128) for t in tmap(half)])

ALIBI_SLOPES = np.asarray([2.0 ** (-(i + 1) / 2.0) for i in range(16)], np.float32)


class _Handle:
    def __init__(self):
        self.incs = []
    def then_inc(self, sem, n=1):
        self.incs.append((sem, n))
        return self


class _EngProxy:
    def __init__(self, plan, eng):
        self._plan = plan
        self._eng = eng
    def __getattr__(self, name):
        def rec(*args, **kwargs):
            h = _Handle()
            self._plan.ops[self._eng].append((name, args, kwargs, h))
            return h
        return rec


class Plan:
    ENGINES = ("sync", "gpsimd", "tensor", "vector", "scalar")
    def __init__(self, nc):
        self.nc = nc
        self.ops = {e: [] for e in self.ENGINES}
        self._vsem = nc.semaphore("vchain").__enter__()
        self._vscr = nc.sbuf_tensor("vscratch", [1, 2], mybir.dt.float32).__enter__()
        self.t = _EngProxy(self, "tensor")
        self.v = _EngProxy(self, "vector")
        self.s = _EngProxy(self, "scalar")
        self.g = _EngProxy(self, "gpsimd")
        self.y = _EngProxy(self, "sync")

    def build(self):
        nc = self.nc
        vsem = self._vsem
        vscr = self._vscr
        with nc.Block() as block:
            for eng in self.ENGINES:
                lst = self.ops[eng]
                if not lst:
                    continue
                def make(lst, eng=eng):
                    def body(e):
                        cnt = 0
                        for name, args, kwargs, h in lst:
                            if eng != "vector" or name == "wait_ge":
                                r = getattr(e, name)(*args, **kwargs)
                                for sm, n in h.incs:
                                    r = r.then_inc(sm, n)
                                continue
                            if cnt > 0:
                                e.wait_ge(vsem, cnt)
                            r = getattr(e, name)(*args, **kwargs)
                            r.then_inc(vsem, 1)
                            cnt += 1
                            for sm, n in h.incs:
                                e.wait_ge(vsem, cnt)
                                e.tensor_copy(vscr[:, 0:1], vscr[:, 1:2])\
                                    .then_inc(sm, n)
                                cnt += 0
                    return body
                getattr(block, eng)(make(lst))


_UID = [0]
def _u(name):
    _UID[0] += 1
    return f"{name}_u{_UID[0]}"

def sb(st, nc, name, shape, dtype, side=None):
    kw = {"side": side} if side else {}
    return st.enter_context(nc.sbuf_tensor(_u(name), shape, dtype, **kw))

def ps(st, nc, name, shape, dtype=F32):
    return st.enter_context(nc.psum_tensor(_u(name), shape, dtype))

def sem(st, nc, name):
    return st.enter_context(nc.semaphore(_u(name)))

def cols(blk, tb=TB):
    return slice(blk * tb, (blk + 1) * tb)


def emit_ln(p, stm, sts, name, src, dst_f32=None, dst_bf=None, gamma=None, beta=None):
    nc = p.nc
    sq = [sb(stm, nc, f"{name}_sq{i}", [128, NTOK], BF16, side="right") for i in range(2)]
    xbf = [sb(stm, nc, f"{name}_xbf{i}", [128, NTOK], BF16, side="right") for i in range(2)]
    srow = [sb(stm, nc, f"{name}_sr{i}", [1, TB], F32, side="right") for i in range(8)]
    R = [sb(stm, nc, f"{name}_R{i}", [1, TB], F32, side="right") for i in range(5)]
    tmp = [sb(stm, nc, f"{name}_tmp{i}", [128, TB], F32, side="right") for i in range(2)]
    ones_bf = sb(stm, nc, f"{name}_1bf", [128, 1], BF16, side="right")
    ones_f = sb(stm, nc, f"{name}_1f", [1, 128], F32, side="right")
    s_pre = sem(sts, nc, f"{name}_pre")
    s_mm = sem(sts, nc, f"{name}_mm")
    s_row = sem(sts, nc, f"{name}_row")
    s_nrm = sem(sts, nc, f"{name}_nrm")

    p.v.memset(ones_bf[:], 1.0)
    p.v.memset(ones_f[:], 1.0)
    pst_scope = ExitStack()
    ps_st = [ps(pst_scope, nc, f"{name}_ps{i}", [1, TB]) for i in range(8)]
    for d in range(NCH):
        if d >= 2:
            p.s.wait_ge(s_mm, d - 1)
            p.v.wait_ge(s_mm, d - 1)
        p.s.activation(sq[d % 2][:], src[d], AF.Square).then_inc(s_pre, 1)
        p.v.tensor_copy(xbf[d % 2][:], src[d]).then_inc(s_pre, 1)
        p.t.wait_ge(s_pre, 2 * d + 2)
        for blk in range(NBLK):
            p.t.matmul(ps_st[blk][:], ones_bf[:], xbf[d % 2][:, cols(blk)],
                       start=(d == 0), stop=(d == NCH - 1))
        for blk in range(NBLK):
            h = p.t.matmul(ps_st[4 + blk][:], ones_bf[:], sq[d % 2][:, cols(blk)],
                           start=(d == 0), stop=(d == NCH - 1))
        h.then_inc(s_mm, 1)

    p.v.wait_ge(s_mm, NCH)
    for blk in range(NBLK):
        p.v.tensor_copy(srow[blk][:], ps_st[blk][:])
        h = p.v.tensor_copy(srow[4 + blk][:], ps_st[4 + blk][:])
    h.then_inc(s_row, 1)
    pst_scope.close()
    ps_b = [ps(stm, nc, f"{name}_pb{i}", [128, TB]) for i in range(4)]

    for blk in range(NBLK):
        base = 1 + 5 * blk
        p.v.tensor_scalar(R[0][:], srow[blk][:], 1.0 / D, None, OP.mult)
        p.v.tensor_scalar(R[1][:], srow[4 + blk][:], 1.0 / D, None, OP.mult)\
            .then_inc(s_row, 1)
        p.s.wait_ge(s_row, base + 1)
        p.s.activation(R[2][:], R[0][:], AF.Square).then_inc(s_row, 1)
        p.v.wait_ge(s_row, base + 2)
        p.v.scalar_tensor_tensor(R[1][:], R[1][:], 1e-5, R[2][:], OP.add, OP.subtract)\
            .then_inc(s_row, 1)
        p.s.wait_ge(s_row, base + 3)
        p.s.activation(R[2][:], R[1][:], AF.Sqrt).then_inc(s_row, 1)
        p.v.wait_ge(s_row, base + 4)
        p.v.reciprocal(R[3][:], R[2][:])
        p.v.tensor_tensor(R[4][:], R[0][:], R[3][:], OP.mult)\
            .then_inc(s_row, 1)
        br, bm = ps_b[2 * (blk % 2)], ps_b[2 * (blk % 2) + 1]
        p.t.wait_ge(s_row, base + 5)
        if blk >= 2:
            p.t.wait_ge(s_nrm, (blk - 1) * NCH)
        p.t.matmul(br[:], ones_f[:], R[3][:], start=True, stop=True)
        p.t.matmul(bm[:], ones_f[:], R[4][:], start=True, stop=True)\
            .then_inc(s_mm, 1)
        p.v.wait_ge(s_mm, NCH + 1 + blk)
        for d in range(NCH):
            t0, t1 = tmp[0], tmp[1]
            p.v.tensor_tensor(t0[:], src[d][:, cols(blk)], br[:], OP.mult)
            hxn = p.v.tensor_tensor(t1[:], t0[:], bm[:], OP.subtract)
            if gamma is not None:
                if dst_f32 is not None:
                    p.v.tensor_scalar(dst_f32[d][:, cols(blk)], t1[:],
                                      gamma[d], beta[d], OP.mult, OP.add)
                if dst_bf is not None:
                    hxn = p.v.tensor_scalar(dst_bf[d][:, cols(blk)], t1[:],
                                            gamma[d], beta[d], OP.mult, OP.add)
            else:
                if dst_f32 is not None:
                    hxn = p.v.tensor_copy(dst_f32[d][:, cols(blk)], t1[:])
                if dst_bf is not None:
                    hxn = p.v.tensor_copy(dst_bf[d][:, cols(blk)], t1[:])
            hxn.then_inc(s_nrm, 1)
    return s_nrm, NBLK * NCH


class StagedEvac:
    def __init__(self, p, stm, sts, name, n_stages=4, tb=TB, dtype=BF16):
        self.p = p
        self.stages = [sb(stm, p.nc, f"{name}_stg{i}", [128, tb], dtype)
                       for i in range(n_stages)]
        self.n_stages = n_stages
        self.s_st = sem(sts, p.nc, f"{name}_sst")
        self.s_dma = sem(sts, p.nc, f"{name}_sdma")
        self.g = 0

    def __call__(self, bank_ap, s_mm, mmcnt, dst_ap, scale=1.0, bias=0.0):
        p = self.p
        g = self.g
        stage = self.stages[g % self.n_stages]
        p.s.wait_ge(s_mm, mmcnt)
        if g >= self.n_stages:
            p.s.wait_ge(self.s_dma, 16 * (g - self.n_stages + 1))
        func = AF.Copy if isinstance(bias, float) else AF.Identity
        h = p.s.activation(stage[:], bank_ap, func, scale=scale, bias=bias)
        h.then_inc(self.s_st, 1)
        p.y.wait_ge(self.s_st, g + 1)
        p.y.dma_start(dst_ap, stage[:]).then_inc(self.s_dma, 16)
        self.g += 1
        return (self.s_st, self.g)

    def flush_wait(self):
        self.p.y.wait_ge(self.s_dma, 16 * self.g)


def emit_fm_matmul(p, stm, sts, name, oc_list, nch, get_lhsT, get_rhs, evac,
                   nblk=NBLK, tb=TB):
    nc = p.nc
    pss = [ps(stm, nc, f"{name}_ps{i}", [128, tb]) for i in range(4)]
    s_mm = sem(sts, nc, f"{name}_mm")
    frees = []
    g = 0
    for oc in oc_list:
        for blk in range(nblk):
            bank = pss[g % 4]
            if g >= 4:
                fs, fv = frees[g - 4]
                p.t.wait_ge(fs, fv)
            for d in range(nch):
                h = p.t.matmul(bank[:], get_lhsT(d, oc), get_rhs(d, blk),
                               start=(d == 0), stop=(d == nch - 1))
            h.then_inc(s_mm, 1)
            frees.append(evac(oc, blk, bank, s_mm, g + 1))
            g += 1
    return frees[-1]


def build_prep(layer):
    nc = bass.Bass("TRN2", target_bir_lowering=False)
    NE = 14 if layer == 0 else 16
    xT = nc.dram_tensor("xT", [NCH, 128, NTOK], F32, kind="ExternalInput").ap()
    wqkvT = nc.dram_tensor("wqkvT", [NCH, 128, 3072], F32, kind="ExternalInput").ap()
    gwT = nc.dram_tensor("gwT", [NCH, 128, NE], F32, kind="ExternalInput").ap()
    ident = nc.dram_tensor("ident", [128, 128], F32, kind="ExternalInput").ap()
    if layer == 0:
        gcol = nc.dram_tensor("gcol", [NCH, 128, 1], F32, kind="ExternalInput").ap()
        bcol = nc.dram_tensor("bcol", [NCH, 128, 1], F32, kind="ExternalInput").ap()
    else:
        bgc = nc.dram_tensor("bgc", [NE, 1], F32, kind="ExternalInput").ap()
    o_xn = nc.dram_tensor("o_xn", [NCH, 128, NTOK], F32, kind="ExternalOutput").ap()
    o_xnbf = nc.dram_tensor("o_xnbf", [NCH, 128, NTOK], BF16, kind="ExternalOutput").ap()
    o_q = nc.dram_tensor("o_q", [NCH, 128, NTOK], BF16, kind="ExternalOutput").ap()
    o_k = nc.dram_tensor("o_k", [NCH, 128, NTOK], BF16, kind="ExternalOutput").ap()
    o_v = nc.dram_tensor("o_v", [NTILE, 128, D], BF16, kind="ExternalOutput").ap()
    o_gate = nc.dram_tensor("o_gate", [NE, NTOK], F32, kind="ExternalOutput").ap()

    p = Plan(nc)
    st = ExitStack()
    x_scope = ExitStack()
    x_sb = [sb(x_scope, nc, f"x{d}", [128, NTOK], F32, side="right") for d in range(NCH)]
    xn_sb = [sb(st, nc, f"xn{d}", [128, NTOK], F32) for d in range(NCH)]
    xnbf_sb = [sb(st, nc, f"xnbf{d}", [128, NTOK], BF16) for d in range(NCH)]
    id_sb = sb(st, nc, "idf", [128, 128], F32)
    s_ld = sem(st, nc, "ld")
    s_fin = sem(st, nc, "fin")
    fin = 0
    ld = 0

    for d in range(NCH):
        p.y.dma_start(x_sb[d][:], xT[d]).then_inc(s_ld, 16)
        ld += 16
    p.g.dma_start(id_sb[:], ident).then_inc(s_ld, 16)
    ld += 16
    if layer == 0:
        gc_sb = [sb(st, nc, f"gc{d}", [128, 1], F32) for d in range(NCH)]
        bc_sb = [sb(st, nc, f"bc{d}", [128, 1], F32) for d in range(NCH)]
        for d in range(NCH):
            p.g.dma_start(gc_sb[d][:], gcol[d]).then_inc(s_ld, 16)
            p.g.dma_start(bc_sb[d][:], bcol[d]).then_inc(s_ld, 16)
            ld += 32
    else:
        bg_sb = sb(st, nc, "bgsb", [NE, 1], F32)
        p.g.dma_start(bg_sb[:], bgc).then_inc(s_ld, 16)
        ld += 16
    LD_ALL = ld

    for e in (p.s, p.v, p.t):
        e.wait_ge(s_ld, LD_ALL)

    ln1_scope = ExitStack()
    s_n1, n1 = emit_ln(p, ln1_scope, st, "ln1", [x_sb[d][:] for d in range(NCH)],
                       dst_f32=xn_sb, dst_bf=xnbf_sb)
    p.y.wait_ge(s_n1, n1)
    for d in range(NCH):
        p.y.dma_start(o_xn[d], xn_sb[d][:]).then_inc(s_fin, 16)
        p.y.dma_start(o_xnbf[d], xnbf_sb[d][:]).then_inc(s_fin, 16)
        fin += 32

    if layer == 0:
        for e in (p.s, p.v, p.t):
            e.wait_ge(s_n1, n1)
        ln1_scope.close()
        x_scope.close()
        xn2bf_sb = [sb(st, nc, f"xn2bf{d}", [128, NTOK], BF16) for d in range(NCH)]
        ln2_scope = ExitStack()
        s_n2, n2 = emit_ln(p, ln2_scope, st, "ln2", [xn_sb[d][:] for d in range(NCH)],
                           dst_bf=xn2bf_sb, gamma=[gc_sb[d][:] for d in range(NCH)],
                           beta=[bc_sb[d][:] for d in range(NCH)])
        qkv_src = xn2bf_sb
        s_rdy, nrdy = s_n2, n2
        end_scope = ln2_scope
    else:
        qkv_src = xnbf_sb
        s_rdy, nrdy = s_n1, n1
        end_scope = ln1_scope

    for e in (p.s, p.v, p.t):
        e.wait_ge(s_rdy, nrdy)
    end_scope.close()
    if layer != 0:
        x_scope.close()

    w_sb = [sb(st, nc, f"w{d}", [128, 3072], BF16) for d in range(NCH)]
    gw_sb = [sb(st, nc, f"gw{d}", [128, NE], F32) for d in range(NCH)]
    p.g.wait_ge(s_rdy, nrdy)
    for d in range(NCH):
        p.g.dma_start(w_sb[d][:], wqkvT[d]).then_inc(s_ld, 16)
        p.g.dma_start(gw_sb[d][:], gwT[d]).then_inc(s_ld, 16)
        ld += 32
    LD_W = ld
    for e in (p.s, p.v, p.t):
        e.wait_ge(s_ld, LD_W)

    qkv_scope = ExitStack()
    ev_qk = StagedEvac(p, qkv_scope, st, "eqk")

    def evac_qk(oc, blk, bank, s_mm, mmcnt):
        dst = o_q if oc < 8 else o_k
        return ev_qk(bank[:], s_mm, mmcnt, dst[oc % 8, :, cols(blk)],
                     scale=(0.125 if oc < 8 else 1.0))

    emit_fm_matmul(p, qkv_scope, st, "qk", list(range(16)), NCH,
                   lambda d, oc: w_sb[d][:, oc*128:(oc+1)*128],
                   lambda d, blk: qkv_src[d][:, cols(blk)], evac_qk)

    def evac_v(tt, vb, bank, s_mm, mmcnt):
        return ev_qk(bank[:], s_mm, mmcnt, o_v[tt, :, vb*TB:(vb+1)*TB])

    s_ev_v, n_ev_v = emit_fm_matmul(p, qkv_scope, st, "v", list(range(NTILE)), NCH,
                   lambda d, tt: qkv_src[d][:, tt*128:(tt+1)*128],
                   lambda d, vb: w_sb[d][:, 2048+vb*TB:2048+(vb+1)*TB],
                   evac_v, nblk=2)
    ev_qk.flush_wait()
    p.t.wait_ge(s_ev_v, n_ev_v)
    p.v.wait_ge(ev_qk.s_dma, 16 * ev_qk.g)
    p.s.wait_ge(ev_qk.s_dma, 16 * ev_qk.g)
    qkv_scope.close()

    gscope = ExitStack()
    ps_g = [ps(gscope, nc, f"gps{i}", [128, NE]) for i in range(2)]
    ps_t = [ps(gscope, nc, f"gpt{i}", [128, 128]) for i in range(2)]
    s_gmm = sem(st, nc, "gmm")
    s_gv = sem(st, nc, "gv")
    s_gt = sem(st, nc, "gt")
    gateT_sb = sb(gscope, nc, "gateT", [NE, NTOK], F32)
    zeroNE = sb(gscope, nc, "zeroNE", [128, NE], F32)
    p.v.memset(zeroNE[:], 0.0)
    coef = [sb(gscope, nc, f"coef{i}", [128, NE], F32) for i in range(2)]
    work = {nm: sb(gscope, nc, f"gw_{nm}", [128, NE], F32)
            for nm in ("e", "pp", "pm", "ge2", "pw")}
    colw = {nm: sb(gscope, nc, f"gc_{nm}", [128, 1], F32)
            for nm in ("m", "negm", "sum", "rsum", "m1", "m2", "den", "rden")}
    VPT = 3 if layer == 0 else 1
    for tt in range(NTILE):
        bank = ps_g[tt % 2]
        if tt >= 2:
            p.t.wait_ge(s_gv, VPT * (tt - 2) + VPT)
        for d in range(NCH):
            h = p.t.matmul(bank[:], xn_sb[d][:, tt*128:(tt+1)*128],
                           gw_sb[d][:], start=(d == 0), stop=(d == NCH - 1))
        h.then_inc(s_gmm, 1)
        cf = coef[tt % 2]
        p.v.wait_ge(s_gmm, 2 * tt + 1)
        if layer == 0:
            p.v.reduce_max(colw["m"][:], bank[:], axis=mybir.AxisListType.X)
            p.v.tensor_scalar(colw["negm"][:], colw["m"][:], -1.0, None, OP.mult)\
                .then_inc(s_gv, 1)
            p.s.wait_ge(s_gv, VPT * tt + 1)
            p.s.activation(work["e"][:], bank[:], AF.Exp, bias=colw["negm"][:])\
                .then_inc(s_gv, 1)
            p.v.wait_ge(s_gv, VPT * tt + 2)
            p.v.reduce_sum(colw["sum"][:], work["e"][:], axis=mybir.AxisListType.X)
            p.v.reciprocal(colw["rsum"][:], colw["sum"][:])
            p.v.tensor_scalar(work["pp"][:], work["e"][:], colw["rsum"][:], None, OP.mult)
            p.v.reduce_max(colw["m1"][:], work["pp"][:], axis=mybir.AxisListType.X)
            p.v.tensor_scalar(work["ge2"][:], work["pp"][:], colw["m1"][:], None, OP.is_ge)
            p.v.tensor_tensor(work["pm"][:], work["pp"][:], work["ge2"][:], OP.mult)
            p.v.tensor_tensor(work["pm"][:], work["pp"][:], work["pm"][:], OP.subtract)
            p.v.reduce_max(colw["m2"][:], work["pm"][:], axis=mybir.AxisListType.X)
            p.v.tensor_tensor(colw["den"][:], colw["m1"][:], colw["m2"][:], OP.add)
            p.v.reciprocal(colw["rden"][:], colw["den"][:])
            p.v.tensor_scalar(work["ge2"][:], work["pp"][:], colw["m2"][:], None, OP.is_ge)
            p.v.tensor_scalar(work["pw"][:], work["pp"][:], colw["rden"][:], None, OP.mult)
            p.v.tensor_tensor(cf[:], work["pw"][:], work["ge2"][:], OP.mult)\
                .then_inc(s_gv, 1)
        else:
            p.v.tensor_copy(cf[:], bank[:]).then_inc(s_gv, 1)
        tbk = ps_t[tt % 2]
        p.t.wait_ge(s_gv, VPT * tt + VPT)
        if tt >= 2:
            p.t.wait_ge(s_gt, tt - 1)
        p.t.transpose(tbk[:NE, :], cf[:], id_sb[:]).then_inc(s_gmm, 1)
        p.v.wait_ge(s_gmm, 2 * tt + 2)
        if layer == 0:
            hh = p.v.tensor_copy(gateT_sb[:, tt*128:(tt+1)*128], tbk[:NE, :])
        else:
            hh = p.v.tensor_scalar(gateT_sb[:, tt*128:(tt+1)*128], tbk[:NE, :],
                                   bg_sb[:], None, OP.add)
        hh.then_inc(s_gt, 1)
    p.y.wait_ge(s_gt, NTILE)
    p.y.dma_start(o_gate, gateT_sb[:]).then_inc(s_fin, 16)
    fin += 16

    p.y.wait_ge(s_fin, fin)
    p.build()
    gscope.close()
    qkv_scope.close()
    st.close()
    return nc


def host_prep_inputs(layer, d, core):
    half = core % 2
    b = core // 2
    idx = own_token_idx(half)
    if layer == 0:
        x = d["L0_attn_x"][b][idx]
        wq = d["L0_attn_w_qkv"]
        wqkvT = np.ascontiguousarray(wq.T).reshape(NCH, 128, 3072)
        gwT = np.ascontiguousarray(d["L0_ffn_gate_w"].T).reshape(NCH, 128, 14)
        m = {
            "xT": np.ascontiguousarray(x.T).reshape(NCH, 128, NTOK),
            "wqkvT": wqkvT,
            "gwT": gwT,
            "ident": np.eye(128, dtype=np.float32),
            "gcol": d["L0_attn_ln_gamma"].reshape(NCH, 128, 1).astype(np.float32),
            "bcol": d["L0_attn_ln_beta"].reshape(NCH, 128, 1).astype(np.float32),
        }
    else:
        x = d["x1"][b][idx]
        wqkvT = np.ascontiguousarray(d["L1_attn_W_qkv"]).reshape(NCH, 128, 3072)
        gwT = np.ascontiguousarray(d["L1_ffn_Wg"]).reshape(NCH, 128, 16)
        m = {
            "xT": np.ascontiguousarray(x.T).reshape(NCH, 128, NTOK),
            "wqkvT": wqkvT,
            "gwT": gwT,
            "ident": np.eye(128, dtype=np.float32),
            "bgc": d["L1_ffn_bg"].reshape(16, 1).astype(np.float32),
        }
    return m


def attn_schedule(alibi):
    for J in range(4):
        for h in range(16):
            qb = qblk_of_head(h, alibi)
            for s_ in range(512 // qb):
                off = J * 512 + s_ * qb
                j0 = off // 128
                QB = max(qb // 128, 1)
                KT = 2 * (j0 + QB)
                NM = 2 * QB
                yield (J, h, qb, j0, off, KT, NM)

def qblk_of_head(h, alibi):
    if not alibi:
        return 512
    return {0: 64, 1: 128, 2: 128, 3: 128, 4: 128, 5: 256, 6: 256}.get(h, 512)

MASK_CLASSES_L0 = ((512, 8, [0], [0]),)
MASK_CLASSES_L1 = ((512, 8, [0], [0]), (256, 4, [0], [0]),
                   (128, 2, [0, 1], [0]), (64, 2, [0, 1], [0, 1]))

def mask_patterns(half, alibi):
    pats = {}
    def par_fn(lt):
        return (lt % 2) if half == 0 else (1 - lt % 2)
    for qb, rels, pars, h64s in (MASK_CLASSES_L1 if alibi else MASK_CLASSES_L0):
        QB = max(qb // 128, 1)
        for par in pars:
            for h64 in h64s:
                j0 = par
                for rel in range(rels):
                    m = np.zeros((128, qb), np.float32)
                    for c in range(qb):
                        lt = j0 + (c + (h64 * 64 if qb == 64 else 0)) // 128
                        a = 2 * lt + par_fn(lt)
                        qpos = a * 128 + (c + h64 * 64) % 128
                        kbase = (2 * j0 + rel) * 128
                        i = np.arange(128)
                        m[:, c] = (kbase + i <= qpos).astype(np.float32)
                    pats[(qb, rel, par, h64)] = m
    keys = sorted(pats.keys())
    offs, bank, o = {}, [], 0
    for k in keys:
        offs[k] = o
        bank.append(pats[k])
        o += k[0]
    return np.concatenate(bank, axis=1), offs

def mask_offsets(alibi):
    offs, o = {}, 0
    for k in sorted([(qb, rel, par, h64)
                     for qb, rels, pars, h64s in (MASK_CLASSES_L1 if alibi else MASK_CLASSES_L0)
                     for par in pars for h64 in h64s for rel in range(rels)]):
        offs[k] = o
        o += k[0]
    return offs, o

def host_attn_inputs(layer, core, q, k_full, vpad, woutT, boutcol):
    half = core % 2
    alibi = layer == 1
    tm = tmap(half)
    maskbank, _ = mask_patterns(half, alibi)
    ncols = sum(KT for (_, _, _, _, _, KT, _) in attn_schedule(alibi))
    biascols = np.zeros((ncols, 128), np.float32)
    i = 0
    for (J, h, qb, j0, off, KT, NM) in attn_schedule(alibi):
        lts = range(j0, j0 + max(qb // 128, 1))
        if qb == 64:
            a = 2 * j0 + ((j0 % 2) if half == 0 else (1 - j0 % 2))
            h64 = (off % 128) // 64
            maxpos = a * 128 + h64 * 64 + 63
        else:
            maxpos = max(2 * lt + (((lt % 2) if half == 0 else (1 - lt % 2)))
                         for lt in lts) * 128 + 127
        q_end = maxpos + 1
        own_max_tile = maxpos // 128
        sl = ALIBI_SLOPES[h] if alibi else 0.0
        for kt in range(KT):
            if kt > own_max_tile:
                biascols[i] = -60000.0
            elif alibi:
                biascols[i] = sl * ((kt * 128 + np.arange(128)) - q_end)
            i += 1
    import ml_dtypes
    m = {
        "q": q, "k": k_full, "vpad": vpad, "woutT": woutT, "boutcol": boutcol,
        "maskbank": np.ascontiguousarray(maskbank).astype(ml_dtypes.bfloat16),
    }
    if alibi:
        m["biascols"] = np.ascontiguousarray(biascols.T)
    return m


def build_attn(layer):
    alibi = layer == 1
    nc = bass.Bass("TRN2", target_bir_lowering=False)
    moffs, mcols = mask_offsets(alibi)
    ncols = sum(KT for (_, _, _, _, _, KT, _) in attn_schedule(alibi))
    q_h = nc.dram_tensor("q", [NCH, 128, NTOK], BF16, kind="ExternalInput").ap()
    k_h = nc.dram_tensor("k", [NCH, 128, 4096], BF16, kind="ExternalInput").ap()
    v_h = nc.dram_tensor("vpad", [32, 128, 1040], BF16, kind="ExternalInput").ap()
    w_h = nc.dram_tensor("woutT", [NCH, 128, D], F32, kind="ExternalInput").ap()
    bo_h = nc.dram_tensor("boutcol", [NCH, 128, 1], F32, kind="ExternalInput").ap()
    if alibi:
        bc_h = nc.dram_tensor("biascols", [128, ncols], F32, kind="ExternalInput").ap()
    mk_h = nc.dram_tensor("maskbank", [128, mcols], BF16, kind="ExternalInput").ap()
    o_s = nc.dram_tensor("o_s", [NCH, 128, NTOK], F32, kind="ExternalOutput").ap()

    p = Plan(nc)
    st = ExitStack()
    qj_sb = [[sb(st, nc, f"q{i}_{d}", [128, TB], BF16) for d in range(NCH)]
             for i in range(2)]
    k_sb = [sb(st, nc, f"k{d}", [128, 4096], BF16) for d in range(NCH)]
    v_sb = [sb(st, nc, f"v{t}", [128, 1040], BF16) for t in range(32)]
    w_sb = [sb(st, nc, f"w{d}", [128, D], BF16) for d in range(NCH)]
    bo_sb = [sb(st, nc, f"bo{d}", [128, 1], F32) for d in range(NCH)]
    bc_sb = sb(st, nc, "bc", [128, ncols], F32) if alibi else None
    mk_sb = sb(st, nc, "mk", [128, mcols], BF16)
    ones_f = sb(st, nc, "onesf", [1, 64], F32)
    abuf = [sb(st, nc, f"ab{i}", [128, TB], BF16) for i in range(4)]
    rrow = sb(st, nc, "rrow", [1, TB], F32)
    rb_sb = sb(st, nc, "rb", [64, TB], F32)
    s_buf = [sb(st, nc, f"sb{d}", [128, TB], BF16) for d in range(NCH)]

    s_ld = sem(st, nc, "ld")
    s_sc = sem(st, nc, "sc")
    s_exp = sem(st, nc, "exp")
    s_msk = sem(st, nc, "msk")
    s_av = sem(st, nc, "av")
    s_fz = sem(st, nc, "fz")
    s_pj = sem(st, nc, "pj")
    s_fin = sem(st, nc, "fin")

    s_qld = sem(st, nc, "qld")
    qld = [0]
    def load_qJ(Jn):
        for d in range(NCH):
            p.y.dma_start(qj_sb[Jn % 2][d][:], q_h[d, :, Jn*TB:(Jn+1)*TB])\
                .then_inc(s_qld, 16)
            qld[0] += 16
    ld = 0
    load_qJ(0)
    load_qJ(1)
    for d in range(NCH):
        p.y.dma_start(k_sb[d][:], k_h[d]).then_inc(s_ld, 16); ld += 16
    for t in range(32):
        p.y.dma_start(v_sb[t][:], v_h[t]).then_inc(s_ld, 16); ld += 16
    for d in range(NCH):
        p.g.dma_start(w_sb[d][:], w_h[d]).then_inc(s_ld, 16); ld += 16
        p.g.dma_start(bo_sb[d][:], bo_h[d]).then_inc(s_ld, 16); ld += 16
    if alibi:
        p.g.dma_start(bc_sb[:], bc_h).then_inc(s_ld, 16); ld += 16
    p.g.dma_start(mk_sb[:], mk_h).then_inc(s_ld, 16); ld += 16
    p.v.memset(ones_f[:], 1.0)
    for e in (p.t, p.s, p.v):
        e.wait_ge(s_ld, ld)

    ps_sc = [ps(st, nc, f"psc{i}", [128, TB]) for i in range(3)]
    ps_o = ps(st, nc, "pso", [65, TB])
    ps_b = ps(st, nc, "psb", [64, TB])
    ps_pj = [ps(st, nc, f"pspj{i}", [128, TB]) for i in range(2)]
    ev = StagedEvac(p, st, st, "es", n_stages=2, dtype=F32)

    C = dict(sc=0, exp=0, msk=0, av=0, fz=0, pj=0, g=0)
    frees_pj = []
    colix = 0
    sched = list(attn_schedule(alibi))
    si = 0
    for J in range(4):
        p.t.wait_ge(s_qld, 128 * (J + 1))
        items = [it for it in sched if it[0] == J]
        for (J_, h, qb, j0, off, KT, NM) in items:
            hp, hh = h // 2, h % 2
            rows = slice(hh * 64, hh * 64 + 64)
            qcols = slice(off - J * 512, off - J * 512 + qb)
            for kt in range(KT):
                g = C["g"]
                bank = ps_sc[g % 3]
                if g >= 3:
                    p.t.wait_ge(s_exp, g - 2)
                p.t.matmul(bank[:, :qb], k_sb[hp][rows, kt*128:(kt+1)*128],
                           qj_sb[J % 2][hp][rows, qcols], start=True, stop=True)\
                    .then_inc(s_sc, 1)
                C["sc"] += 1
                ab = abuf[g % 4]
                p.s.wait_ge(s_sc, C["sc"])
                if g >= 4:
                    p.s.wait_ge(s_av, g - 3)
                bias = bc_sb[:, colix:colix+1] if alibi else 0.0
                p.s.activation(ab[:, :qb], bank[:, :qb], AF.Exp, bias=bias)\
                    .then_inc(s_exp, 1)
                C["exp"] += 1
                colix += 1
                masked = kt >= KT - NM
                if masked:
                    rel = kt - (KT - NM)
                    par = j0 % 2 if qb <= 128 else 0
                    h64 = (off % 128) // 64 if qb == 64 else 0
                    mo = moffs[(qb, rel, par, h64)]
                    p.v.wait_ge(s_exp, C["exp"])
                    p.v.tensor_tensor(ab[:, :qb], ab[:, :qb],
                                      mk_sb[:, mo:mo+qb], OP.mult)\
                        .then_inc(s_msk, 1)
                    C["msk"] += 1
                if masked:
                    p.t.wait_ge(s_msk, C["msk"])
                else:
                    p.t.wait_ge(s_exp, C["exp"])
                if kt == 0:
                    if si > 0:
                        p.t.wait_ge(s_fz, 3 * si)
                p.t.matmul(ps_o[:, :qb], v_sb[kt][:, h*65:(h+1)*65],
                           ab[:, :qb], start=(kt == 0), stop=(kt == KT - 1))\
                    .then_inc(s_av, 1)
                C["av"] += 1
                C["g"] += 1
            p.v.wait_ge(s_av, C["av"])
            p.v.reciprocal(rrow[:, :qb], ps_o[64:65, :qb]).then_inc(s_fz, 1)
            C["fz"] += 1
            p.t.wait_ge(s_fz, C["fz"])
            if si > 0:
                p.t.wait_ge(s_fz, 3 * si)
            p.t.matmul(ps_b[:, :qb], ones_f[:], rrow[:, :qb], start=True,
                       stop=True).then_inc(s_sc, 1)
            C["sc"] += 1
            p.v.wait_ge(s_sc, C["sc"])
            p.v.tensor_copy(rb_sb[:, :qb], ps_b[:, :qb]).then_inc(s_fz, 1)
            C["fz"] += 1
            scols = slice(off - J * 512, off - J * 512 + qb)
            p.v.tensor_tensor(s_buf[hp][rows, scols], ps_o[0:64, :qb],
                              rb_sb[:, :qb], OP.mult).then_inc(s_fz, 1)
            C["fz"] += 1
            si += 1
        p.t.wait_ge(s_fz, 3 * si)
        for oc in range(NCH):
            bank = ps_pj[oc % 2]
            gp = J * NCH + oc
            if gp >= 2:
                fs, fv = frees_pj[gp - 2]
                p.t.wait_ge(fs, fv)
            for e in range(NCH):
                hmm = p.t.matmul(bank[:], w_sb[e][:, oc*128:(oc+1)*128],
                                 s_buf[e][:], start=(e == 0), stop=(e == NCH - 1))
            hmm.then_inc(s_pj, 1)
            C["pj"] += 1
            frees_pj.append(ev(bank[:], s_pj, C["pj"], o_s[oc, :, J*512:(J+1)*512],
                               bias=bo_sb[oc][:]))
        p.v.wait_ge(s_pj, C["pj"])
        if J + 2 < 4:
            p.y.wait_ge(s_pj, C["pj"])
            load_qJ(J + 2)
    ev.flush_wait()
    p.build()
    st.close()
    return nc


def build_moe0(NEXP=14):
    nc = bass.Bass("TRN2", target_bir_lowering=False)
    NE = 14
    xnbf_h = nc.dram_tensor("xnbf", [NCH, 128, NTOK], BF16, kind="ExternalInput").ap()
    xn_h = nc.dram_tensor("xn", [NCH, 128, NTOK], F32, kind="ExternalInput").ap()
    cf_h = nc.dram_tensor("coeffT", [NE, NTOK], F32, kind="ExternalInput").ap()
    w1_h = nc.dram_tensor("w1", [NE, NCH, 128, 2048], F32, kind="ExternalInput").ap()
    b1_h = nc.dram_tensor("b1c", [128, NE * 16], F32, kind="ExternalInput").ap()
    w2_h = nc.dram_tensor("w2", [NE, 16, 128, D], F32, kind="ExternalInput").ap()
    b2_h = nc.dram_tensor("b2r", [NE, D], F32, kind="ExternalInput").ap()
    x_h = nc.dram_tensor("xT", [NCH, 128, NTOK], F32, kind="ExternalInput").ap()
    s_h = nc.dram_tensor("s0", [NCH, 128, NTOK], F32, kind="ExternalInput").ap()
    sel_h = nc.dram_tensor("sel", [14, 128 * 14], F32, kind="ExternalInput").ap()
    o_x1 = nc.dram_tensor("o_x1", [NCH, 128, NTOK], F32, kind="ExternalOutput").ap()

    p = Plan(nc)
    st = ExitStack()
    xnbf_sb = [sb(st, nc, f"xnbf{d}", [128, NTOK], BF16) for d in range(NCH)]
    gacc = [sb(st, nc, f"ga{d}", [128, NTOK], F32) for d in range(NCH)]
    cf_sb = sb(st, nc, "cfsb", [NE, NTOK], F32)
    b1_sb = sb(st, nc, "b1sb", [128, NE * 16], F32)
    b2_sb = sb(st, nc, "b2sb", [NE, D], F32)
    ones_f = sb(st, nc, "onesf", [1, 128], F32)
    sel_sb = sb(st, nc, "selsb", [14, 128 * 14], F32)
    cb_sb = sb(st, nc, "cb", [128, TB], F32)
    tmp_sb = sb(st, nc, "tmp", [128, TB], F32)
    s_ld = sem(st, nc, "ld")
    s_w = sem(st, nc, "w")
    s_mm = sem(st, nc, "mm")
    s_ev = sem(st, nc, "ev")
    s_acc = sem(st, nc, "acc")
    s_cb = sem(st, nc, "cbk")
    s_fin = sem(st, nc, "fin")
    p.v.memset(ones_f[:], 1.0)

    ld = 0
    for d in range(NCH):
        p.y.dma_start(xnbf_sb[d][:], xnbf_h[d]).then_inc(s_ld, 16); ld += 16
    p.y.dma_start(cf_sb[:], cf_h).then_inc(s_ld, 16); ld += 16
    p.y.dma_start(b1_sb[:], b1_h).then_inc(s_ld, 16); ld += 16
    p.y.dma_start(b2_sb[:], b2_h).then_inc(s_ld, 16); ld += 16
    p.y.dma_start(sel_sb[:], sel_h).then_inc(s_ld, 16); ld += 16
    for e in (p.t, p.s, p.v):
        e.wait_ge(s_ld, ld)

    iscope = ExitStack()
    xnb = [sb(iscope, nc, f"xnb{i}", [128, NTOK], F32, side="right") for i in range(2)]
    with ExitStack() as sti:
        ps_i = [ps(sti, nc, f"pi{i}", [128, TB]) for i in range(4)]
        g = 0
        for d in range(NCH):
            if d >= 2:
                p.y.wait_ge(s_acc, (d - 1) * NBLK)
            p.y.dma_start(xnb[d % 2][:], xn_h[d]).then_inc(s_ld, 16)
            ld += 16
            p.v.wait_ge(s_ld, ld)
            for blk in range(NBLK):
                bank = ps_i[g % 4]
                if g >= 4:
                    p.t.wait_ge(s_acc, g - 3)
                p.t.matmul(bank[:], b2_sb[:, d*128:(d+1)*128], cf_sb[:, cols(blk)],
                           start=True, stop=True).then_inc(s_mm, 1)
                p.v.wait_ge(s_mm, g + 1)
                p.v.tensor_tensor(gacc[d][:, cols(blk)], xnb[d % 2][:, cols(blk)],
                                  bank[:], OP.add).then_inc(s_acc, 1)
                g += 1
        NB_INIT = g

    iscope.close()
    wscope = ExitStack()
    w1_sb = [sb(wscope, nc, f"w1_{d}", [128, 2048], BF16) for d in range(NCH)]
    w2_sb = [sb(wscope, nc, f"w2_{f}", [128, D], BF16) for f in range(16)]
    h_sb = [sb(wscope, nc, f"h{f}", [128, TB], BF16) for f in range(16)]
    ps_h = [ps(wscope, nc, f"ph{i}", [128, TB]) for i in range(2)]
    ps_o = [ps(wscope, nc, f"po{i}", [128, TB]) for i in range(2)]
    ps_c = ps(wscope, nc, "pc", [128, TB])
    wld = [0]
    ghc = [0]
    omc = [0]
    acc2 = [NB_INIT]
    cbc = [0]
    mmc = [NB_INIT]

    def load_we(e):
        p.g.wait_ge(s_acc, NB_INIT + e * 32)
        for d in range(NCH):
            p.g.dma_start(w1_sb[d][:], w1_h[e, d]).then_inc(s_w, 16); wld[0] += 16
        for f in range(16):
            p.g.dma_start(w2_sb[f][:], w2_h[e, f]).then_inc(s_w, 16); wld[0] += 16

    for e in range(NEXP):
        load_we(e)
        p.t.wait_ge(s_w, wld[0])
        for tb in range(NBLK):
            if cbc[0] > 0:
                p.t.wait_ge(s_cb, 2 * cbc[0])
            p.t.matmul(ps_c[:], sel_sb[:, e*128:(e+1)*128], cf_sb[:, cols(tb)],
                       start=True, stop=True).then_inc(s_mm, 1)
            mmc[0] += 1
            p.v.wait_ge(s_mm, mmc[0])
            p.v.tensor_copy(cb_sb[:], ps_c[:]).then_inc(s_cb, 1)
            for fc in range(16):
                bank = ps_h[fc % 2]
                gh = ghc[0]
                if gh >= 2:
                    p.t.wait_ge(s_ev, gh - 1)
                for d in range(NCH):
                    h = p.t.matmul(bank[:], w1_sb[d][:, fc*128:(fc+1)*128],
                                   xnbf_sb[d][:, cols(tb)],
                                   start=(d == 0), stop=(d == NCH - 1))
                h.then_inc(s_mm, 1)
                mmc[0] += 1
                p.s.wait_ge(s_mm, mmc[0])
                p.s.activation(h_sb[fc][:], bank[:], AF.Gelu,
                               bias=b1_sb[:, e*16+fc:e*16+fc+1]).then_inc(s_ev, 1)
                ghc[0] += 1
            for dd in range(NCH):
                bank = ps_o[dd % 2]
                om = omc[0]
                if dd == 0:
                    p.t.wait_ge(s_ev, ghc[0])
                if om >= 2:
                    p.t.wait_ge(s_acc, acc2[0] - 1)
                for f in range(16):
                    h = p.t.matmul(bank[:], w2_sb[f][:, dd*128:(dd+1)*128],
                                   h_sb[f][:], start=(f == 0), stop=(f == 15))
                h.then_inc(s_mm, 1)
                mmc[0] += 1
                p.v.wait_ge(s_mm, mmc[0])
                p.v.tensor_tensor(tmp_sb[:], bank[:], cb_sb[:], OP.mult)
                p.v.tensor_tensor(gacc[dd][:, cols(tb)], gacc[dd][:, cols(tb)],
                                  tmp_sb[:], OP.add).then_inc(s_acc, 1)
                acc2[0] += 1
                omc[0] += 1
            p.v.nop().then_inc(s_cb, 1)
            cbc[0] += 1
    wscope.close()

    fscope = ExitStack()
    xb = [sb(fscope, nc, f"xb{i}", [128, NTOK], F32, side="right") for i in range(2)]
    sb_ = [sb(fscope, nc, f"sbb{i}", [128, NTOK], F32, side="right") for i in range(2)]
    gb = sb(fscope, nc, "gb", [128, NTOK], F32, side="right")
    x1b = [sb(fscope, nc, f"x1b{i}", [128, NTOK], F32, side="right") for i in range(2)]
    s_clip = sem(st, nc, "clip")
    s_silu = sem(st, nc, "silu")
    s_x1 = sem(st, nc, "x1")
    s_od = sem(st, nc, "od")
    echo = sb(fscope, nc, "echo", [128, 1], F32, side="right")
    p.y.wait_ge(s_acc, acc2[0])
    for d in range(NCH):
        if d >= 2:
            p.y.wait_ge(s_x1, d - 1)
        p.y.dma_start(xb[d % 2][:], x_h[d]).then_inc(s_ld, 16)
        p.y.dma_start(sb_[d % 2][:], s_h[d]).then_inc(s_ld, 16)
        ld += 32
        p.v.wait_ge(s_ld, ld)
        p.v.tensor_scalar(gacc[d][:], gacc[d][:], 100.0, -100.0, OP.min, OP.max)\
            .then_inc(s_clip, 1)
        p.s.wait_ge(s_clip, d + 1)
        if d >= 1:
            p.s.wait_ge(s_x1, d)
        p.s.activation(gb[:], gacc[d][:], AF.Silu).then_inc(s_silu, 1)
        p.v.wait_ge(s_silu, d + 1)
        if d >= 2:
            p.v.wait_ge(s_od, 16 * (d - 1))
        p.v.tensor_tensor(gb[:], gb[:], sb_[d % 2][:], OP.mult)
        p.v.tensor_tensor(x1b[d % 2][:], gb[:], xb[d % 2][:], OP.add)
        p.v.tensor_copy(echo[:], x1b[d % 2][:, NTOK-1:NTOK]).then_inc(s_x1, 1)
        p.y.wait_ge(s_x1, d + 1)
        p.y.dma_start(o_x1[d], x1b[d % 2][:]).then_inc(s_od, 16)
    p.y.wait_ge(s_od, 16 * NCH)
    p.build()
    fscope.close()
    st.close()
    return nc


def host_moe0_inputs(d, core, xn, xnbf, coeffT, s0, ri=None):
    half = core % 2
    b = core // 2
    return {
        "xnbf": xnbf, "xn": xn, "coeffT": coeffT,
        "w1": np.ascontiguousarray(d["L0_ffn_w1"].reshape(14, 8, 128, 2048)),
        "b1c": np.ascontiguousarray(d["L0_ffn_b1"].reshape(14 * 16, 128).T),
        "w2": np.ascontiguousarray(d["L0_ffn_w2"].reshape(14, 16, 128, 1024)),
        "b2r": np.ascontiguousarray(d["L0_ffn_b2"]),
        "xT": np.ascontiguousarray(d["L0_attn_x"][b][own_token_idx(half)].T)
              .reshape(NCH, 128, NTOK),
        "s0": s0,
        "sel": np.ascontiguousarray(
            np.repeat(np.eye(14, dtype=np.float32), 128, axis=1)),
    }


def build_ffn1(NEXP=16, BIS=40):
    nc = bass.Bass("TRN2", target_bir_lowering=False)
    NE = 16
    S = 4096
    xnbf_h = nc.dram_tensor("xnbf", [NCH, 128, NTOK], BF16, kind="ExternalInput").ap()
    lg_h = nc.dram_tensor("logitsP", [NE, S], F32, kind="ExternalInput").ap()
    w1_h = nc.dram_tensor("w1", [NE, NCH, 128, D], F32, kind="ExternalInput").ap()
    w2_h = nc.dram_tensor("w2", [NE, NCH, 128, D], F32, kind="ExternalInput").ap()
    w3_h = nc.dram_tensor("w3", [NE, NCH, 128, D], F32, kind="ExternalInput").ap()
    b1_h = nc.dram_tensor("b1r", [1, NE], F32, kind="ExternalInput").ap()
    b2_h = nc.dram_tensor("b2r", [1, NE], F32, kind="ExternalInput").ap()
    b3_h = nc.dram_tensor("b3c", [NE, 1], F32, kind="ExternalInput").ap()
    sel_h = nc.dram_tensor("sel", [NE, 128 * NE], F32, kind="ExternalInput").ap()
    x_h = nc.dram_tensor("x1T", [NCH, 128, NTOK], F32, kind="ExternalInput").ap()
    s_h = nc.dram_tensor("s1", [NCH, 128, NTOK], F32, kind="ExternalInput").ap()
    o_x2 = nc.dram_tensor("o_x2", [NCH, 128, NTOK], F32, kind="ExternalOutput").ap()
    o_cf = nc.dram_tensor("o_cf", [NE, S], F32, kind="ExternalOutput").ap()

    p = Plan(nc)
    st = ExitStack()
    xnbf_sb = [sb(st, nc, f"xnbf{d}", [128, NTOK], BF16) for d in range(NCH)]
    gacc = [sb(st, nc, f"ga{d}", [128, NTOK], F32) for d in range(NCH)]
    coefP = sb(st, nc, "coefP", [128, S], F32)
    sel_sb = sb(st, nc, "selsb", [NE, 128 * NE], F32)
    b3c_sb = sb(st, nc, "b3csb", [NE, 1], F32)
    ones_f = sb(st, nc, "onesf", [1, 128], F32)
    cb_sb = sb(st, nc, "cb", [128, TB], F32)
    tmp_sb = sb(st, nc, "tmp", [128, TB], F32)
    B1c = sb(st, nc, "B1c", [128, NE], F32)
    B2c = sb(st, nc, "B2c", [128, NE], F32)
    rowb = sb(st, nc, "rowb", [1, TB], F32)
    colw = {nm: sb(st, nc, f"c_{nm}", [128, 1], F32)
            for nm in ("mx", "negmx", "sum", "rsum", "lo", "hi", "u", "t",
                       "cnt", "sel1", "nsel", "dlt")}
    s_ld = sem(st, nc, "ld")
    s_w = sem(st, nc, "w")
    s_mm = sem(st, nc, "mm")
    s_ev = sem(st, nc, "ev")
    s_acc = sem(st, nc, "acc")
    s_cb = sem(st, nc, "cbk")
    s_gate = sem(st, nc, "gate")
    p.v.memset(ones_f[:], 1.0)

    ld = 0
    for d in range(NCH):
        p.y.dma_start(xnbf_sb[d][:], xnbf_h[d]).then_inc(s_ld, 16); ld += 16
    p.y.dma_start(sel_sb[:], sel_h).then_inc(s_ld, 16); ld += 16
    p.y.dma_start(b3c_sb[:], b3_h).then_inc(s_ld, 16); ld += 16

    gscope = ExitStack()
    lv_sb = sb(gscope, nc, "lv", [128, S], F32, side="right")
    pv_sb = sb(gscope, nc, "pv", [128, S], F32, side="right")
    ge_sb = sb(gscope, nc, "ge", [128, S], F32, side="right")
    b1r_sb = sb(gscope, nc, "b1rsb", [1, NE], F32, side="right")
    b2r_sb = sb(gscope, nc, "b2rsb", [1, NE], F32, side="right")
    p.v.memset(lv_sb[:], 0.0).then_inc(s_gate, 100)
    p.y.wait_ge(s_gate, 100)
    p.y.dma_start(lv_sb[0:NE, :], lg_h).then_inc(s_ld, 16); ld += 16
    p.y.dma_start(b1r_sb[:], b1_h).then_inc(s_ld, 16); ld += 16
    p.y.dma_start(b2r_sb[:], b2_h).then_inc(s_ld, 16); ld += 16
    for e in (p.t, p.s, p.v):
        e.wait_ge(s_ld, ld)
    with ExitStack() as stb:
        ps_bc = [ps(stb, nc, f"pbc{i}", [128, NE]) for i in range(2)]
        p.t.matmul(ps_bc[0][:], ones_f[:], b1r_sb[:], start=True, stop=True)
        p.t.matmul(ps_bc[1][:], ones_f[:], b2r_sb[:], start=True, stop=True)\
            .then_inc(s_mm, 1)
        p.v.wait_ge(s_mm, 1)
        p.v.tensor_copy(B1c[:], ps_bc[0][:])
        p.v.tensor_copy(B2c[:], ps_bc[1][:]).then_inc(s_gate, 1)
    X = mybir.AxisListType.X
    p.v.reduce_max(colw["mx"][:], lv_sb[:], axis=X)
    p.v.tensor_scalar(colw["negmx"][:], colw["mx"][:], -1.0, None, OP.mult)\
        .then_inc(s_gate, 1)
    p.s.wait_ge(s_gate, 102)
    p.s.activation(pv_sb[:], lv_sb[:], AF.Exp, bias=colw["negmx"][:])\
        .then_inc(s_gate, 1)
    p.v.wait_ge(s_gate, 103)
    p.v.reduce_sum(colw["sum"][:], pv_sb[:], axis=X)
    p.v.reciprocal(colw["rsum"][:], colw["sum"][:])
    p.v.tensor_scalar(pv_sb[:], pv_sb[:], colw["rsum"][:], None, OP.mult)
    p.v.memset(colw["lo"][:], 0.0)
    p.v.memset(colw["hi"][:], 1.0)
    for it in range(BIS):
        p.v.tensor_tensor(colw["u"][:], colw["lo"][:], colw["hi"][:], OP.add)
        p.v.tensor_scalar(colw["t"][:], colw["u"][:], 0.5, None, OP.mult)
        p.v.tensor_scalar(ge_sb[:], pv_sb[:], colw["t"][:], None, OP.is_ge)
        p.v.reduce_sum(colw["cnt"][:], ge_sb[:], axis=X)
        p.v.tensor_scalar(colw["sel1"][:], colw["cnt"][:], 256.0, None, OP.is_ge)
        p.v.tensor_scalar(colw["nsel"][:], colw["sel1"][:], -1.0, 1.0, OP.mult, OP.add)
        p.v.tensor_tensor(colw["dlt"][:], colw["t"][:], colw["lo"][:], OP.subtract)
        p.v.tensor_tensor(colw["dlt"][:], colw["dlt"][:], colw["sel1"][:], OP.mult)
        p.v.tensor_tensor(colw["lo"][:], colw["lo"][:], colw["dlt"][:], OP.add)
        p.v.tensor_tensor(colw["dlt"][:], colw["t"][:], colw["hi"][:], OP.subtract)
        p.v.tensor_tensor(colw["dlt"][:], colw["dlt"][:], colw["nsel"][:], OP.mult)
        p.v.tensor_tensor(colw["hi"][:], colw["hi"][:], colw["dlt"][:], OP.add)
    p.v.tensor_scalar(ge_sb[:], pv_sb[:], colw["lo"][:], None, OP.is_ge)
    p.v.tensor_tensor(coefP[:], pv_sb[:], ge_sb[:], OP.mult).then_inc(s_gate, 1)

    p.y.wait_ge(s_gate, 104)
    p.y.dma_start(o_cf, coefP[0:NE, :]).then_inc(s_ld, 16)
    ld += 16
    with ExitStack() as sti:
        ps_r = ps(sti, nc, "pr", [1, TB])
        ps_b = [ps(sti, nc, f"pb{i}", [128, TB]) for i in range(2)]
        p.t.wait_ge(s_gate, 104)
        g = 0
        acc0 = 0
        for tb in range(NBLK):
            if tb >= 1:
                p.t.wait_ge(s_gate, 104 + tb)
            p.t.matmul(ps_r[:], b3c_sb[:], coefP[0:NE, cols(tb)], start=True,
                       stop=True).then_inc(s_mm, 1)
            p.v.wait_ge(s_mm, 2 + 3 * tb)
            p.v.tensor_copy(rowb[:], ps_r[:]).then_inc(s_gate, 1)
            bank = ps_b[tb % 2]
            p.t.wait_ge(s_gate, 105 + tb)
            if tb >= 2:
                p.t.wait_ge(s_acc, (tb - 1) * NCH)
            p.t.matmul(bank[:], ones_f[:], rowb[:], start=True, stop=True)\
                .then_inc(s_mm, 2)
            p.v.wait_ge(s_mm, 4 + 3 * tb)
            for d in range(NCH):
                p.v.tensor_copy(gacc[d][:, cols(tb)], bank[:]).then_inc(s_acc, 1)
                acc0 += 1
        NB_INIT = acc0
        MM_INIT = 1 + 3 * NBLK
    gscope.close()

    wscope = ExitStack()
    w1_sb = [sb(wscope, nc, f"w1_{d}", [128, D], BF16) for d in range(NCH)]
    w2_sb = [sb(wscope, nc, f"w2_{d}", [128, D], BF16) for d in range(NCH)]
    w3_sb = [sb(wscope, nc, f"w3_{d}", [128, D], BF16) for d in range(NCH)]
    h1 = [sb(wscope, nc, f"h1_{d}", [128, TB], BF16) for d in range(NCH)]
    h2 = [sb(wscope, nc, f"h2_{d}", [128, TB], BF16) for d in range(NCH)]
    ps_h = [ps(wscope, nc, f"ph{i}", [128, TB]) for i in range(2)]
    ps_o = [ps(wscope, nc, f"po{i}", [128, TB]) for i in range(2)]
    ps_c = ps(wscope, nc, "pc", [128, TB])
    wld = [0]
    evc = [0]
    omc = [0]
    accc = [NB_INIT]
    cbc = [0]
    mmc = [MM_INIT]

    for e in range(NEXP):
        p.g.wait_ge(s_acc, NB_INIT + e * 32)
        for d in range(NCH):
            p.g.dma_start(w1_sb[d][:], w1_h[e, d]).then_inc(s_w, 16); wld[0] += 16
            p.g.dma_start(w2_sb[d][:], w2_h[e, d]).then_inc(s_w, 16); wld[0] += 16
            p.g.dma_start(w3_sb[d][:], w3_h[e, d]).then_inc(s_w, 16); wld[0] += 16
        p.t.wait_ge(s_w, wld[0])
        for tb in range(NBLK):
            if cbc[0] > 0:
                p.t.wait_ge(s_cb, 2 * cbc[0])
            p.t.matmul(ps_c[:], sel_sb[:, e*128:(e+1)*128], coefP[0:NE, cols(tb)],
                       start=True, stop=True).then_inc(s_mm, 1)
            mmc[0] += 1
            p.v.wait_ge(s_mm, mmc[0])
            p.v.tensor_copy(cb_sb[:], ps_c[:]).then_inc(s_cb, 1)
            for stage, wbuf, src, dst, bcol in (
                    (1, w1_sb, None, h1, B1c), (2, w2_sb, h1, h2, B2c)):
                for oc in range(NCH):
                    bank = ps_h[oc % 2]
                    gh = evc[0]
                    if stage == 2 and oc == 0:
                        p.t.wait_ge(s_ev, evc[0])
                    elif gh >= 2:
                        p.t.wait_ge(s_ev, gh - 1)
                    for d in range(NCH):
                        rhs = xnbf_sb[d][:, cols(tb)] if stage == 1 else src[d][:]
                        h = p.t.matmul(bank[:], wbuf[d][:, oc*128:(oc+1)*128],
                                       rhs, start=(d == 0), stop=(d == NCH - 1))
                    h.then_inc(s_mm, 1)
                    mmc[0] += 1
                    p.s.wait_ge(s_mm, mmc[0])
                    p.s.activation(dst[oc][:], bank[:], AF.Relu,
                                   bias=bcol[:, e:e+1]).then_inc(s_ev, 1)
                    evc[0] += 1
            for dd in range(NCH):
                bank = ps_o[dd % 2]
                om = omc[0]
                if dd == 0:
                    p.t.wait_ge(s_ev, evc[0])
                if om >= 2:
                    p.t.wait_ge(s_acc, accc[0] - 1)
                for f in range(NCH):
                    h = p.t.matmul(bank[:], w3_sb[f][:, dd*128:(dd+1)*128],
                                   h2[f][:], start=(f == 0), stop=(f == NCH - 1))
                h.then_inc(s_mm, 1)
                mmc[0] += 1
                p.v.wait_ge(s_mm, mmc[0])
                p.v.tensor_tensor(tmp_sb[:], bank[:], cb_sb[:], OP.mult)
                p.v.tensor_tensor(gacc[dd][:, cols(tb)], gacc[dd][:, cols(tb)],
                                  tmp_sb[:], OP.add).then_inc(s_acc, 1)
                accc[0] += 1
                omc[0] += 1
            p.v.nop().then_inc(s_cb, 1)
            cbc[0] += 1
    wscope.close()

    fscope = ExitStack()
    xb = [sb(fscope, nc, f"xb{i}", [128, NTOK], F32, side="right") for i in range(2)]
    sb_ = [sb(fscope, nc, f"sbb{i}", [128, NTOK], F32, side="right") for i in range(2)]
    gb = sb(fscope, nc, "gb", [128, NTOK], F32, side="right")
    x1b = [sb(fscope, nc, f"x1b{i}", [128, NTOK], F32, side="right") for i in range(2)]
    echo = sb(fscope, nc, "echo", [128, 1], F32, side="right")
    s_silu = sem(st, nc, "silu")
    s_x1 = sem(st, nc, "x1")
    s_od = sem(st, nc, "od")
    p.y.wait_ge(s_acc, accc[0])
    for d in range(NCH):
        if d >= 2:
            p.y.wait_ge(s_x1, d - 1)
        p.y.dma_start(xb[d % 2][:], x_h[d]).then_inc(s_ld, 16)
        p.y.dma_start(sb_[d % 2][:], s_h[d]).then_inc(s_ld, 16)
        ld += 32
        p.s.wait_ge(s_ld, ld)
        if d >= 1:
            p.s.wait_ge(s_x1, d)
        p.s.activation(gb[:], gacc[d][:], AF.Silu).then_inc(s_silu, 1)
        p.v.wait_ge(s_silu, d + 1)
        p.v.wait_ge(s_ld, ld)
        if d >= 2:
            p.v.wait_ge(s_od, 16 * (d - 1))
        p.v.tensor_tensor(gb[:], gb[:], sb_[d % 2][:], OP.mult)
        p.v.tensor_tensor(x1b[d % 2][:], gb[:], xb[d % 2][:], OP.add)
        p.v.tensor_copy(echo[:], x1b[d % 2][:, NTOK-1:NTOK]).then_inc(s_x1, 1)
        p.y.wait_ge(s_x1, d + 1)
        p.y.dma_start(o_x2[d], x1b[d % 2][:]).then_inc(s_od, 16)
    p.y.wait_ge(s_od, 16 * NCH)
    p.build()
    fscope.close()
    st.close()
    return nc


def host_ffn1_inputs(d, core, xnbf, logitsP, x1T, s1):
    return {
        "xnbf": xnbf, "logitsP": logitsP,
        "w1": np.ascontiguousarray(d["L1_ffn_w1"].reshape(16, 8, 128, 1024)),
        "w2": np.ascontiguousarray(d["L1_ffn_w2"].reshape(16, 8, 128, 1024)),
        "w3": np.ascontiguousarray(d["L1_ffn_w3"].reshape(16, 8, 128, 1024)),
        "b1r": np.ascontiguousarray(d["L1_ffn_b1"].reshape(1, 16)),
        "b2r": np.ascontiguousarray(d["L1_ffn_b2"].reshape(1, 16)),
        "b3c": np.ascontiguousarray(d["L1_ffn_b3"].reshape(16, 1)),
        "sel": np.ascontiguousarray(
            np.repeat(np.eye(16, dtype=np.float32), 128, axis=1)),
        "x1T": x1T, "s1": s1,
    }


import concourse.mybir as mybir
from concourse.alu_op_type import AluOpType as OP

X = mybir.AxisListType.X


def barrier(p, st, nc, name):
    s_b = sem(st, nc, name)
    p.y.nop().then_inc(s_b, 1)
    for e in (p.t, p.s, p.v, p.g, p.y):
        e.wait_ge(s_b, 1)


def emit_prep(p, st, nc, layer, io, do_main, do_kv, do_gate=None, x_is_bf16=False):
    NE = 14 if layer == 0 else 16
    if do_gate is None:
        do_gate = do_main
    xT, wqkvT, gwT, ident = io["xT"], io["wqkvT"], io["gwT"], io["ident"]

    x_scope = ExitStack()
    x_sb = [sb(x_scope, nc, f"x{d}", [128, NTOK], F32, side="right") for d in range(NCH)]
    xn_sb = [sb(st, nc, f"xn{d}", [128, NTOK], F32) for d in range(NCH)]
    xnbf_sb = [sb(st, nc, f"xnbf{d}", [128, NTOK], BF16) for d in range(NCH)]
    id_sb = sb(st, nc, "idf", [128, 128], F32)
    s_ld = sem(st, nc, "ld")
    s_fin = sem(st, nc, "fin")
    fin = 0
    ld = 0
    for d in range(NCH):
        eng = p.g if x_is_bf16 else p.y
        eng.dma_start(x_sb[d][:], xT[d]).then_inc(s_ld, 16)
        ld += 16
    p.g.dma_start(id_sb[:], ident).then_inc(s_ld, 16)
    ld += 16
    if layer == 0:
        gc_sb = [sb(st, nc, f"gc{d}", [128, 1], F32) for d in range(NCH)]
        bc_sb = [sb(st, nc, f"bc{d}", [128, 1], F32) for d in range(NCH)]
        for d in range(NCH):
            p.g.dma_start(gc_sb[d][:], io["gcol"][d]).then_inc(s_ld, 16)
            p.g.dma_start(bc_sb[d][:], io["bcol"][d]).then_inc(s_ld, 16)
            ld += 32
    elif do_gate:
        bg_sb = sb(st, nc, "bgsb", [NE, 1], F32)
        p.g.dma_start(bg_sb[:], io["bgc"]).then_inc(s_ld, 16)
        ld += 16
    for e in (p.s, p.v, p.t):
        e.wait_ge(s_ld, ld)

    ln1_scope = ExitStack()
    s_n1, n1 = emit_ln(p, ln1_scope, st, "ln1", [x_sb[d][:] for d in range(NCH)],
                       dst_f32=xn_sb, dst_bf=xnbf_sb)
    if do_main and ("o_xn" in io or "o_xnbf" in io):
        p.y.wait_ge(s_n1, n1)
        for d in range(NCH):
            if "o_xn" in io:
                p.y.dma_start(io["o_xn"][d], xn_sb[d][:]).then_inc(s_fin, 16)
                fin += 16
            if "o_xnbf" in io:
                p.y.dma_start(io["o_xnbf"][d], xnbf_sb[d][:]).then_inc(s_fin, 16)
                fin += 16

    if layer == 0:
        for e in (p.s, p.v, p.t):
            e.wait_ge(s_n1, n1)
        ln1_scope.close()
        x_scope.close()
        xn2bf_sb = [sb(st, nc, f"xn2bf{d}", [128, NTOK], BF16) for d in range(NCH)]
        ln2_scope = ExitStack()
        s_n2, n2 = emit_ln(p, ln2_scope, st, "ln2", [xn_sb[d][:] for d in range(NCH)],
                           dst_bf=xn2bf_sb, gamma=[gc_sb[d][:] for d in range(NCH)],
                           beta=[bc_sb[d][:] for d in range(NCH)])
        qkv_src = xn2bf_sb
        s_rdy, nrdy = s_n2, n2
        end_scope = ln2_scope
    else:
        qkv_src = xnbf_sb
        s_rdy, nrdy = s_n1, n1
        end_scope = ln1_scope

    for e in (p.s, p.v, p.t):
        e.wait_ge(s_rdy, nrdy)
    end_scope.close()
    if layer != 0:
        x_scope.close()

    w_sb = [sb(st, nc, f"w{d}", [128, 3072], BF16) for d in range(NCH)]
    gw_sb = [sb(st, nc, f"gw{d}", [128, NE], F32) for d in range(NCH)]
    p.g.wait_ge(s_rdy, nrdy)
    for d in range(NCH):
        p.g.dma_start(w_sb[d][:], wqkvT[d]).then_inc(s_ld, 16)
        ld += 16
        if do_gate:
            p.g.dma_start(gw_sb[d][:], gwT[d]).then_inc(s_ld, 16)
            ld += 16
    LD_W = ld
    for e in (p.s, p.v, p.t):
        e.wait_ge(s_ld, LD_W)

    qkv_scope = ExitStack()
    ev_qk = StagedEvac(p, qkv_scope, st, "eqk")

    if do_main:
        def evac_q(oc, blk, bank, s_mm, mmcnt):
            return ev_qk(bank[:], s_mm, mmcnt, io["o_q"][oc % 8, :, cols(blk)],
                         scale=0.125)
        with ExitStack() as fm1:
            lf = emit_fm_matmul(p, fm1, st, "q", list(range(8)), NCH,
                           lambda d, oc: w_sb[d][:, oc*128:(oc+1)*128],
                           lambda d, blk: qkv_src[d][:, cols(blk)], evac_q)
            p.t.wait_ge(lf[0], lf[1])
    last_free = None
    if do_kv:
        def evac_k(oc, blk, bank, s_mm, mmcnt):
            return ev_qk(bank[:], s_mm, mmcnt, io["o_k"][oc % 8, :, cols(blk)])
        with ExitStack() as fm2:
            last_free = emit_fm_matmul(p, fm2, st, "k", list(range(8, 16)), NCH,
                           lambda d, oc: w_sb[d][:, oc*128:(oc+1)*128],
                           lambda d, blk: qkv_src[d][:, cols(blk)], evac_k)
            p.t.wait_ge(last_free[0], last_free[1])

        ones_v = sb(qkv_scope, nc, "onesv", [128, 1040], BF16)
        s_ov = sem(st, nc, "ov")
        p.v.memset(ones_v[:], 1.0).then_inc(s_ov, 1)
        p.y.wait_ge(s_ov, 1)
        for tt in range(NTILE):
            p.y.dma_start(io["o_v"][tt], ones_v[:]).then_inc(s_ov, 16)
        p.y.wait_ge(s_ov, 1 + 16 * NTILE)

        def evac_v(tt, vb, bank, s_mm, mmcnt):
            dst = io["o_v"][tt].rearrange("p (h c) -> p h c", c=65)[:, vb*8:(vb+1)*8, 0:64]
            return ev_qk(bank[:], s_mm, mmcnt, dst)
        with ExitStack() as fm3:
            last_free = emit_fm_matmul(p, fm3, st, "v", list(range(NTILE)), NCH,
                           lambda d, tt: qkv_src[d][:, tt*128:(tt+1)*128],
                           lambda d, vb: w_sb[d][:, 2048+vb*TB:2048+(vb+1)*TB],
                           evac_v, nblk=2)
            p.t.wait_ge(last_free[0], last_free[1])
    ev_qk.flush_wait()
    p.v.wait_ge(ev_qk.s_dma, 16 * ev_qk.g)
    p.s.wait_ge(ev_qk.s_dma, 16 * ev_qk.g)
    qkv_scope.close()

    if do_gate:
        gscope = ExitStack()
        ps_g = [ps(gscope, nc, f"gps{i}", [128, NE]) for i in range(2)]
        ps_t = [ps(gscope, nc, f"gpt{i}", [128, 128]) for i in range(2)]
        s_gmm = sem(st, nc, "gmm")
        s_gv = sem(st, nc, "gv")
        s_gt = sem(st, nc, "gt")
        gateT_sb = sb(gscope, nc, "gateT", [NE, NTOK], F32)
        coef = [sb(gscope, nc, f"coef{i}", [128, NE], F32) for i in range(2)]
        work = {nm: sb(gscope, nc, f"gw_{nm}", [128, NE], F32)
                for nm in ("e", "pp", "pm", "ge2", "pw")}
        colw = {nm: sb(gscope, nc, f"gc_{nm}", [128, 1], F32)
                for nm in ("m", "negm", "sum", "rsum", "m1", "m2", "den", "rden")}
        VPT = 3 if layer == 0 else 1
        for tt in range(NTILE):
            bank = ps_g[tt % 2]
            if tt >= 2:
                p.t.wait_ge(s_gv, VPT * (tt - 2) + VPT)
            for d in range(NCH):
                h = p.t.matmul(bank[:], xn_sb[d][:, tt*128:(tt+1)*128],
                               gw_sb[d][:], start=(d == 0), stop=(d == NCH - 1))
            h.then_inc(s_gmm, 1)
            cf = coef[tt % 2]
            p.v.wait_ge(s_gmm, 2 * tt + 1)
            if layer == 0:
                p.v.reduce_max(colw["m"][:], bank[:], axis=X)
                p.v.tensor_scalar(colw["negm"][:], colw["m"][:], -1.0, None, OP.mult)\
                    .then_inc(s_gv, 1)
                p.s.wait_ge(s_gv, VPT * tt + 1)
                p.s.activation(work["e"][:], bank[:], AF.Exp, bias=colw["negm"][:])\
                    .then_inc(s_gv, 1)
                p.v.wait_ge(s_gv, VPT * tt + 2)
                p.v.reduce_sum(colw["sum"][:], work["e"][:], axis=X)
                p.v.reciprocal(colw["rsum"][:], colw["sum"][:])
                p.v.tensor_scalar(work["pp"][:], work["e"][:], colw["rsum"][:], None, OP.mult)
                p.v.reduce_max(colw["m1"][:], work["pp"][:], axis=X)
                p.v.tensor_scalar(work["ge2"][:], work["pp"][:], colw["m1"][:], None, OP.is_ge)
                p.v.tensor_tensor(work["pm"][:], work["pp"][:], work["ge2"][:], OP.mult)
                p.v.tensor_tensor(work["pm"][:], work["pp"][:], work["pm"][:], OP.subtract)
                p.v.reduce_max(colw["m2"][:], work["pm"][:], axis=X)
                p.v.tensor_tensor(colw["den"][:], colw["m1"][:], colw["m2"][:], OP.add)
                p.v.reciprocal(colw["rden"][:], colw["den"][:])
                p.v.tensor_scalar(work["ge2"][:], work["pp"][:], colw["m2"][:], None, OP.is_ge)
                p.v.tensor_scalar(work["pw"][:], work["pp"][:], colw["rden"][:], None, OP.mult)
                p.v.tensor_tensor(cf[:], work["pw"][:], work["ge2"][:], OP.mult)\
                    .then_inc(s_gv, 1)
            else:
                p.v.tensor_copy(cf[:], bank[:]).then_inc(s_gv, 1)
            tbk = ps_t[tt % 2]
            p.t.wait_ge(s_gv, VPT * tt + VPT)
            if tt >= 2:
                p.t.wait_ge(s_gt, tt - 1)
            p.t.transpose(tbk[:NE, :], cf[:], id_sb[:]).then_inc(s_gmm, 1)
            p.v.wait_ge(s_gmm, 2 * tt + 2)
            if layer == 0:
                hh = p.v.tensor_copy(gateT_sb[:, tt*128:(tt+1)*128], tbk[:NE, :])
            else:
                hh = p.v.tensor_scalar(gateT_sb[:, tt*128:(tt+1)*128], tbk[:NE, :],
                                       bg_sb[:], None, OP.add)
            hh.then_inc(s_gt, 1)
        p.y.wait_ge(s_gt, NTILE)
        p.y.dma_start(io["o_gate"], gateT_sb[:]).then_inc(s_fin, 16)
        fin += 16
        p.y.wait_ge(s_fin, fin)
        gscope.close()
    else:
        if fin:
            p.y.wait_ge(s_fin, fin)


def emit_attn(p, st, nc, layer, io):
    alibi = layer == 1
    moffs, mcols = mask_offsets(alibi)
    ncols = sum(KT for (_, _, _, _, _, KT, _) in attn_schedule(alibi))
    qj_sb = [[sb(st, nc, f"q{i}_{d}", [128, TB], BF16) for d in range(NCH)]
             for i in range(2)]
    kH_sb = [[sb(st, nc, f"k{hf}_{d}", [128, NTOK], BF16) for d in range(NCH)]
             for hf in range(2)]
    v_sb = [sb(st, nc, f"v{t}", [128, 1040], BF16) for t in range(32)]
    import os as _os2
    WRES = _os2.environ.get("ATTN_WRES")
    w_sb = None
    wsl = None
    if WRES:
        w_sb = [sb(st, nc, f"w{d}", [128, D], BF16) for d in range(NCH)]
    else:
        wsl = [[sb(st, nc, f"wsl{i}_{d}", [128, 128], BF16) for d in range(NCH)]
               for i in range(2)]
    bo_sb = [sb(st, nc, f"bo{d}", [128, 1], F32) for d in range(NCH)]
    bc_sb = sb(st, nc, "bc", [128, ncols], F32) if alibi else None
    mk_sb = sb(st, nc, "mk", [128, mcols], BF16)
    ones_f = sb(st, nc, "onesf", [1, 64], F32)
    abuf = [sb(st, nc, f"ab{i}", [128, TB], BF16) for i in range(4)]
    rrow = sb(st, nc, "rrow", [1, TB], F32)
    rb_sb = sb(st, nc, "rb", [64, TB], F32)
    s_buf = [sb(st, nc, f"sb{d}", [128, TB], BF16) for d in range(NCH)]

    s_ld = sem(st, nc, "ld")
    s_sc = sem(st, nc, "sc")
    s_exp = sem(st, nc, "exp")
    s_msk = sem(st, nc, "msk")
    s_av = sem(st, nc, "av")
    s_fz = sem(st, nc, "fz")
    s_pj = sem(st, nc, "pj")
    s_vm = sem(st, nc, "vm")

    s_qld = sem(st, nc, "qld")
    qld = [0]
    def load_qJ(Jn):
        for d in range(NCH):
            p.y.dma_start(qj_sb[Jn % 2][d][:], io["q"][d, :, Jn*TB:(Jn+1)*TB])\
                .then_inc(s_qld, 16)
            qld[0] += 16
    import os as _os
    NOK = _os.environ.get("ATTN_NO_K"); NOV = _os.environ.get("ATTN_NO_V")
    ld = 0
    load_qJ(0)
    load_qJ(1)
    for d in range(NCH):
        for hf, srct in ((0, io["kA"]), (1, io["kB"])):
            p.y.dma_start(kH_sb[hf][d][:], srct[d]).then_inc(s_ld, 16)
            ld += 16
    p.v.memset(ones_f[:], 1.0).then_inc(s_vm, 1)
    for t in range(32):
        half = 0 if (t % 4) in (0, 3) else 1
        src = io["vA"] if half == 0 else io["vB"]
        lt = tmap(half).index(t)
        p.g.dma_start(v_sb[t][:], src[lt]).then_inc(s_ld, 16)
        ld += 16
    s_wld = sem(st, nc, "wld")
    wldc = [0]
    if WRES:
        for d in range(NCH):
            p.g.dma_start(w_sb[d][:], io["woutT"][d]).then_inc(s_ld, 16); ld += 16
    for d in range(NCH):
        p.g.dma_start(bo_sb[d][:], io["boutcol"][d]).then_inc(s_ld, 16); ld += 16
    if alibi:
        p.g.dma_start(bc_sb[:], io["biascols"]).then_inc(s_ld, 16); ld += 16
    p.g.dma_start(mk_sb[:], io["maskbank"]).then_inc(s_ld, 16); ld += 16
    for e in (p.t, p.s, p.v):
        e.wait_ge(s_ld, ld)
    import os as _os
    if _os.environ.get("ATTN_LOADS_ONLY"):
        p.y.wait_ge(s_ld, ld)
        p.y.wait_ge(s_qld, 256)
        return

    pscope = ExitStack()
    ps_sc = [ps(pscope, nc, f"psc{i}", [128, TB]) for i in range(3)]
    ps_oo = [ps(pscope, nc, f"pso{i}", [65, TB]) for i in range(2)]
    ps_b = ps(pscope, nc, "psb", [64, TB])
    ps_pj = [ps(pscope, nc, f"pspj{i}", [128, TB]) for i in range(2)]
    ev = StagedEvac(p, pscope, st, "es", n_stages=2, dtype=F32)

    C = dict(sc=0, exp=0, msk=0, av=0, fz=0, pj=0, g=0)
    frees_pj = []
    colix = [0]
    sched = list(attn_schedule(alibi))
    si = [0]
    tinfo = {}

    def emit_front(J, items, tasks, ti):
        ii, kt = tasks[ti]
        (J_, h, qb, j0, off, KT, NM) = items[ii]
        hp, hh = h // 2, h % 2
        rows = slice(hh * 64, hh * 64 + 64)
        qcols = slice(off - J * 512, off - J * 512 + qb)
        g = C["g"]
        bank = ps_sc[g % 3]
        if g >= 3:
            p.t.wait_ge(s_exp, g - 2)
        khalf = 0 if (kt % 4) in (0, 3) else 1
        klt = tmap(khalf).index(kt)
        p.t.matmul(bank[:, :qb], kH_sb[khalf][hp][rows, klt*128:(klt+1)*128],
                   qj_sb[J % 2][hp][rows, qcols], start=True, stop=True)\
            .then_inc(s_sc, 1)
        C["sc"] += 1
        ab = abuf[g % 4]
        p.s.wait_ge(s_sc, C["sc"])
        if g >= 4:
            p.s.wait_ge(s_av, g - 3)
        bias = bc_sb[:, colix[0]:colix[0]+1] if alibi else 0.0
        p.s.activation(ab[:, :qb], bank[:, :qb], AF.Exp, bias=bias)\
            .then_inc(s_exp, 1)
        C["exp"] += 1
        colix[0] += 1
        masked = kt >= KT - NM
        mtgt = None
        if masked:
            rel = kt - (KT - NM)
            par = j0 % 2 if qb <= 128 else 0
            h64 = (off % 128) // 64 if qb == 64 else 0
            mo = moffs[(qb, rel, par, h64)]
            p.v.wait_ge(s_exp, C["exp"])
            p.v.tensor_tensor(ab[:, :qb], ab[:, :qb],
                              mk_sb[:, mo:mo+qb], OP.mult)\
                .then_inc(s_msk, 1)
            C["msk"] += 1
            mtgt = C["msk"]
        tinfo[ti] = (C["exp"], mtgt, ab)
        C["g"] += 1

    def emit_pv(items, tasks, ti, sid):
        ii, kt = tasks[ti]
        (J_, h, qb, j0, off, KT, NM) = items[ii]
        etgt, mtgt, ab = tinfo[ti]
        if mtgt is not None:
            p.t.wait_ge(s_msk, mtgt)
        else:
            p.t.wait_ge(s_exp, etgt)
        if kt == 0 and sid >= 2:
            p.t.wait_ge(s_fz, 3 * (sid - 1))
        p.t.matmul(ps_oo[sid % 2][:, :qb], v_sb[kt][:, h*65:(h+1)*65],
                   ab[:, :qb], start=(kt == 0), stop=(kt == KT - 1))\
            .then_inc(s_av, 1)
        C["av"] += 1

    def emit_finalize(J, item, sid):
        (J_, h, qb, j0, off, KT, NM) = item
        hp, hh = h // 2, h % 2
        rows = slice(hh * 64, hh * 64 + 64)
        ob = ps_oo[sid % 2]
        p.v.wait_ge(s_av, C["av"])
        p.v.reciprocal(rrow[:, :qb], ob[64:65, :qb]).then_inc(s_fz, 1)
        C["fz"] += 1
        p.t.wait_ge(s_fz, C["fz"])
        if sid > 0:
            p.t.wait_ge(s_fz, 3 * sid)
        p.t.matmul(ps_b[:, :qb], ones_f[:], rrow[:, :qb], start=True,
                   stop=True).then_inc(s_sc, 1)
        C["sc"] += 1
        p.v.wait_ge(s_sc, C["sc"])
        p.v.tensor_copy(rb_sb[:, :qb], ps_b[:, :qb]).then_inc(s_fz, 1)
        C["fz"] += 1
        scols = slice(off - J * 512, off - J * 512 + qb)
        p.v.tensor_tensor(s_buf[hp][rows, scols], ob[0:64, :qb],
                          rb_sb[:, :qb], OP.mult).then_inc(s_fz, 1)
        C["fz"] += 1
        si[0] += 1

    for J in range(4):
        p.t.wait_ge(s_qld, 128 * (J + 1))
        items = [it for it in sched if it[0] == J]
        tasks = [(ii, kt) for ii, it in enumerate(items) for kt in range(it[5])]
        si_J = si[0]
        qptr = 0
        for ti in range(len(tasks)):
            while qptr < min(ti + 3, len(tasks)):
                emit_front(J, items, tasks, qptr)
                qptr += 1
            ii, kt = tasks[ti]
            sid = si_J + ii
            emit_pv(items, tasks, ti, sid)
            if kt == items[ii][5] - 1:
                emit_finalize(J, items[ii], sid)
        p.t.wait_ge(s_fz, 3 * si[0])
        for oc in range(NCH):
            bank = ps_pj[oc % 2]
            gp = J * NCH + oc
            if not WRES:
                if gp >= 2:
                    p.g.wait_ge(s_pj, gp - 1)
                for e in range(NCH):
                    p.g.dma_start(wsl[gp % 2][e][:],
                                  io["woutT"][e, :, oc*128:(oc+1)*128]).then_inc(s_wld, 16)
                    wldc[0] += 16
            if gp >= 2:
                fs, fv = frees_pj[gp - 2]
                p.t.wait_ge(fs, fv)
            if not WRES:
                p.t.wait_ge(s_wld, wldc[0])
            for e in range(NCH):
                lhsw = w_sb[e][:, oc*128:(oc+1)*128] if WRES else wsl[gp % 2][e][:]
                hmm = p.t.matmul(bank[:], lhsw,
                                 s_buf[e][:], start=(e == 0), stop=(e == NCH - 1))
            hmm.then_inc(s_pj, 1)
            C["pj"] += 1
            frees_pj.append(ev(bank[:], s_pj, C["pj"], io["o_s"][oc, :, J*512:(J+1)*512],
                               bias=bo_sb[oc][:]))
        p.v.wait_ge(s_pj, C["pj"])
        if J + 2 < 4:
            p.y.wait_ge(s_pj, C["pj"])
            load_qJ(J + 2)
    ev.flush_wait()
    pscope.close()


def emit_moe0(p, st, nc, io):
    NE = 14
    xnbf_sb = [sb(st, nc, f"xnbf{d}", [128, NTOK], BF16) for d in range(NCH)]
    gacc = [sb(st, nc, f"ga{d}", [128, NTOK], F32) for d in range(NCH)]
    cf_sb = sb(st, nc, "cfsb", [NE, NTOK], F32)
    b1_sb = sb(st, nc, "b1sb", [128, NE * 16], F32)
    b2_sb = sb(st, nc, "b2sb", [NE, D], F32)
    ones_f = sb(st, nc, "onesf", [1, 128], F32)
    sel_sb = sb(st, nc, "selsb", [14, 128 * 14], F32)
    cb_sb = sb(st, nc, "cb", [128, TB], F32)
    tmp_sb = sb(st, nc, "tmp", [128, TB], F32)
    s_ld = sem(st, nc, "ld")
    s_w = sem(st, nc, "w")
    s_mm = sem(st, nc, "mm")
    s_ev = sem(st, nc, "ev")
    s_acc = sem(st, nc, "acc")
    s_cb = sem(st, nc, "cbk")
    p.v.memset(ones_f[:], 1.0)
    ld = 0
    for d in range(NCH):
        p.y.dma_start(xnbf_sb[d][:], io["xnbf"][d]).then_inc(s_ld, 16); ld += 16
    p.y.dma_start(cf_sb[:], io["coeffT"]).then_inc(s_ld, 16); ld += 16
    p.y.dma_start(b1_sb[:], io["b1c"]).then_inc(s_ld, 16); ld += 16
    p.y.dma_start(b2_sb[:], io["b2r"]).then_inc(s_ld, 16); ld += 16
    p.y.dma_start(sel_sb[:], io["sel"]).then_inc(s_ld, 16); ld += 16
    for e in (p.t, p.s, p.v):
        e.wait_ge(s_ld, ld)

    iscope = ExitStack()
    xnb = [sb(iscope, nc, f"xnb{i}", [128, NTOK], F32, side="right") for i in range(2)]
    with ExitStack() as sti:
        ps_i = [ps(sti, nc, f"pi{i}", [128, TB]) for i in range(4)]
        g = 0
        for d in range(NCH):
            if d >= 2:
                p.y.wait_ge(s_acc, (d - 1) * NBLK)
            p.y.dma_start(xnb[d % 2][:], io["xn"][d]).then_inc(s_ld, 16)
            ld += 16
            p.v.wait_ge(s_ld, ld)
            for blk in range(NBLK):
                bank = ps_i[g % 4]
                if g >= 4:
                    p.t.wait_ge(s_acc, g - 3)
                p.t.matmul(bank[:], b2_sb[:, d*128:(d+1)*128], cf_sb[:, cols(blk)],
                           start=True, stop=True).then_inc(s_mm, 1)
                p.v.wait_ge(s_mm, g + 1)
                p.v.tensor_tensor(gacc[d][:, cols(blk)], xnb[d % 2][:, cols(blk)],
                                  bank[:], OP.add).then_inc(s_acc, 1)
                g += 1
        NB_INIT = g
    iscope.close()

    wscope = ExitStack()
    w1_sb = [sb(wscope, nc, f"w1_{d}", [128, 2048], BF16) for d in range(NCH)]
    w2_sb = [sb(wscope, nc, f"w2_{f}", [128, D], BF16) for f in range(16)]
    h_sb = [sb(wscope, nc, f"h{f}", [128, TB], BF16) for f in range(16)]
    ps_h = [ps(wscope, nc, f"ph{i}", [128, TB]) for i in range(2)]
    ps_o = [ps(wscope, nc, f"po{i}", [128, TB]) for i in range(2)]
    ps_c = ps(wscope, nc, "pc", [128, TB])
    wld = [0]; ghc = [0]; omc = [0]; acc2 = [NB_INIT]; cbc = [0]; mmc = [NB_INIT]

    for e in range(NE):
        p.g.wait_ge(s_ev, 64 * e)
        if e == 0:
            p.g.wait_ge(s_acc, NB_INIT)
        for d in range(NCH):
            p.g.dma_start(w1_sb[d][:], io["w1"][e, d]).then_inc(s_w, 16); wld[0] += 16
        w1done = wld[0]
        p.g.wait_ge(s_acc, NB_INIT + e * 32)
        for f in range(16):
            p.g.dma_start(w2_sb[f][:], io["w2"][e, f]).then_inc(s_w, 16); wld[0] += 16
        walldone = wld[0]
        for tb in range(NBLK):
            if cbc[0] > 0:
                p.t.wait_ge(s_cb, 2 * cbc[0])
            p.t.matmul(ps_c[:], sel_sb[:, e*128:(e+1)*128], cf_sb[:, cols(tb)],
                       start=True, stop=True).then_inc(s_mm, 1)
            mmc[0] += 1
            p.v.wait_ge(s_mm, mmc[0])
            p.v.tensor_copy(cb_sb[:], ps_c[:]).then_inc(s_cb, 1)
            for fc in range(16):
                bank = ps_h[fc % 2]
                gh = ghc[0]
                if fc == 0 and tb == 0:
                    p.t.wait_ge(s_w, w1done)
                if gh >= 2:
                    p.t.wait_ge(s_ev, gh - 1)
                for d in range(NCH):
                    h = p.t.matmul(bank[:], w1_sb[d][:, fc*128:(fc+1)*128],
                                   xnbf_sb[d][:, cols(tb)],
                                   start=(d == 0), stop=(d == NCH - 1))
                h.then_inc(s_mm, 1)
                mmc[0] += 1
                p.s.wait_ge(s_mm, mmc[0])
                p.s.activation(h_sb[fc][:], bank[:], AF.Gelu,
                               bias=b1_sb[:, e*16+fc:e*16+fc+1]).then_inc(s_ev, 1)
                ghc[0] += 1
            for dd in range(NCH):
                bank = ps_o[dd % 2]
                om = omc[0]
                if dd == 0:
                    p.t.wait_ge(s_ev, ghc[0])
                    if tb == 0:
                        p.t.wait_ge(s_w, walldone)
                if om >= 2:
                    p.t.wait_ge(s_acc, acc2[0] - 1)
                for f in range(16):
                    h = p.t.matmul(bank[:], w2_sb[f][:, dd*128:(dd+1)*128],
                                   h_sb[f][:], start=(f == 0), stop=(f == 15))
                h.then_inc(s_mm, 1)
                mmc[0] += 1
                p.v.wait_ge(s_mm, mmc[0])
                p.v.tensor_tensor(tmp_sb[:], bank[:], cb_sb[:], OP.mult)
                p.v.tensor_tensor(gacc[dd][:, cols(tb)], gacc[dd][:, cols(tb)],
                                  tmp_sb[:], OP.add).then_inc(s_acc, 1)
                acc2[0] += 1
                omc[0] += 1
            p.v.nop().then_inc(s_cb, 1)
            cbc[0] += 1
    wscope.close()

    fscope = ExitStack()
    xb = [sb(fscope, nc, f"xb{i}", [128, NTOK], F32, side="right") for i in range(2)]
    sb_ = [sb(fscope, nc, f"sbb{i}", [128, NTOK], F32, side="right") for i in range(2)]
    gb = sb(fscope, nc, "gb", [128, NTOK], F32, side="right")
    x1b = [sb(fscope, nc, f"x1b{i}", [128, NTOK], F32, side="right") for i in range(2)]
    echo = sb(fscope, nc, "echo", [128, 1], F32, side="right")
    s_clip = sem(st, nc, "clip")
    s_silu = sem(st, nc, "silu")
    s_x1 = sem(st, nc, "x1")
    s_od = sem(st, nc, "od")
    p.y.wait_ge(s_acc, acc2[0])
    for d in range(NCH):
        if d >= 2:
            p.y.wait_ge(s_x1, d - 1)
        p.y.dma_start(xb[d % 2][:], io["xT"][d]).then_inc(s_ld, 16)
        p.y.dma_start(sb_[d % 2][:], io["s0"][d]).then_inc(s_ld, 16)
        ld += 32
        p.v.wait_ge(s_ld, ld)
        p.v.tensor_scalar(gacc[d][:], gacc[d][:], 100.0, -100.0, OP.min, OP.max)\
            .then_inc(s_clip, 1)
        p.s.wait_ge(s_clip, d + 1)
        if d >= 1:
            p.s.wait_ge(s_x1, d)
        p.s.activation(gb[:], gacc[d][:], AF.Silu).then_inc(s_silu, 1)
        p.v.wait_ge(s_silu, d + 1)
        if d >= 2:
            p.v.wait_ge(s_od, 16 * (d - 1))
        p.v.tensor_tensor(gb[:], gb[:], sb_[d % 2][:], OP.mult)
        p.v.tensor_tensor(x1b[d % 2][:], gb[:], xb[d % 2][:], OP.add)
        p.v.tensor_copy(echo[:], x1b[d % 2][:, NTOK-1:NTOK]).then_inc(s_x1, 1)
        p.y.wait_ge(s_x1, d + 1)
        p.y.dma_start(io["o_x1"][d], x1b[d % 2][:]).then_inc(s_od, 16)
    p.y.wait_ge(s_od, 16 * NCH)
    fscope.close()


def emit_exchange(p, st, nc, io):
    xscope = ExitStack()
    x1bf = [sb(xscope, nc, f"x1bf{d}", [128, NTOK], BF16) for d in range(NCH)]
    stg = [sb(xscope, nc, f"stg{i}", [128, NTOK], BF16) for i in range(2)]
    acc = sb(xscope, nc, "xacc", [128, NTOK], F32)
    msl = sb(xscope, nc, "msl", [128, 8], F32)
    mpr = sb(xscope, nc, "mpr", [128, 8], F32)
    s_ld = sem(st, nc, "ld")
    s_st = sem(st, nc, "stg")
    s_cc = sem(st, nc, "cc")
    s_rd = sem(st, nc, "rd")
    ld = 0
    p.g.dma_start(msl[:], io["mslot"]).then_inc(s_ld, 16); ld += 16
    p.g.dma_start(mpr[:], io["mpair"]).then_inc(s_ld, 16); ld += 16
    for d in range(NCH):
        p.g.dma_start(x1bf[d][:], io["x1"][d]).then_inc(s_ld, 16); ld += 16
    p.v.wait_ge(s_ld, ld)
    nst = 0
    for s in range(8):
        for d in range(NCH):
            p.v.wait_ge(s_st, 0 if nst < 2 else 16 * (nst - 1))
            p.v.tensor_scalar(stg[nst % 2][:], x1bf[d][:], msl[:, s:s+1], None,
                              OP.mult).then_inc(s_cc, 1)
            nst += 1
            p.y.wait_ge(s_cc, nst)
            p.y.dma_start(io["arin"][s, d], stg[(nst - 1) % 2][:]).then_inc(s_st, 16)
    p.g.wait_ge(s_st, 16 * nst)
    p.g.collective_compute("AllReduce", mybir.AluOpType.add,
                           replica_groups=[list(range(8))],
                           ins=[io["arin_t"].ap().opt()],
                           outs=[io["arout_t"].ap().opt()]).then_inc(s_cc, 16)
    p.g.wait_ge(s_cc, nst + 16)
    s_cs = sem(st, nc, "cs")
    nrd = 0
    for d in range(NCH):
        for s in range(8):
            if nrd >= 2:
                p.g.wait_ge(s_cs, nrd - 1)
            p.g.dma_start(stg[s % 2][:], io["arout"][s, d]).then_inc(s_ld, 16)
            ld += 16
            p.v.wait_ge(s_ld, ld)
            if s == 0:
                h = p.v.tensor_scalar(acc[:], stg[s % 2][:], mpr[:, s:s+1], None,
                                      OP.mult)
            else:
                h = p.v.scalar_tensor_tensor(acc[:], stg[s % 2][:], mpr[:, s:s+1],
                                             acc[:], OP.mult, OP.add)
            h.then_inc(s_cs, 1)
            if s == 7:
                p.v.tensor_copy(x1bf[d][:], acc[:]).then_inc(s_rd, 1)
            nrd += 1
        p.y.wait_ge(s_rd, d + 1)
        p.y.dma_start(io["o_x1pair"][d], x1bf[d][:]).then_inc(s_st, 16)
        nst += 1
    p.y.wait_ge(s_st, 16 * nst)
    xscope.close()


def emit_combine(p, st, nc, io):
    cscope = ExitStack()
    a = sb(cscope, nc, "ca", [128, NTOK], BF16)
    b = sb(cscope, nc, "cb2", [128, NTOK], BF16)
    o = sb(cscope, nc, "co", [128, NTOK], BF16)
    t1 = sb(cscope, nc, "ct", [128, NTOK], BF16)
    mh = sb(cscope, nc, "cmh", [128, 2], F32)
    s_ld = sem(st, nc, "ld")
    s_wr = sem(st, nc, "wr")
    s_cs = sem(st, nc, "cs")
    p.g.dma_start(mh[:], io["mh"]).then_inc(s_ld, 16)
    ld = [16]
    nio = [0]

    def one(dst, own_ap, pair_ap, mown_col, width):
        if nio[0] >= 1:
            p.g.wait_ge(s_cs, nio[0])
        p.g.dma_start(a[:, :width], own_ap).then_inc(s_ld, 16)
        p.g.dma_start(b[:, :width], pair_ap).then_inc(s_ld, 16)
        ld[0] += 32
        p.v.wait_ge(s_ld, ld[0])
        if nio[0] >= 1:
            p.v.wait_ge(s_wr, 16 * nio[0])
        p.v.tensor_scalar(t1[:, :width], b[:, :width], mh[:, 1-mown_col:2-mown_col],
                          None, OP.mult)
        p.v.scalar_tensor_tensor(o[:, :width], a[:, :width], mh[:, mown_col:mown_col+1],
                                 t1[:, :width], OP.mult, OP.add).then_inc(s_cs, 1)
        nio[0] += 1
        p.v.wait_ge(s_cs, nio[0])
        p.y.wait_ge(s_cs, nio[0])
        p.y.dma_start(dst, o[:, :width]).then_inc(s_wr, 16)

    for d in range(NCH):
        one(io["o_kA"][d], io["k_own"][d], io["k_pair"][d], 0, NTOK)
        one(io["o_kB"][d], io["k_own"][d], io["k_pair"][d], 1, NTOK)
    for t in range(NTILE):
        one(io["o_vA"][t], io["v_own"][t], io["v_pair"][t], 0, D)
        one(io["o_vB"][t], io["v_own"][t], io["v_pair"][t], 1, D)
    p.y.wait_ge(s_wr, 16 * nio[0])


def emit_ffn1(p, st, nc, io, BIS=40):
    NE = 16
    S = 4096
    xnbf_sb = [sb(st, nc, f"xnbf{d}", [128, NTOK], BF16) for d in range(NCH)]
    gacc = [sb(st, nc, f"ga{d}", [128, NTOK], F32) for d in range(NCH)]
    coefP = sb(st, nc, "coefP", [128, S], F32)
    sel_sb = sb(st, nc, "selsb", [NE, 128 * NE], F32)
    b3c_sb = sb(st, nc, "b3csb", [NE, 1], F32)
    ones_f = sb(st, nc, "onesf", [1, 128], F32)
    cb_sb = sb(st, nc, "cb", [128, TB], F32)
    tmp_sb = sb(st, nc, "tmp", [128, TB], F32)
    B1c = sb(st, nc, "B1c", [128, NE], F32)
    B2c = sb(st, nc, "B2c", [128, NE], F32)
    rowb = sb(st, nc, "rowb", [1, TB], F32)
    colw = {nm: sb(st, nc, f"c_{nm}", [128, 1], F32)
            for nm in ("mx", "negmx", "sum", "rsum", "lo", "hi", "u", "t",
                       "cnt", "sel1", "nsel", "dlt")}
    s_ld = sem(st, nc, "ld")
    s_w = sem(st, nc, "w")
    s_mm = sem(st, nc, "mm")
    s_ev = sem(st, nc, "ev")
    s_acc = sem(st, nc, "acc")
    s_cb = sem(st, nc, "cbk")
    s_gate = sem(st, nc, "gate")
    p.v.memset(ones_f[:], 1.0)
    ld = 0
    for d in range(NCH):
        p.y.dma_start(xnbf_sb[d][:], io["xnbf"][d]).then_inc(s_ld, 16); ld += 16
    p.y.dma_start(sel_sb[:], io["sel"]).then_inc(s_ld, 16); ld += 16
    p.y.dma_start(b3c_sb[:], io["b3c"]).then_inc(s_ld, 16); ld += 16

    gscope = ExitStack()
    lv_sb = sb(gscope, nc, "lv", [128, S], F32, side="right")
    pv_sb = sb(gscope, nc, "pv", [128, S], F32, side="right")
    ge_sb = sb(gscope, nc, "ge", [128, S], F32, side="right")
    b1r_sb = sb(gscope, nc, "b1rsb", [1, NE], F32, side="right")
    b2r_sb = sb(gscope, nc, "b2rsb", [1, NE], F32, side="right")
    lgA = sb(gscope, nc, "lgA", [NE, NTOK], F32, side="right")
    lgB = sb(gscope, nc, "lgB", [NE, NTOK], F32, side="right")
    mh2 = sb(gscope, nc, "mh2", [128, 2], F32, side="right")
    lgt = sb(gscope, nc, "lgt", [NE, NTOK], F32, side="right")
    p.v.memset(lv_sb[:], 0.0).then_inc(s_gate, 100)
    p.y.dma_start(lgA[:], io["lgA"]).then_inc(s_ld, 16); ld += 16
    p.y.dma_start(lgB[:], io["lgB"]).then_inc(s_ld, 16); ld += 16
    p.y.dma_start(mh2[:], io["mh"]).then_inc(s_ld, 16); ld += 16
    p.y.dma_start(b1r_sb[:], io["b1r"]).then_inc(s_ld, 16); ld += 16
    p.y.dma_start(b2r_sb[:], io["b2r"]).then_inc(s_ld, 16); ld += 16
    for e in (p.t, p.s, p.v):
        e.wait_ge(s_ld, ld)
    p.v.wait_ge(s_gate, 100)
    p.v.tensor_scalar(lgt[:], lgB[:], mh2[0:NE, 1:2], None, OP.mult)
    p.v.scalar_tensor_tensor(lv_sb[0:NE, 0:NTOK], lgA[:], mh2[0:NE, 0:1],
                             lgt[:], OP.mult, OP.add)
    p.v.tensor_scalar(lgt[:], lgA[:], mh2[0:NE, 1:2], None, OP.mult)
    p.v.scalar_tensor_tensor(lv_sb[0:NE, NTOK:2*NTOK], lgB[:], mh2[0:NE, 0:1],
                             lgt[:], OP.mult, OP.add)
    with ExitStack() as stb:
        ps_bc = [ps(stb, nc, f"pbc{i}", [128, NE]) for i in range(2)]
        p.t.matmul(ps_bc[0][:], ones_f[:], b1r_sb[:], start=True, stop=True)
        p.t.matmul(ps_bc[1][:], ones_f[:], b2r_sb[:], start=True, stop=True)\
            .then_inc(s_mm, 1)
        p.v.wait_ge(s_mm, 1)
        p.v.tensor_copy(B1c[:], ps_bc[0][:])
        p.v.tensor_copy(B2c[:], ps_bc[1][:]).then_inc(s_gate, 1)
    p.v.reduce_max(colw["mx"][:], lv_sb[:], axis=X)
    p.v.tensor_scalar(colw["negmx"][:], colw["mx"][:], -1.0, None, OP.mult)\
        .then_inc(s_gate, 1)
    p.s.wait_ge(s_gate, 102)
    p.s.activation(pv_sb[:], lv_sb[:], AF.Exp, bias=colw["negmx"][:])\
        .then_inc(s_gate, 1)
    p.v.wait_ge(s_gate, 103)
    p.v.reduce_sum(colw["sum"][:], pv_sb[:], axis=X)
    p.v.reciprocal(colw["rsum"][:], colw["sum"][:])
    p.v.tensor_scalar(pv_sb[:], pv_sb[:], colw["rsum"][:], None, OP.mult)
    p.v.memset(colw["lo"][:], 0.0)
    p.v.memset(colw["hi"][:], 1.0)
    for it in range(BIS):
        p.v.tensor_tensor(colw["u"][:], colw["lo"][:], colw["hi"][:], OP.add)
        p.v.tensor_scalar(colw["t"][:], colw["u"][:], 0.5, None, OP.mult)
        p.v.tensor_scalar(ge_sb[:], pv_sb[:], colw["t"][:], None, OP.is_ge)
        p.v.reduce_sum(colw["cnt"][:], ge_sb[:], axis=X)
        p.v.tensor_scalar(colw["sel1"][:], colw["cnt"][:], 256.0, None, OP.is_ge)
        p.v.tensor_scalar(colw["nsel"][:], colw["sel1"][:], -1.0, 1.0, OP.mult, OP.add)
        p.v.tensor_tensor(colw["dlt"][:], colw["t"][:], colw["lo"][:], OP.subtract)
        p.v.tensor_tensor(colw["dlt"][:], colw["dlt"][:], colw["sel1"][:], OP.mult)
        p.v.tensor_tensor(colw["lo"][:], colw["lo"][:], colw["dlt"][:], OP.add)
        p.v.tensor_tensor(colw["dlt"][:], colw["t"][:], colw["hi"][:], OP.subtract)
        p.v.tensor_tensor(colw["dlt"][:], colw["dlt"][:], colw["nsel"][:], OP.mult)
        p.v.tensor_tensor(colw["hi"][:], colw["hi"][:], colw["dlt"][:], OP.add)
    p.v.tensor_scalar(ge_sb[:], pv_sb[:], colw["lo"][:], None, OP.is_ge)
    p.v.tensor_tensor(coefP[:], pv_sb[:], ge_sb[:], OP.mult).then_inc(s_gate, 1)

    with ExitStack() as sti:
        ps_r = ps(sti, nc, "pr", [1, TB])
        ps_b = [ps(sti, nc, f"pb{i}", [128, TB]) for i in range(2)]
        p.t.wait_ge(s_gate, 104)
        acc0 = 0
        for tb in range(NBLK):
            if tb >= 1:
                p.t.wait_ge(s_gate, 104 + tb)
            p.t.matmul(ps_r[:], b3c_sb[:], coefP[0:NE, cols(tb)], start=True,
                       stop=True).then_inc(s_mm, 1)
            p.v.wait_ge(s_mm, 2 + 3 * tb)
            p.v.tensor_copy(rowb[:], ps_r[:]).then_inc(s_gate, 1)
            bank = ps_b[tb % 2]
            p.t.wait_ge(s_gate, 105 + tb)
            if tb >= 2:
                p.t.wait_ge(s_acc, (tb - 1) * NCH)
            p.t.matmul(bank[:], ones_f[:], rowb[:], start=True, stop=True)\
                .then_inc(s_mm, 2)
            p.v.wait_ge(s_mm, 4 + 3 * tb)
            for d in range(NCH):
                p.v.tensor_copy(gacc[d][:, cols(tb)], bank[:]).then_inc(s_acc, 1)
                acc0 += 1
        NB_INIT = acc0
        MM_INIT = 1 + 3 * NBLK
    gscope.close()

    wscope = ExitStack()
    w1_sb = [sb(wscope, nc, f"w1_{d}", [128, D], BF16) for d in range(NCH)]
    w2_sb = [sb(wscope, nc, f"w2_{d}", [128, D], BF16) for d in range(NCH)]
    w3_sb = [sb(wscope, nc, f"w3_{d}", [128, D], BF16) for d in range(NCH)]
    h1 = [sb(wscope, nc, f"h1_{d}", [128, TB], BF16) for d in range(NCH)]
    h2 = [sb(wscope, nc, f"h2_{d}", [128, TB], BF16) for d in range(NCH)]
    ps_h = [ps(wscope, nc, f"ph{i}", [128, TB]) for i in range(2)]
    ps_o = [ps(wscope, nc, f"po{i}", [128, TB]) for i in range(2)]
    ps_c = ps(wscope, nc, "pc", [128, TB])
    wld = [0]; evc = [0]; omc = [0]; accc = [NB_INIT]; cbc = [0]; mmc = [MM_INIT]

    for e in range(NE):
        p.g.wait_ge(s_ev, max(0, 64 * e - 8))
        if e == 0:
            p.g.wait_ge(s_acc, NB_INIT)
        for d in range(NCH):
            p.g.dma_start(w1_sb[d][:], io["w1"][e, d]).then_inc(s_w, 16); wld[0] += 16
        w1done = wld[0]
        p.g.wait_ge(s_ev, 64 * e)
        for d in range(NCH):
            p.g.dma_start(w2_sb[d][:], io["w2"][e, d]).then_inc(s_w, 16); wld[0] += 16
        w2done = wld[0]
        p.g.wait_ge(s_acc, NB_INIT + e * 32)
        for d in range(NCH):
            p.g.dma_start(w3_sb[d][:], io["w3"][e, d]).then_inc(s_w, 16); wld[0] += 16
        walldone = wld[0]
        for tb in range(NBLK):
            if cbc[0] > 0:
                p.t.wait_ge(s_cb, 2 * cbc[0])
            p.t.matmul(ps_c[:], sel_sb[:, e*128:(e+1)*128], coefP[0:NE, cols(tb)],
                       start=True, stop=True).then_inc(s_mm, 1)
            mmc[0] += 1
            p.v.wait_ge(s_mm, mmc[0])
            p.v.tensor_copy(cb_sb[:], ps_c[:]).then_inc(s_cb, 1)
            for stage, wbuf, srcb, dst, bcol in (
                    (1, w1_sb, None, h1, B1c), (2, w2_sb, h1, h2, B2c)):
                for oc in range(NCH):
                    bank = ps_h[oc % 2]
                    gh = evc[0]
                    if oc == 0 and tb == 0:
                        p.t.wait_ge(s_w, w1done if stage == 1 else w2done)
                    if stage == 2 and oc == 0:
                        p.t.wait_ge(s_ev, evc[0])
                    elif gh >= 2:
                        p.t.wait_ge(s_ev, gh - 1)
                    for d in range(NCH):
                        rhs = xnbf_sb[d][:, cols(tb)] if stage == 1 else srcb[d][:]
                        h = p.t.matmul(bank[:], wbuf[d][:, oc*128:(oc+1)*128],
                                       rhs, start=(d == 0), stop=(d == NCH - 1))
                    h.then_inc(s_mm, 1)
                    mmc[0] += 1
                    p.s.wait_ge(s_mm, mmc[0])
                    p.s.activation(dst[oc][:], bank[:], AF.Relu,
                                   bias=bcol[:, e:e+1]).then_inc(s_ev, 1)
                    evc[0] += 1
            for dd in range(NCH):
                bank = ps_o[dd % 2]
                om = omc[0]
                if dd == 0:
                    p.t.wait_ge(s_ev, evc[0])
                    if tb == 0:
                        p.t.wait_ge(s_w, walldone)
                if om >= 2:
                    p.t.wait_ge(s_acc, accc[0] - 1)
                for f in range(NCH):
                    h = p.t.matmul(bank[:], w3_sb[f][:, dd*128:(dd+1)*128],
                                   h2[f][:], start=(f == 0), stop=(f == NCH - 1))
                h.then_inc(s_mm, 1)
                mmc[0] += 1
                p.v.wait_ge(s_mm, mmc[0])
                p.v.tensor_tensor(tmp_sb[:], bank[:], cb_sb[:], OP.mult)
                p.v.tensor_tensor(gacc[dd][:, cols(tb)], gacc[dd][:, cols(tb)],
                                  tmp_sb[:], OP.add).then_inc(s_acc, 1)
                accc[0] += 1
                omc[0] += 1
            p.v.nop().then_inc(s_cb, 1)
            cbc[0] += 1
    wscope.close()

    fscope = ExitStack()
    xb = [sb(fscope, nc, f"xb{i}", [128, NTOK], F32, side="right") for i in range(2)]
    sb_ = [sb(fscope, nc, f"sbb{i}", [128, NTOK], F32, side="right") for i in range(2)]
    gb = sb(fscope, nc, "gb", [128, NTOK], F32, side="right")
    x1b = [sb(fscope, nc, f"x1b{i}", [128, NTOK], F32, side="right") for i in range(2)]
    echo = sb(fscope, nc, "echo", [128, 1], F32, side="right")
    s_silu = sem(st, nc, "silu")
    s_x1 = sem(st, nc, "x1")
    s_od = sem(st, nc, "od")
    p.y.wait_ge(s_acc, accc[0])
    for d in range(NCH):
        if d >= 2:
            p.y.wait_ge(s_x1, d - 1)
        p.y.dma_start(xb[d % 2][:], io["x1T"][d]).then_inc(s_ld, 16)
        p.y.dma_start(sb_[d % 2][:], io["s1"][d]).then_inc(s_ld, 16)
        ld += 32
        p.s.wait_ge(s_ld, ld)
        if d >= 1:
            p.s.wait_ge(s_x1, d)
        p.s.activation(gb[:], gacc[d][:], AF.Silu).then_inc(s_silu, 1)
        p.v.wait_ge(s_silu, d + 1)
        p.v.wait_ge(s_ld, ld)
        if d >= 2:
            p.v.wait_ge(s_od, 16 * (d - 1))
        p.v.tensor_tensor(gb[:], gb[:], sb_[d % 2][:], OP.mult)
        p.v.tensor_tensor(x1b[d % 2][:], gb[:], xb[d % 2][:], OP.add)
        p.v.tensor_copy(echo[:], x1b[d % 2][:, NTOK-1:NTOK]).then_inc(s_x1, 1)
        p.y.wait_ge(s_x1, d + 1)
        p.y.dma_start(io["o_x2"][d], x1b[d % 2][:]).then_inc(s_od, 16)
    p.y.wait_ge(s_od, 16 * NCH)
    fscope.close()


class PhaseStack:
    def __init__(self, launch_stack):
        self.mem = ExitStack()
        self.launch = launch_stack
    def enter_context(self, cm):
        if type(cm).__name__ == "_GeneratorContextManager":
            return self.launch.enter_context(cm)
        return self.mem.enter_context(cm)
    def close(self, p):
        self.mem.close()


def build_l0():
    nc = bass.Bass("TRN2", target_bir_lowering=False)
    E = lambda n, s, dt: nc.dram_tensor(n, s, dt, kind="ExternalInput").ap()
    I = lambda n, s, dt: nc.dram_tensor(n, s, dt)
    NE0 = 14
    _, mc0 = mask_offsets(False)
    ins = dict(
        x_own=E("x_own", [NCH, 128, NTOK], F32),
        x_absA=E("x_absA", [NCH, 128, NTOK], F32),
        x_absB=E("x_absB", [NCH, 128, NTOK], F32),
        ident=E("ident", [128, 128], F32),
        wqkvT0=E("wqkvT0", [NCH, 128, 3072], F32),
        gwT0=E("gwT0", [NCH, 128, NE0], F32),
        gcol=E("gcol", [NCH, 128, 1], F32),
        bcol=E("bcol", [NCH, 128, 1], F32),
        wout0=E("wout0", [NCH, 128, D], F32),
        bout0=E("bout0", [NCH, 128, 1], F32),
        mask0=E("mask0", [128, mc0], BF16),
        w1m=E("w1m", [NE0, NCH, 128, 2048], F32),
        b1c=E("b1c", [128, NE0 * 16], F32),
        w2m=E("w2m", [NE0, 16, 128, D], F32),
        b2r=E("b2r", [NE0, D], F32),
        sel14=E("sel14", [14, 128 * 14], F32),
    )
    o_x1 = nc.dram_tensor("o_x1", [NCH, 128, NTOK], F32, kind="ExternalOutput").ap()
    xn0 = I("xn0", [NCH, 128, NTOK], F32)
    xnbf0 = I("xnbf0", [NCH, 128, NTOK], BF16)
    q0 = I("q0", [NCH, 128, NTOK], BF16)
    gate0 = I("gate0", [NE0, NTOK], F32)
    kA0 = I("kA0", [NCH, 128, NTOK], BF16)
    kB0 = I("kB0", [NCH, 128, NTOK], BF16)
    vA0 = I("vA0", [NTILE, 128, 1040], BF16)
    vB0 = I("vB0", [NTILE, 128, 1040], BF16)
    s0 = I("s0", [NCH, 128, NTOK], F32)

    p = Plan(nc)
    gstack = ExitStack()
    s_bar = gstack.enter_context(nc.semaphore("sbar"))
    barn = [0]

    def phase(fn):
        stp = PhaseStack(gstack)
        fn(stp)
        barn[0] += 1
        p.y.nop().then_inc(s_bar, 1)
        for e in (p.t, p.s, p.v, p.g, p.y):
            e.wait_ge(s_bar, barn[0])
        stp.close(p)

    A = lambda t: t.ap()
    phase(lambda stp: emit_prep(p, stp, nc, 0, dict(
        xT=ins["x_own"], wqkvT=ins["wqkvT0"], gwT=ins["gwT0"], ident=ins["ident"],
        gcol=ins["gcol"], bcol=ins["bcol"], o_xn=A(xn0), o_xnbf=A(xnbf0),
        o_q=A(q0), o_gate=A(gate0)), do_main=True, do_kv=False))
    phase(lambda stp: emit_prep(p, stp, nc, 0, dict(
        xT=ins["x_absA"], wqkvT=ins["wqkvT0"], gwT=ins["gwT0"], ident=ins["ident"],
        gcol=ins["gcol"], bcol=ins["bcol"], o_k=A(kA0), o_v=A(vA0)),
        do_main=False, do_kv=True))
    phase(lambda stp: emit_prep(p, stp, nc, 0, dict(
        xT=ins["x_absB"], wqkvT=ins["wqkvT0"], gwT=ins["gwT0"], ident=ins["ident"],
        gcol=ins["gcol"], bcol=ins["bcol"], o_k=A(kB0), o_v=A(vB0)),
        do_main=False, do_kv=True))
    phase(lambda stp: emit_attn(p, stp, nc, 0, dict(
        q=A(q0), kA=A(kA0), kB=A(kB0), vA=A(vA0), vB=A(vB0),
        woutT=ins["wout0"], boutcol=ins["bout0"], maskbank=ins["mask0"],
        o_s=A(s0))))
    phase(lambda stp: emit_moe0(p, stp, nc, dict(
        xnbf=A(xnbf0), xn=A(xn0), coeffT=A(gate0), w1=ins["w1m"], b1c=ins["b1c"],
        w2=ins["w2m"], b2r=ins["b2r"], sel=ins["sel14"], xT=ins["x_own"],
        s0=A(s0), o_x1=o_x1)))
    p.build()
    gstack.close()
    return nc


def build_l1():
    nc = bass.Bass("TRN2", target_bir_lowering=False)
    E = lambda n, s, dt: nc.dram_tensor(n, s, dt, kind="ExternalInput").ap()
    I = lambda n, s, dt: nc.dram_tensor(n, s, dt)
    NE1 = 16
    _, mc1 = mask_offsets(True)
    nb1 = sum(KT for (_, _, _, _, _, KT, _) in attn_schedule(True))
    ins = dict(
        x_own=E("x_own", [NCH, 128, NTOK], F32),
        x_absA=E("x_absA", [NCH, 128, NTOK], F32),
        x_absB=E("x_absB", [NCH, 128, NTOK], F32),
        ident=E("ident", [128, 128], F32),
        wqkvT1=E("wqkvT1", [NCH, 128, 3072], F32),
        gwT1=E("gwT1", [NCH, 128, NE1], F32),
        bgc=E("bgc", [NE1, 1], F32),
        wout1=E("wout1", [NCH, 128, D], F32),
        bout1=E("bout1", [NCH, 128, 1], F32),
        mask1=E("mask1", [128, mc1], BF16),
        bias1=E("bias1", [128, nb1], F32),
        mh=E("mh", [128, 2], F32),
        w1f=E("w1f", [NE1, NCH, 128, D], F32),
        w2f=E("w2f", [NE1, NCH, 128, D], F32),
        w3f=E("w3f", [NE1, NCH, 128, D], F32),
        b1r1=E("b1r1", [1, NE1], F32),
        b2r1=E("b2r1", [1, NE1], F32),
        b3c1=E("b3c1", [NE1, 1], F32),
        sel16=E("sel16", [NE1, 128 * NE1], F32),
    )
    o_x2 = nc.dram_tensor("o_x2", [NCH, 128, NTOK], F32, kind="ExternalOutput").ap()
    xnbf1 = I("xnbf1", [NCH, 128, NTOK], BF16)
    q1 = I("q1", [NCH, 128, NTOK], BF16)
    lgA = I("lgA", [NE1, NTOK], F32)
    lgB = I("lgB", [NE1, NTOK], F32)
    kA1 = I("kA1", [NCH, 128, NTOK], BF16)
    kB1 = I("kB1", [NCH, 128, NTOK], BF16)
    vA1 = I("vA1", [NTILE, 128, 1040], BF16)
    vB1 = I("vB1", [NTILE, 128, 1040], BF16)
    s1 = I("s1", [NCH, 128, NTOK], F32)

    p = Plan(nc)
    gstack = ExitStack()
    s_bar = gstack.enter_context(nc.semaphore("sbar"))
    barn = [0]

    def phase(fn):
        stp = PhaseStack(gstack)
        fn(stp)
        barn[0] += 1
        p.y.nop().then_inc(s_bar, 1)
        for e in (p.t, p.s, p.v, p.g, p.y):
            e.wait_ge(s_bar, barn[0])
        stp.close(p)

    A = lambda t: t.ap()
    phase(lambda stp: emit_prep(p, stp, nc, 1, dict(
        xT=ins["x_own"], wqkvT=ins["wqkvT1"], gwT=ins["gwT1"], ident=ins["ident"],
        bgc=ins["bgc"], o_xnbf=A(xnbf1), o_q=A(q1), o_gate=None),
        do_main=True, do_kv=False, do_gate=False))
    phase(lambda stp: emit_prep(p, stp, nc, 1, dict(
        xT=ins["x_absA"], wqkvT=ins["wqkvT1"], gwT=ins["gwT1"], ident=ins["ident"],
        bgc=ins["bgc"], o_k=A(kA1), o_v=A(vA1), o_gate=A(lgA)),
        do_main=False, do_kv=True, do_gate=True))
    phase(lambda stp: emit_prep(p, stp, nc, 1, dict(
        xT=ins["x_absB"], wqkvT=ins["wqkvT1"], gwT=ins["gwT1"], ident=ins["ident"],
        bgc=ins["bgc"], o_k=A(kB1), o_v=A(vB1), o_gate=A(lgB)),
        do_main=False, do_kv=True, do_gate=True))
    phase(lambda stp: emit_attn(p, stp, nc, 1, dict(
        q=A(q1), kA=A(kA1), kB=A(kB1), vA=A(vA1), vB=A(vB1),
        woutT=ins["wout1"], boutcol=ins["bout1"], maskbank=ins["mask1"],
        biascols=ins["bias1"], o_s=A(s1))))
    phase(lambda stp: emit_ffn1(p, stp, nc, dict(
        xnbf=A(xnbf1), lgA=A(lgA), lgB=A(lgB), mh=ins["mh"],
        w1=ins["w1f"], w2=ins["w2f"], w3=ins["w3f"],
        b1r=ins["b1r1"], b2r=ins["b2r1"], b3c=ins["b3c1"],
        sel=ins["sel16"], x1T=ins["x_own"], s1=A(s1), o_x2=o_x2)))
    p.build()
    gstack.close()
    return nc


from concourse.bass_utils import run_bass_kernel_spmd
import ml_dtypes


def _alibi_biascols(half):
    ncols = sum(KT for (_, _, _, _, _, KT, _) in attn_schedule(True))
    biascols = np.zeros((ncols, 128), np.float32)
    i = 0
    for (J, h, qb, j0, off, KT, NM) in attn_schedule(True):
        lts = range(j0, j0 + max(qb // 128, 1))
        if qb == 64:
            a_ = 2 * j0 + ((j0 % 2) if half == 0 else (1 - j0 % 2))
            h64 = (off % 128) // 64
            maxpos = a_ * 128 + h64 * 64 + 63
        else:
            maxpos = max(2 * lt + (((lt % 2) if half == 0 else (1 - lt % 2)))
                         for lt in lts) * 128 + 127
        q_end = maxpos + 1
        own_max_tile = maxpos // 128
        sl = ALIBI_SLOPES[h]
        for kt in range(KT):
            if kt > own_max_tile:
                biascols[i] = -60000.0
            else:
                biascols[i] = sl * ((kt * 128 + np.arange(128)) - q_end)
            i += 1
    return np.ascontiguousarray(biascols.T)


def kernel(**d):
    cores = list(range(8))
    f32 = np.float32
    ac = np.ascontiguousarray
    x0 = np.asarray(d["L0_attn_x"], dtype=f32)
    eye128 = np.eye(128, dtype=f32)
    idxh = [own_token_idx(0), own_token_idx(1)]

    sh0 = dict(
        ident=eye128,
        wqkvT0=ac(d["L0_attn_w_qkv"].T.astype(f32)).reshape(NCH, 128, 3072),
        gwT0=ac(d["L0_ffn_gate_w"].T.astype(f32)).reshape(NCH, 128, 14),
        gcol=ac(d["L0_attn_ln_gamma"].astype(f32)).reshape(NCH, 128, 1),
        bcol=ac(d["L0_attn_ln_beta"].astype(f32)).reshape(NCH, 128, 1),
        wout0=ac(d["L0_attn_w_out"].T.astype(f32)).reshape(NCH, 128, D),
        bout0=ac(d["L0_attn_b_out"].astype(f32)).reshape(NCH, 128, 1),
        w1m=ac(d["L0_ffn_w1"].astype(f32)).reshape(14, 8, 128, 2048),
        b1c=ac(d["L0_ffn_b1"].astype(f32).reshape(14 * 16, 128).T),
        w2m=ac(d["L0_ffn_w2"].astype(f32)).reshape(14, 16, 128, 1024),
        b2r=ac(d["L0_ffn_b2"].astype(f32)),
        sel14=ac(np.repeat(np.eye(14, dtype=f32), 128, axis=1)),
    )
    maskb = [None, None]
    for half in (0, 1):
        mb0, _ = mask_patterns(half, False)
        maskb[half] = ac(mb0).astype(ml_dtypes.bfloat16)
    in_maps = []
    for c in cores:
        b, half = c // 2, c % 2
        m = dict(sh0)
        m["x_own"] = ac(x0[b][idxh[half]].T).reshape(NCH, 128, NTOK)
        m["x_absA"] = ac(x0[b][idxh[0]].T).reshape(NCH, 128, NTOK)
        m["x_absB"] = ac(x0[b][idxh[1]].T).reshape(NCH, 128, NTOK)
        m["mask0"] = maskb[half]
        in_maps.append(m)
    nc0 = build_l0()
    r0 = run_bass_kernel_spmd(nc0, in_maps, core_ids=cores).results

    x1 = np.empty((4, 4096, 1024), f32)
    for c in cores:
        b, half = c // 2, c % 2
        x1[b][idxh[half]] = r0[c]["o_x1"].reshape(1024, NTOK).T

    sh1 = dict(
        ident=eye128,
        wqkvT1=ac(d["L1_attn_W_qkv"].astype(f32)).reshape(NCH, 128, 3072),
        gwT1=ac(d["L1_ffn_Wg"].astype(f32)).reshape(NCH, 128, 16),
        bgc=ac(d["L1_ffn_bg"].astype(f32)).reshape(16, 1),
        wout1=ac(d["L1_attn_W_out"].astype(f32)).reshape(NCH, 128, D),
        bout1=np.zeros((NCH, 128, 1), f32),
        w1f=ac(d["L1_ffn_w1"].astype(f32)).reshape(16, 8, 128, 1024),
        w2f=ac(d["L1_ffn_w2"].astype(f32)).reshape(16, 8, 128, 1024),
        w3f=ac(d["L1_ffn_w3"].astype(f32)).reshape(16, 8, 128, 1024),
        b1r1=ac(d["L1_ffn_b1"].astype(f32).reshape(1, 16)),
        b2r1=ac(d["L1_ffn_b2"].astype(f32).reshape(1, 16)),
        b3c1=ac(d["L1_ffn_b3"].astype(f32).reshape(16, 1)),
        sel16=ac(np.repeat(np.eye(16, dtype=f32), 128, axis=1)),
    )
    maskb1 = [None, None]
    bias1 = [None, None]
    for half in (0, 1):
        mb1, _ = mask_patterns(half, True)
        maskb1[half] = ac(mb1).astype(ml_dtypes.bfloat16)
        bias1[half] = _alibi_biascols(half)
    in_maps = []
    for c in cores:
        b, half = c // 2, c % 2
        m = dict(sh1)
        m["x_own"] = ac(x1[b][idxh[half]].T).reshape(NCH, 128, NTOK)
        m["x_absA"] = ac(x1[b][idxh[0]].T).reshape(NCH, 128, NTOK)
        m["x_absB"] = ac(x1[b][idxh[1]].T).reshape(NCH, 128, NTOK)
        m["mask1"] = maskb1[half]
        m["bias1"] = bias1[half]
        mhh = np.zeros((128, 2), f32); mhh[:, half] = 1.0
        m["mh"] = mhh
        in_maps.append(m)
    nc1 = build_l1()
    r1 = run_bass_kernel_spmd(nc1, in_maps, core_ids=cores).results

    out = np.empty((4, 4096, 1024), f32)
    for c in cores:
        b, half = c // 2, c % 2
        out[b][idxh[half]] = r1[c]["o_x2"].reshape(1024, NTOK).T
    return out
